# version 08
# baseline (speedup 1.0000x reference)
import math
import numpy as np
import concourse.bass as bass
import concourse.mybir as mybir
from concourse.bass_utils import run_bass_kernel_spmd

F32 = mybir.dt.float32
BF16 = mybir.dt.bfloat16
AF = mybir.ActivationFunctionType
ALU = mybir.AluOpType
AXX = mybir.AxisListType.X

V, D, FF, L, TOK = 4096, 512, 2048, 2048, 1024
VG, DG, FG = 32, 4, 16
DEPTH, Q, NCH = 4, 128, 16
EPS = 1e-5
EC = 256
NECK = L // EC

SC = {}
_c = 0
def _col(name, n):
    global _c
    SC[name] = _c
    _c += n
_col("bin", 4); _col("ln1_w", 16); _col("ln1_b", 16); _col("ln2_w", 16)
_col("ln2_b", 16); _col("phz_re", 16); _col("phz_im", 16); _col("lc_re", 16)
_col("lc_im", 16); _col("sc_lin", 16); _col("b2", 16); _col("lnl_w", 4)
_col("lnl_b", 4); _col("b1", 64); _col("bout", 32); _col("b_base", 32)
_col("sel", 2)
NS = _c


class Plan:
    ENGS = ("sync", "tensor", "vector", "scalar", "gpsimd")
    OWN = {"tensor": "T", "vector": "V", "scalar": "S"}

    def __init__(self):
        self.q = {e: [] for e in self.ENGS}
        self.cnt = dict(D=0, G=0, T=0, V=0, S=0)
        self.last_wait = {e: {} for e in self.ENGS}
        self.small_mode = False

    def wait(self, eng, tok):
        if tok is None:
            return
        s, v = tok
        if v <= 0:
            return
        if self.OWN.get(eng) == s:
            return
        lw = self.last_wait[eng]
        if lw.get(s, -1) >= v:
            return
        lw[s] = v
        self.q[eng].append(("w", s, v))

    def waits(self, eng, *toks):
        for t in toks:
            self.wait(eng, t)

    def do(self, eng, fn, sem=None, amt=1):
        tok = None
        if sem is not None:
            self.cnt[sem] += amt
            tok = (sem, self.cnt[sem])
            self.last_wait[eng][sem] = self.cnt[sem] if self.OWN.get(eng) == sem else self.last_wait[eng].get(sem, -1)
        self.q[eng].append(("i", fn, sem, amt))
        return tok

    def t(self, fn, sem="T"):
        return self.do("tensor", fn, sem)

    def v(self, fn, sem="V"):
        tok = self.do("vector", fn, sem)
        if self.small_mode:
            self.q["vector"].append(("w", "V", tok[1]))
        return tok

    def s(self, fn, sem="S"):
        tok = self.do("scalar", fn, sem)
        if self.small_mode:
            self.q["scalar"].append(("w", "S", tok[1]))
        return tok

    def g(self, fn):
        return self.do("gpsimd", fn, "G", 1)

    def vbar(self):
        self.q["vector"].append(("w", "V", self.cnt["V"]))

    def dma(self, eng, out_ap, in_ap):
        sem = "D" if eng == "sync" else "G"
        return self.do(eng, lambda e, o=out_ap, i=in_ap: e.dma_start(out=o, in_=i), sem, 16)


def build_program(stop=None):
    nc = bass.Bass("TRN2", target_bir_lowering=False)

    d_xT = nc.dram_tensor("xT", [128, VG, L], F32, kind="ExternalInput").ap()
    d_xown = nc.dram_tensor("xown", [128, VG, TOK], F32, kind="ExternalInput").ap()
    d_winT = nc.dram_tensor("winT", [128, VG, D], F32, kind="ExternalInput").ap()
    d_w1T = nc.dram_tensor("w1T", [DEPTH, 128, DG, FF], F32, kind="ExternalInput").ap()
    d_w2T = nc.dram_tensor("w2T", [DEPTH, 128, FG, D], F32, kind="ExternalInput").ap()
    d_woutT = nc.dram_tensor("woutT", [128, DG, V], F32, kind="ExternalInput").ap()
    d_wbT = nc.dram_tensor("wbT", [VG, 128, VG, 128], F32, kind="ExternalInput").ap()
    d_aT = nc.dram_tensor("aT", [128, VG, 136], F32, kind="ExternalInput").ap()
    d_bT = nc.dram_tensor("bT", [128, V], F32, kind="ExternalInput").ap()
    d_sm = nc.dram_tensor("smalls", [128, NS], F32, kind="ExternalInput").ap()
    d_id = nc.dram_tensor("id128", [128, 128], F32, kind="ExternalInput").ap()
    d_ascr = nc.dram_tensor("a_scr", [128, VG, TOK], F32, kind="Internal").ap()
    d_x1scr = nc.dram_tensor("x1_scr", [128, VG, TOK], F32, kind="Internal").ap()
    d_fscr = nc.dram_tensor("f_scr", [128, VG, TOK], F32, kind="Internal").ap()
    d_out = nc.dram_tensor("OUT", [128, VG, TOK], F32, kind="ExternalOutput").ap()
    d_tap = None
    if stop is not None:
        d_tap = nc.dram_tensor("TAP", [128, 32768], F32, kind="ExternalOutput").ap()

    AR_h = nc.alloc_sbuf_tensor("ARENA", [128, 32768], F32)
    AR = AR_h.ap()
    A2_h = nc.alloc_sbuf_tensor("ARENA2", [128, 12288], F32)
    A2 = A2_h.ap()
    ar_addr = nc.lookup_mloc(AR_h).addr
    a2_addr = nc.lookup_mloc(A2_h).addr

    def _alias(name, n_bf16, f32_off, base=None):
        return nc.alloc_sbuf_tensor_at(
            name, [128, n_bf16], BF16,
            offset=(ar_addr if base is None else base) + 4 * f32_off).ap()
    tCB = nc.alloc_sbuf_tensor("tCB", [128, 4 * 129], F32).ap()
    tSB = nc.alloc_sbuf_tensor("tSB", [128, 4 * 129], F32).ap()
    tNSB = nc.alloc_sbuf_tensor("tNSB", [128, 4 * 129], F32).ap()
    tRP = nc.alloc_sbuf_tensor("tRP", [128, 4 * 128], F32).ap()
    tCQ = nc.alloc_sbuf_tensor("tCQ", [128, 4 * 17], F32).ap()
    tSQ = nc.alloc_sbuf_tensor("tSQ", [128, 4 * 17], F32).ap()
    tS1 = nc.alloc_sbuf_tensor("tS1", [128, 4 * 64], F32).ap()
    tS2 = nc.alloc_sbuf_tensor("tS2", [128, 4 * 64], F32).ap()
    SC4 = nc.alloc_sbuf_tensor("SC4", [128, 32], F32).ap()
    S16 = nc.alloc_sbuf_tensor("S16", [128, 16 * 16], F32).ap()
    SM = nc.alloc_sbuf_tensor("SM", [128, NS], F32).ap()
    IDT = nc.alloc_sbuf_tensor("IDT", [128, 128], F32).ap()
    ONESK = nc.alloc_sbuf_tensor("ONESK", [128, 1], F32).ap()
    ONES1 = nc.alloc_sbuf_tensor("ONES1", [1, 128], F32).ap()
    MUROW = nc.alloc_sbuf_tensor("MUROW", [1, 512], F32).ap()
    RROW = nc.alloc_sbuf_tensor("RROW", [1, 512], F32).ap()
    SROW = nc.alloc_sbuf_tensor("SROW", [1, 512], F32).ap()
    RROWB = nc.alloc_sbuf_tensor("RROWB", [1, TOK], F32).ap()
    CROWB = nc.alloc_sbuf_tensor("CROWB", [1, TOK], F32).ap()
    WCS = nc.alloc_sbuf_tensor("WCS", [128, 4], F32).ap()

    PX = [nc.alloc_psum_tensor(f"PX{i}", [128, 512], F32).ap() for i in range(8)]

    h3 = AR[:, 24576:32768].rearrange("p (g t) -> p g t", t=L)
    hf = AR[:, 24576:32768]
    winTS = AR[:, 0:16384].rearrange("p (v d) -> p v d", d=D)
    woutTS = AR[:, 0:16384].rearrange("p (k o) -> p k o", o=V)
    AR4 = AR.rearrange("p (q v t) -> p q v t", v=VG, t=256)
    w1stg = AR[:, 0:8192].rearrange("p (k f) -> p k f", f=FF)
    w2stg = AR[:, 0:8192].rearrange("p (k d) -> p k d", d=D)
    W1Hf = _alias("W1H", 8192, 8192)
    W1Lf = _alias("W1L", 8192, 12288)
    W2Hf = _alias("W2H", 8192, 16384)
    W2Lf = _alias("W2L", 8192, 20480)
    MIDHf = _alias("MIDH", 8192, 0)
    MIDLf = _alias("MIDL", 8192, 4096)
    ZHf = _alias("ZH", 8192, 0, base=a2_addr)
    ZLf = _alias("ZL", 8192, 4096, base=a2_addr)
    W1H = W1Hf.rearrange("p (k f) -> p k f", f=FF)
    W1L = W1Lf.rearrange("p (k f) -> p k f", f=FF)
    W2H = W2Hf.rearrange("p (k d) -> p k d", d=D)
    W2L = W2Lf.rearrange("p (k d) -> p k d", d=D)
    MIDH = MIDHf.rearrange("p (k t) -> p k t", t=512)
    MIDL = MIDLf.rearrange("p (k t) -> p k t", t=512)
    ZH = ZHf.rearrange("p (k t) -> p k t", t=L)
    ZL = ZLf.rearrange("p (k t) -> p k t", t=L)
    ztmp3 = A2[:, 8192:10240].rearrange("p (k t) -> p k t", t=512)
    mtmp = [A2[:, 10240:10752], A2[:, 10752:11264]]

    R = [A2[:, j * 2048:(j + 1) * 2048] for j in range(6)]
    sqb = A2[:, 0:8192]
    hn3 = A2[:, 0:8192].rearrange("p (g t) -> p g t", t=L)
    hno3 = A2[:, 8192:12288].rearrange("p (g t) -> p g t", t=TOK)
    atile = [A2[:, 0:1024], A2[:, 1024:2048]]
    asq = [A2[:, 2048:3072], A2[:, 3072:4096]]
    aq3 = A2[:, 0:8192].rearrange("p (v t) -> p v t", t=256)
    EV = A2[:, 8192:8704]

    wostg = AR[:, 16384:24576].rearrange("p (k o) -> p k o", o=V)
    WOHf = _alias("WOH", 16384, 0)
    WOLf = _alias("WOL", 16384, 8192)
    WOH = WOHf.rearrange("p (k o) -> p k o", o=V)
    WOL = WOLf.rearrange("p (k o) -> p k o", o=V)
    HNHf = _alias("HNH", 4096, 16384)
    HNLf = _alias("HNL", 4096, 18432)
    HNH = HNHf.rearrange("p (k t) -> p k t", t=TOK)
    HNL = HNLf.rearrange("p (k t) -> p k t", t=TOK)
    VHf = _alias("VH", 16384, 0)
    VLf = _alias("VL", 16384, 8192)
    VH = VHf.rearrange("p (v t) -> p v t", t=512)
    VL = VLf.rearrange("p (v t) -> p v t", t=512)
    xq3 = AR[:, 16384:24576].rearrange("p (v t) -> p v t", t=256)
    sqv = AR[:, 16384:24576]
    sqv3 = xq3
    atstg = AR[:, 16384:20736]
    ATHf = _alias("ATH", 4352, 24576)
    ATLf = _alias("ATL", 4352, 26752)
    ATH = ATHf.rearrange("p (v e) -> p v e", e=136)
    ATL = ATLf.rearrange("p (v e) -> p v e", e=136)
    h2b = AR[:, 28928:29440]
    H2T = AR[:, 29440:29568]
    RT = AR[:, 29568:29616]
    H2W = AR[:, 29616:29744]
    ftile = AR[:, 29744:30256]
    fsq = AR[:, 30256:30768]
    H2Hf = _alias("H2H", 512, 30768)
    H2Lf = _alias("H2L", 512, 31024)
    btstg = [AR[:, 31280:31408], AR[:, 31408:31536]]
    BTH = [_alias("BT0H", 128, 31616), _alias("BT1H", 128, 31744)]
    BTL = [_alias("BT0L", 128, 31680), _alias("BT1L", 128, 31808)]
    wbstg = A2[:, 8192:12288]
    WBHf = [_alias("WBH0", 4096, 0, base=a2_addr),
            _alias("WBH1", 4096, 4096, base=a2_addr)]
    WBLf = [_alias("WBL0", 4096, 2048, base=a2_addr),
            _alias("WBL1", 4096, 6144, base=a2_addr)]
    WBH = [t.rearrange("p (k o) -> p k o", o=128) for t in WBHf]
    WBL = [t.rearrange("p (k o) -> p k o", o=128) for t in WBLf]
    fch3 = AR[:, 0:8192].rearrange("p (v t) -> p v t", t=256)
    xch3 = AR[:, 8192:16384].rearrange("p (v t) -> p v t", t=256)

    P = Plan()
    guard = [None] * 8
    last_repl = [None]

    def bankw(i):
        P.wait("tensor", guard[i])

    def coef_rows(ps_bank, w, n, tok_stats, qoff=0):
        P.wait("vector", tok_stats)
        P.wait("vector", last_repl[0])
        P.small_mode = True
        P.v(lambda e: e.tensor_scalar(MUROW[0:1, 0:w], ps_bank[0:1, qoff:qoff + w], 1.0 / n, None, ALU.mult))
        P.v(lambda e: e.tensor_scalar(SROW[0:1, 0:w], ps_bank[32:33, qoff:qoff + w], 1.0 / n, None, ALU.mult))
        P.v(lambda e: e.tensor_tensor(RROW[0:1, 0:w], MUROW[0:1, 0:w], MUROW[0:1, 0:w], ALU.mult))
        P.v(lambda e: e.tensor_tensor(SROW[0:1, 0:w], SROW[0:1, 0:w], RROW[0:1, 0:w], ALU.subtract))
        t1 = P.v(lambda e: e.tensor_scalar(SROW[0:1, 0:w], SROW[0:1, 0:w], EPS, None, ALU.add))
        P.wait("scalar", t1)
        t2 = P.s(lambda e: e.activation(SROW[0:1, 0:w], SROW[0:1, 0:w], AF.Sqrt))
        P.wait("vector", t2)
        P.v(lambda e: e.reciprocal(RROW[0:1, 0:w], SROW[0:1, 0:w]))
        P.v(lambda e: e.tensor_tensor(MUROW[0:1, 0:w], MUROW[0:1, 0:w], RROW[0:1, 0:w], ALU.mult))
        tok = P.v(lambda e: e.tensor_scalar(MUROW[0:1, 0:w], MUROW[0:1, 0:w], -1.0, None, ALU.mult))
        P.small_mode = False
        return tok

    def repl(bank_r, bank_c, w, tok_coef, src_r=None, src_c=None, qoff=0):
        sr = RROW[0:1, 0:w] if src_r is None else src_r
        scr = MUROW[0:1, 0:w] if src_c is None else src_c
        P.wait("tensor", tok_coef)
        bankw(bank_r)
        P.t(lambda e, b=bank_r, sr=sr, w=w, qoff=qoff: e.matmul(
            PX[b][:, qoff:qoff + w], ONES1, sr, start=True, stop=True))
        bankw(bank_c)
        tok = P.t(lambda e, b=bank_c, scr=scr, w=w, qoff=qoff: e.matmul(
            PX[b][:, qoff:qoff + w], ONES1, scr, start=True, stop=True))
        last_repl[0] = tok
        return tok

    tok_sm = P.dma("sync", SM[:, :], d_sm)
    tok_id = P.dma("sync", IDT[:, :], d_id)
    tok_win = P.dma("sync", winTS, d_winT)
    P.v(lambda e: e.memset(ONESK[:, :], 1.0))
    tok_ones = P.v(lambda e: e.memset(ONES1[:, :], 1.0))

    P.waits("tensor", tok_win, tok_ones)
    for dg in range(DG):
        for vg in range(VG):
            tk = P.t(lambda e, vg=vg, dg=dg: e.matmul(
                PX[6][:, dg:dg + 1], winTS[:, vg, dg * 128:(dg + 1) * 128], ONESK[:, :],
                start=(vg == 0), stop=(vg == VG - 1)))
    P.wait("vector", tk)
    P.small_mode = True
    tok_wcs = P.v(lambda e: e.tensor_copy(WCS[:, :], PX[6][:, 0:4]))
    P.small_mode = False
    guard[6] = tok_wcs

    xcbuf = [AR[:, 16384:24576], A2[:, 0:8192]]
    xcfree = [None, None]
    P.wait("vector", tok_sm)
    for ck in range(NECK):
        buf = xcbuf[ck % 2]
        buf3 = buf.rearrange("p (v t) -> p v t", t=EC)
        P.wait("sync", xcfree[ck % 2])
        tok_xc = P.dma("sync", buf3, d_xT[:, :, ck * EC:(ck + 1) * EC])
        P.wait("tensor", tok_xc)
        P.wait("tensor", tok_sm)
        bankw(0)
        for vg in range(VG):
            P.t(lambda e, vg=vg, buf3=buf3: e.matmul(
                PX[0][0:1, 0:EC], ONESK[:, :], buf3[:, vg, :],
                start=(vg == 0), stop=(vg == VG - 1)))
        for dg in range(DG):
            bankw(2 + dg)
            for vg in range(VG):
                tok_mm = P.t(lambda e, vg=vg, dg=dg, buf3=buf3: e.matmul(
                    PX[2 + dg][:, 0:EC], winTS[:, vg, dg * 128:(dg + 1) * 128], buf3[:, vg, :],
                    start=(vg == 0), stop=(vg == VG - 1)))
        P.wait("scalar", tok_mm)
        tok_sq = P.s(lambda e, buf=buf: e.activation(buf, buf, AF.Square))
        P.wait("tensor", tok_sq)
        for vg in range(VG):
            tok_st = P.t(lambda e, vg=vg, buf3=buf3: e.matmul(
                PX[0][32:33, 0:EC], ONESK[:, :], buf3[:, vg, :],
                start=(vg == 0), stop=(vg == VG - 1)))
        xcfree[ck % 2] = tok_st
        tok_coef = coef_rows(PX[0], EC, V, tok_st)
        guard[0] = tok_coef
        tok_repl = repl(6, 7, EC, tok_coef)
        P.wait("vector", tok_repl)
        P.wait("vector", tok_wcs)
        P.v(lambda e: e.tensor_copy(EV[:, 0:EC], PX[6][:, 0:EC]))
        for dg in range(DG):
            hsl = h3[:, dg, ck * EC:(ck + 1) * EC]
            P.v(lambda e, hsl=hsl, dg=dg: e.tensor_tensor(hsl, PX[2 + dg][:, 0:EC], EV[:, 0:EC], ALU.mult))
            P.v(lambda e, hsl=hsl, dg=dg: e.scalar_tensor_tensor(
                hsl, PX[7][:, 0:EC], WCS[:, dg:dg + 1], hsl, ALU.mult, ALU.add))
            tok_ev = P.v(lambda e, hsl=hsl, dg=dg: e.tensor_scalar(
                hsl, hsl, SM[:, SC["bin"] + dg:SC["bin"] + dg + 1], None, ALU.add))
        for i in range(2, 8):
            guard[i] = tok_ev
    tok_h = tok_ev
    tok_winfree = tok_ev

    if stop == "embed":
        P.wait("gpsimd", tok_h)
        tg = P.dma("gpsimd", d_tap[:, 0:8192], hf)
        P.wait("gpsimd", tg)
        return nc, P

    tok_w1_used = tok_winfree
    tok_w2_used = tok_winfree
    tok_stage_free = tok_winfree
    tok_scanbufs_free = None
    tok_mid_free = None

    def emit_tables(i):
        c = lambda n: SM[:, SC[n] + i * 4: SC[n] + (i + 1) * 4]
        RR, RHO, CS1, SN1, T4a, T4b = (SC4[:, j * 4:(j + 1) * 4] for j in range(6))
        P.wait("vector", tok_sm)
        P.small_mode = True
        P.v(lambda e: e.tensor_tensor(RR, c("phz_re"), c("phz_re"), ALU.mult))
        P.v(lambda e: e.tensor_tensor(T4a, c("phz_im"), c("phz_im"), ALU.mult))
        t = P.v(lambda e: e.tensor_tensor(RR, RR, T4a, ALU.add))
        P.wait("scalar", t)
        t = P.s(lambda e: e.activation(T4a, RR, AF.Sqrt))
        P.wait("vector", t)
        P.v(lambda e: e.reciprocal(T4b, T4a))
        P.v(lambda e: e.tensor_tensor(CS1, RR, T4b, ALU.mult))
        P.v(lambda e: e.tensor_tensor(CS1, CS1, T4b, ALU.mult))
        P.v(lambda e: e.tensor_scalar(CS1, CS1, -0.5, 1.5, ALU.mult, ALU.add))
        P.v(lambda e: e.tensor_tensor(T4b, T4b, CS1, ALU.mult))
        t = P.v(lambda e: e.tensor_tensor(RR, RR, T4b, ALU.mult))
        P.wait("scalar", t)
        t2 = P.s(lambda e: e.activation(RHO, RR, AF.Tanh))
        P.wait("vector", t2)
        P.v(lambda e: e.tensor_tensor(CS1, c("phz_re"), T4b, ALU.mult))
        P.v(lambda e: e.tensor_tensor(SN1, c("phz_im"), T4b, ALU.mult))
        cb = tCB.rearrange("p (g r) -> p g r", r=129)
        sb = tSB.rearrange("p (g r) -> p g r", r=129)
        s1 = tS1.rearrange("p (g r) -> p g r", r=64)
        s2 = tS2.rearrange("p (g r) -> p g r", r=64)
        cs3 = CS1.unsqueeze(2)
        sn3 = SN1.unsqueeze(2)
        P.v(lambda e: e.memset(cb[:, :, 0:1], 1.0))
        P.v(lambda e: e.memset(sb[:, :, 0:1], 0.0))
        P.v(lambda e: e.tensor_copy(cb[:, :, 1:2], cs3))
        P.v(lambda e: e.tensor_copy(sb[:, :, 1:2], sn3))
        n = 2
        while n < 129:
            m = min(n, 129 - n)
            cf, sf = T4a.unsqueeze(2), T4b.unsqueeze(2)
            P.v(lambda e, n=n: e.tensor_tensor(s1[:, :, 0:1], cb[:, :, n - 1:n], cs3, ALU.mult))
            P.v(lambda e, n=n: e.tensor_tensor(s2[:, :, 0:1], sb[:, :, n - 1:n], sn3, ALU.mult))
            P.v(lambda e: e.tensor_tensor(cf, s1[:, :, 0:1], s2[:, :, 0:1], ALU.subtract))
            P.v(lambda e, n=n: e.tensor_tensor(s1[:, :, 0:1], sb[:, :, n - 1:n], cs3, ALU.mult))
            P.v(lambda e, n=n: e.tensor_tensor(s2[:, :, 0:1], cb[:, :, n - 1:n], sn3, ALU.mult))
            P.v(lambda e: e.tensor_tensor(sf, s1[:, :, 0:1], s2[:, :, 0:1], ALU.add))
            cfb = cf.broadcast_to([128, 4, m])
            sfb = sf.broadcast_to([128, 4, m])
            P.v(lambda e, m=m, cfb=cfb: e.tensor_tensor(s1[:, :, 0:m], cb[:, :, 0:m], cfb, ALU.mult))
            P.v(lambda e, m=m, sfb=sfb: e.tensor_tensor(s2[:, :, 0:m], sb[:, :, 0:m], sfb, ALU.mult))
            P.v(lambda e, n=n, m=m: e.tensor_tensor(cb[:, :, n:n + m], s1[:, :, 0:m], s2[:, :, 0:m], ALU.subtract))
            P.v(lambda e, m=m, cfb=cfb: e.tensor_tensor(s1[:, :, 0:m], sb[:, :, 0:m], cfb, ALU.mult))
            P.v(lambda e, m=m, sfb=sfb: e.tensor_tensor(s2[:, :, 0:m], cb[:, :, 0:m], sfb, ALU.mult))
            P.v(lambda e, n=n, m=m: e.tensor_tensor(sb[:, :, n:n + m], s1[:, :, 0:m], s2[:, :, 0:m], ALU.add))
            n *= 2
        P.v(lambda e: e.tensor_scalar(tNSB[:, :], tSB[:, :], -1.0, None, ALU.mult))
        rp = tRP.rearrange("p (g r) -> p g r", r=128)
        P.v(lambda e: e.tensor_copy(rp[:, :, 0:1], RHO.unsqueeze(2)))
        n = 1
        while n < 128:
            m = min(n, 128 - n)
            fb = rp[:, :, n - 1:n].broadcast_to([128, 4, m])
            P.v(lambda e, n=n, m=m, fb=fb: e.tensor_tensor(rp[:, :, n:n + m], rp[:, :, 0:m], fb, ALU.mult))
            n *= 2
        cq = tCQ.rearrange("p (g r) -> p g r", r=17)
        sq = tSQ.rearrange("p (g r) -> p g r", r=17)
        P.v(lambda e: e.memset(cq[:, :, 0:1], 1.0))
        P.v(lambda e: e.memset(sq[:, :, 0:1], 0.0))
        P.v(lambda e: e.tensor_copy(cq[:, :, 1:2], cb[:, :, Q:Q + 1]))
        P.v(lambda e: e.tensor_copy(sq[:, :, 1:2], sb[:, :, Q:Q + 1]))
        n = 2
        while n < 17:
            m = min(n, 17 - n)
            cf, sf = T4a.unsqueeze(2), T4b.unsqueeze(2)
            P.v(lambda e, n=n: e.tensor_tensor(s1[:, :, 0:1], cq[:, :, n - 1:n], cq[:, :, 1:2], ALU.mult))
            P.v(lambda e, n=n: e.tensor_tensor(s2[:, :, 0:1], sq[:, :, n - 1:n], sq[:, :, 1:2], ALU.mult))
            P.v(lambda e: e.tensor_tensor(cf, s1[:, :, 0:1], s2[:, :, 0:1], ALU.subtract))
            P.v(lambda e, n=n: e.tensor_tensor(s1[:, :, 0:1], sq[:, :, n - 1:n], cq[:, :, 1:2], ALU.mult))
            P.v(lambda e, n=n: e.tensor_tensor(s2[:, :, 0:1], cq[:, :, n - 1:n], sq[:, :, 1:2], ALU.mult))
            P.v(lambda e: e.tensor_tensor(sf, s1[:, :, 0:1], s2[:, :, 0:1], ALU.add))
            cfb = cf.broadcast_to([128, 4, m])
            sfb = sf.broadcast_to([128, 4, m])
            P.v(lambda e, m=m, cfb=cfb: e.tensor_tensor(s1[:, :, 0:m], cq[:, :, 0:m], cfb, ALU.mult))
            P.v(lambda e, m=m, sfb=sfb: e.tensor_tensor(s2[:, :, 0:m], sq[:, :, 0:m], sfb, ALU.mult))
            P.v(lambda e, n=n, m=m: e.tensor_tensor(cq[:, :, n:n + m], s1[:, :, 0:m], s2[:, :, 0:m], ALU.subtract))
            P.v(lambda e, m=m, cfb=cfb: e.tensor_tensor(s1[:, :, 0:m], sq[:, :, 0:m], cfb, ALU.mult))
            P.v(lambda e, m=m, sfb=sfb: e.tensor_tensor(s2[:, :, 0:m], cq[:, :, 0:m], sfb, ALU.mult))
            tk = P.v(lambda e, n=n, m=m: e.tensor_tensor(sq[:, :, n:n + m], s1[:, :, 0:m], s2[:, :, 0:m], ALU.add))
            n *= 2
        P.small_mode = False
        return tk

    def inner_layer(i):
        nonlocal tok_w1_used, tok_w2_used, tok_scanbufs_free, tok_mid_free, tok_h
        nonlocal tok_stage_free
        P.waits("sync", tok_stage_free)
        tok_w1s = P.dma("sync", w1stg, d_w1T[i])
        P.waits("scalar", tok_w1s, tok_w1_used)
        tok_w1h = P.s(lambda e: e.activation(W1Hf, AR[:, 0:8192], AF.Copy))
        P.waits("gpsimd", tok_w1h, tok_w1_used)
        tok_w1l = P.g(lambda e: e.tensor_tensor(
            W1Lf, AR[:, 0:8192], W1Hf, ALU.subtract))
        P.waits("sync", tok_w1h, tok_w1l)
        tok_w2s = P.dma("sync", w2stg, d_w2T[i])
        P.waits("scalar", tok_w2s, tok_w2_used)
        tok_w2h = P.s(lambda e: e.activation(W2Hf, AR[:, 0:8192], AF.Copy))
        P.waits("gpsimd", tok_w2h, tok_w2_used)
        tok_w2l = P.g(lambda e: e.tensor_tensor(
            W2Lf, AR[:, 0:8192], W2Hf, ALU.subtract))
        tok_tab = emit_tables(i)
        P.waits("scalar", tok_h, tok_scanbufs_free)
        tok_sq = P.s(lambda e: e.activation(sqb, hf, AF.Square))
        sq3 = A2[:, 0:8192].rearrange("p (g t) -> p g t", g=4)
        repl_toks = []
        st_toks = []
        for tc in range(4):
            b = tc
            bankw(b)
            P.wait("tensor", tok_h)
            for g in range(4):
                P.t(lambda e, g=g, tc=tc, b=b: e.matmul(
                    PX[b][0:1, 0:512], ONESK[:, :], h3[:, g, tc * 512:(tc + 1) * 512],
                    start=(g == 0), stop=(g == 3)))
            P.wait("tensor", tok_sq)
            for g in range(4):
                tok_st = P.t(lambda e, g=g, tc=tc, b=b: e.matmul(
                    PX[b][32:33, 0:512], ONESK[:, :], sq3[:, g, tc * 512:(tc + 1) * 512],
                    start=(g == 0), stop=(g == 3)))
            st_toks.append(tok_st)
        for tc in range(4):
            tok_coef = coef_rows(PX[tc], 512, D, st_toks[tc])
            guard[tc] = tok_coef
            tok_r = repl(tc, 4 + tc, 512, tok_coef)
            repl_toks.append(tok_r)
        P.waits("vector", tok_tab, *repl_toks)
        lcre = SM[:, SC["lc_re"] + i * 4: SC["lc_re"] + i * 4 + 4]
        lcim = SM[:, SC["lc_im"] + i * 4: SC["lc_im"] + i * 4 + 4]
        cb = tCB.rearrange("p (g r) -> p g r", r=129)
        sb = tSB.rearrange("p (g r) -> p g r", r=129)
        nsb = tNSB.rearrange("p (g r) -> p g r", r=129)
        rp = tRP.rearrange("p (g r) -> p g r", r=128)
        cq = tCQ.rearrange("p (g r) -> p g r", r=17)
        sq = tSQ.rearrange("p (g r) -> p g r", r=17)
        RHO = SC4[:, 4:8]
        CS1, SN1 = SC4[:, 8:12], SC4[:, 12:16]
        sl = lambda j: S16[:, j * 16:(j + 1) * 16]
        ER, EI, ETR, ETI, VR, VI, VSR, VSI, GR, GI, HR, HI, D0C, TMPA = (sl(j) for j in range(14))
        for g in range(4):
            r0, r1, r2, r3, r4, r5 = R[0], R[1], R[2], R[3], R[4], R[5]
            r0v = r0.rearrange("p (n q) -> p n q", q=Q)
            r1v = r1.rearrange("p (n q) -> p n q", q=Q)
            r2v = r2.rearrange("p (n q) -> p n q", q=Q)
            r3v = r3.rearrange("p (n q) -> p n q", q=Q)
            r4v = r4.rearrange("p (n q) -> p n q", q=Q)
            r5v = r5.rearrange("p (n q) -> p n q", q=Q)
            rho_b = RHO[:, g:g + 1].unsqueeze(1).broadcast_to([128, NCH, Q])
            P.small_mode = True
            P.v(lambda e, rho_b=rho_b, r5v=r5v: e.tensor_copy(r5v, rho_b))
            P.v(lambda e, r5v=r5v: e.memset(r5v[:, :, 0:1], 0.0))
            P.small_mode = False
            for tc in range(4):
                hsl = h3[:, g, tc * 512:(tc + 1) * 512]
                P.v(lambda e, hsl=hsl, tc=tc, r0=r0: e.tensor_tensor(
                    r0[:, tc * 512:(tc + 1) * 512], hsl, PX[tc][:, :], ALU.mult))
                tk = P.v(lambda e, tc=tc, r0=r0: e.tensor_tensor(
                    r0[:, tc * 512:(tc + 1) * 512], r0[:, tc * 512:(tc + 1) * 512],
                    PX[4 + tc][:, :], ALU.add))
            P.wait("scalar", tk)
            tk = P.s(lambda e, g=g, r0=r0: e.activation(
                r0, r0, AF.Silu,
                bias=SM[:, SC["ln1_b"] + i * 4 + g: SC["ln1_b"] + i * 4 + g + 1],
                scale=SM[:, SC["ln1_w"] + i * 4 + g: SC["ln1_w"] + i * 4 + g + 1]))
            P.wait("vector", tk)
            cbb = cb[:, g, 0:Q].unsqueeze(1).broadcast_to([128, NCH, Q])
            nsbb = nsb[:, g, 0:Q].unsqueeze(1).broadcast_to([128, NCH, Q])
            sbb = sb[:, g, 0:Q].unsqueeze(1).broadcast_to([128, NCH, Q])
            P.v(lambda e, r1v=r1v, r0v=r0v, cbb=cbb: e.tensor_tensor(r1v, r0v, cbb, ALU.mult))
            P.v(lambda e, r2v=r2v, r0v=r0v, nsbb=nsbb: e.tensor_tensor(r2v, r0v, nsbb, ALU.mult))
            P.small_mode = True
            P.v(lambda e, r3=r3, r5=r5, r1=r1: e.tensor_tensor_scan(r3, r5, r1, 0.0, ALU.mult, ALU.add))
            P.v(lambda e, r4=r4, r5=r5, r2=r2: e.tensor_tensor_scan(r4, r5, r2, 0.0, ALU.mult, ALU.add))
            urE = r3v[:, :, Q - 1]
            uiE = r4v[:, :, Q - 1]
            cbq = cb[:, g, Q - 1:Q]
            sbq = sb[:, g, Q - 1:Q]
            P.v(lambda e, urE=urE, cbq=cbq: e.tensor_scalar(ER, urE, cbq, None, ALU.mult))
            P.v(lambda e, uiE=uiE, sbq=sbq: e.tensor_scalar(TMPA, uiE, sbq, None, ALU.mult))
            P.v(lambda e: e.tensor_tensor(ER, ER, TMPA, ALU.subtract))
            P.v(lambda e, urE=urE, sbq=sbq: e.tensor_scalar(EI, urE, sbq, None, ALU.mult))
            P.v(lambda e, uiE=uiE, cbq=cbq: e.tensor_scalar(TMPA, uiE, cbq, None, ALU.mult))
            P.v(lambda e: e.tensor_tensor(EI, EI, TMPA, ALU.add))
            cq1 = cq[:, g, 1:17]
            sq1 = sq[:, g, 1:17]
            P.v(lambda e, cq1=cq1: e.tensor_tensor(ETR, ER, cq1, ALU.mult))
            P.v(lambda e, sq1=sq1: e.tensor_tensor(TMPA, EI, sq1, ALU.mult))
            P.v(lambda e: e.tensor_tensor(ETR, ETR, TMPA, ALU.add))
            P.v(lambda e, cq1=cq1: e.tensor_tensor(ETI, EI, cq1, ALU.mult))
            P.v(lambda e, sq1=sq1: e.tensor_tensor(TMPA, ER, sq1, ALU.mult))
            P.v(lambda e: e.tensor_tensor(ETI, ETI, TMPA, ALU.subtract))
            rq = rp[:, g, Q - 1:Q]
            P.v(lambda e, g=g, rq=rq, lcre=lcre: e.scalar_tensor_tensor(
                ETR[:, 0:1], lcre[:, g:g + 1], rq, ETR[:, 0:1], ALU.mult, ALU.add))
            P.v(lambda e, g=g, rq=rq, lcim=lcim: e.scalar_tensor_tensor(
                ETI[:, 0:1], lcim[:, g:g + 1], rq, ETI[:, 0:1], ALU.mult, ALU.add))
            P.v(lambda e, rq=rq: e.tensor_copy(D0C, rq.broadcast_to([128, 16])))
            P.v(lambda e: e.memset(D0C[:, 0:1], 0.0))
            P.v(lambda e: e.tensor_tensor_scan(VR, D0C, ETR, 0.0, ALU.mult, ALU.add))
            P.v(lambda e: e.tensor_tensor_scan(VI, D0C, ETI, 0.0, ALU.mult, ALU.add))
            P.v(lambda e: e.tensor_copy(VSR[:, 1:16], VR[:, 0:15]))
            P.v(lambda e, g=g, lcre=lcre: e.tensor_copy(VSR[:, 0:1], lcre[:, g:g + 1]))
            P.v(lambda e: e.tensor_copy(VSI[:, 1:16], VI[:, 0:15]))
            P.v(lambda e, g=g, lcim=lcim: e.tensor_copy(VSI[:, 0:1], lcim[:, g:g + 1]))
            cq0 = cq[:, g, 0:16]
            sq0 = sq[:, g, 0:16]
            P.v(lambda e, cq0=cq0: e.tensor_tensor(GR, VSR, cq0, ALU.mult))
            P.v(lambda e, sq0=sq0: e.tensor_tensor(TMPA, VSI, sq0, ALU.mult))
            P.v(lambda e: e.tensor_tensor(GR, GR, TMPA, ALU.subtract))
            P.v(lambda e, sq0=sq0: e.tensor_tensor(GI, VSR, sq0, ALU.mult))
            P.v(lambda e, cq0=cq0: e.tensor_tensor(TMPA, VSI, cq0, ALU.mult))
            P.v(lambda e: e.tensor_tensor(GI, GI, TMPA, ALU.add))
            cs1g = CS1[:, g:g + 1]
            sn1g = SN1[:, g:g + 1]
            P.v(lambda e, cs1g=cs1g: e.tensor_scalar(HR, GR, cs1g, None, ALU.mult))
            P.v(lambda e, sn1g=sn1g: e.tensor_scalar(TMPA, GI, sn1g, None, ALU.mult))
            P.v(lambda e: e.tensor_tensor(HR, HR, TMPA, ALU.subtract))
            P.v(lambda e, sn1g=sn1g: e.tensor_scalar(HI, GR, sn1g, None, ALU.mult))
            P.v(lambda e, cs1g=cs1g: e.tensor_scalar(TMPA, GI, cs1g, None, ALU.mult))
            P.v(lambda e: e.tensor_tensor(HI, HI, TMPA, ALU.add))
            P.small_mode = False
            rpb = rp[:, g, :].unsqueeze(1).broadcast_to([128, NCH, Q])
            hrb = HR.unsqueeze(2).broadcast_to([128, NCH, Q])
            hib = HI.unsqueeze(2).broadcast_to([128, NCH, Q])
            P.v(lambda e, r0v=r0v, rpb=rpb, hrb=hrb: e.tensor_tensor(r0v, rpb, hrb, ALU.mult))
            P.v(lambda e, r1=r1, r3=r3, r0=r0: e.tensor_tensor(r1, r3, r0, ALU.add))
            P.v(lambda e, r0v=r0v, rpb=rpb, hib=hib: e.tensor_tensor(r0v, rpb, hib, ALU.mult))
            P.v(lambda e, r2=r2, r4=r4, r0=r0: e.tensor_tensor(r2, r4, r0, ALU.add))
            P.v(lambda e, r0v=r0v, r1v=r1v, cbb=cbb: e.tensor_tensor(r0v, r1v, cbb, ALU.mult))
            P.v(lambda e, r1v=r1v, r2v=r2v, sbb=sbb: e.tensor_tensor(r1v, r2v, sbb, ALU.mult))
            P.v(lambda e, r0=r0, r1=r1: e.tensor_tensor(r0, r0, r1, ALU.subtract))
            tok_hg = P.v(lambda e, g=g, r0=r0: e.scalar_tensor_tensor(
                h3[:, g, :], r0, SM[:, SC["sc_lin"] + i * 4 + g: SC["sc_lin"] + i * 4 + g + 1],
                h3[:, g, :], ALU.mult, ALU.add))
            if stop == "dbgl0" and i == 0 and g == 0:
                P.wait("gpsimd", tok_hg)
                for off, src in ((0, tCB), (516, tSB), (1032, tRP), (1544, SC4),
                                 (1576, S16), (2048, R[0]), (4096, R[1]),
                                 (6144, R[2]), (8192, R[3]), (10240, R[4]),
                                 (12288, R[5]), (14336, h3[:, 0, :])):
                    n = src.shape[-1] if src.ndim == 2 else src.free_size()
                    tg = P.dma("gpsimd", d_tap[:, off:off + n], src)
                P.wait("gpsimd", tg)
                return None
        for b in range(8):
            guard[b] = tok_hg
        tok_h = tok_hg
        P.wait("scalar", tok_h)
        tok_sq = P.s(lambda e: e.activation(sqb, hf, AF.Square))
        repl_toks = []
        st_toks = []
        for tc in range(4):
            b = tc
            bankw(b)
            P.wait("tensor", tok_h)
            for g in range(4):
                P.t(lambda e, g=g, tc=tc, b=b: e.matmul(
                    PX[b][0:1, 0:512], ONESK[:, :], h3[:, g, tc * 512:(tc + 1) * 512],
                    start=(g == 0), stop=(g == 3)))
            P.wait("tensor", tok_sq)
            for g in range(4):
                tok_st = P.t(lambda e, g=g, tc=tc, b=b: e.matmul(
                    PX[b][32:33, 0:512], ONESK[:, :], sq3[:, g, tc * 512:(tc + 1) * 512],
                    start=(g == 0), stop=(g == 3)))
            st_toks.append(tok_st)
        for tc in range(4):
            tok_coef = coef_rows(PX[tc], 512, D, st_toks[tc])
            guard[tc] = tok_coef
            repl_toks.append(repl(tc, 4 + tc, 512, tok_coef))
        P.waits("vector", *repl_toks)
        P.wait("vector", tok_st)
        tok_zh = tok_zl = None
        for tc in range(4):
            hs = h3[:, :, tc * 512:(tc + 1) * 512]
            rb = PX[tc][:, :].unsqueeze(1).broadcast_to([128, 4, 512])
            cbk = PX[4 + tc][:, :].unsqueeze(1).broadcast_to([128, 4, 512])
            P.v(lambda e, hs=hs, rb=rb: e.tensor_tensor(ztmp3, hs, rb, ALU.mult))
            P.v(lambda e, cbk=cbk: e.tensor_tensor(ztmp3, ztmp3, cbk, ALU.add))
            for g in range(4):
                tok_zt = P.v(lambda e, g=g: e.tensor_scalar(
                    ztmp3[:, g, :], ztmp3[:, g, :],
                    SM[:, SC["ln2_w"] + i * 4 + g: SC["ln2_w"] + i * 4 + g + 1],
                    SM[:, SC["ln2_b"] + i * 4 + g: SC["ln2_b"] + i * 4 + g + 1],
                    ALU.mult, ALU.add))
            P.waits("scalar", tok_zt, tok_st)
            tok_zh = P.s(lambda e, tc=tc: e.activation(
                ZH[:, :, tc * 512:(tc + 1) * 512], ztmp3, AF.Copy))
            P.wait("vector", tok_zh)
            tok_zl = P.v(lambda e, tc=tc: e.tensor_tensor(
                ZL[:, :, tc * 512:(tc + 1) * 512], ztmp3,
                ZH[:, :, tc * 512:(tc + 1) * 512], ALU.subtract))
        for b in range(8):
            guard[b] = tok_zl
        P.waits("tensor", tok_w1h, tok_w1l, tok_w2h, tok_w2l, tok_zh, tok_zl)
        tok_w1d = None
        tok_w2d = None
        mtmp_free = [None, None]
        for tc in range(4):
            tok_mh = tok_ml = None
            for fg in range(FG):
                b = fg % 2
                bankw(b)
                P.wait("tensor", tok_mid_free)
                for kg in range(4):
                    for si, (ws, zs2) in enumerate(((W1H, ZH), (W1H, ZL), (W1L, ZH))):
                        tok_w1d = P.t(lambda e, fg=fg, kg=kg, tc=tc, b=b, ws=ws, zs2=zs2,
                                      st=(kg == 0 and si == 0), sp=(kg == 3 and si == 2):
                                      e.matmul(PX[b][:, :], ws[:, kg, fg * 128:(fg + 1) * 128],
                                               zs2[:, kg, tc * 512:(tc + 1) * 512],
                                               start=st, stop=sp))
                mt = mtmp[fg % 2]
                P.waits("scalar", tok_w1d, tok_mid_free, mtmp_free[fg % 2])
                tok_silu = P.s(lambda e, fg=fg, b=b, mt=mt: e.activation(
                    mt, PX[b][:, :], AF.Silu,
                    bias=SM[:, SC["b1"] + i * 16 + fg: SC["b1"] + i * 16 + fg + 1]))
                guard[b] = tok_silu
                tok_mh = P.s(lambda e, fg=fg, mt=mt: e.activation(
                    MIDH[:, fg, :], mt, AF.Copy))
                P.wait("vector", tok_mh)
                tok_ml = P.v(lambda e, fg=fg, mt=mt: e.tensor_tensor(
                    MIDL[:, fg, :], mt, MIDH[:, fg, :], ALU.subtract))
                mtmp_free[fg % 2] = tok_ml
            P.waits("tensor", tok_mh, tok_ml)
            for dg in range(4):
                b = 2 + dg % 2
                bankw(b)
                for kg in range(FG):
                    for si, (ws, ms) in enumerate(((W2H, MIDH), (W2H, MIDL), (W2L, MIDH))):
                        tok_w2d = P.t(lambda e, dg=dg, kg=kg, b=b, ws=ws, ms=ms,
                                      st=(kg == 0 and si == 0), sp=(kg == FG - 1 and si == 2):
                                      e.matmul(PX[b][:, :], ws[:, kg, dg * 128:(dg + 1) * 128],
                                               ms[:, kg, :], start=st, stop=sp))
                P.wait("vector", tok_w2d)
                tok_h = P.v(lambda e, dg=dg, tc=tc, b=b: e.scalar_tensor_tensor(
                    h3[:, dg, tc * 512:(tc + 1) * 512], PX[b][:, :],
                    SM[:, SC["b2"] + i * 4 + dg: SC["b2"] + i * 4 + dg + 1],
                    h3[:, dg, tc * 512:(tc + 1) * 512], ALU.add, ALU.add))
                guard[b] = tok_h
            tok_mid_free = tok_w2d
        tok_w1_used = tok_w1d
        tok_w2_used = tok_w2d
        tok_stage_free = tok_w2d
        tok_scanbufs_free = tok_w1d

    for i in range(DEPTH):
        rr_ = inner_layer(i)
        if stop == "dbgl0" and i == 0:
            return nc, P
        if stop == f"l{i}":
            P.wait("gpsimd", tok_h)
            tg = P.dma("gpsimd", d_tap[:, 0:8192], hf)
            P.wait("gpsimd", tg)
            return nc, P

    tok_woh = tok_wol = None
    for hh in range(2):
        P.waits("sync", tok_w2_used, tok_woh, tok_wol)
        tok_wst = P.dma("sync", wostg, d_woutT[:, 2 * hh:2 * hh + 2, :])
        P.waits("scalar", tok_wst, tok_w2_used)
        tok_woh = P.s(lambda e, hh=hh: e.activation(
            WOH[:, 2 * hh:2 * hh + 2, :], wostg, AF.Copy))
        P.waits("gpsimd", tok_woh, tok_w1_used)
        tok_wol = P.g(lambda e, hh=hh: e.tensor_tensor(
            WOL[:, 2 * hh:2 * hh + 2, :], wostg, WOH[:, 2 * hh:2 * hh + 2, :],
            ALU.subtract))
    P.waits("scalar", tok_h, tok_scanbufs_free)
    tok_sq = P.s(lambda e: e.activation(sqb, hf, AF.Square))
    sq3 = A2[:, 0:8192].rearrange("p (g t) -> p g t", g=4)
    repl_toks = []
    st_toks = []
    for tc in range(4):
        b = tc
        bankw(b)
        P.wait("tensor", tok_h)
        for g in range(4):
            P.t(lambda e, g=g, tc=tc, b=b: e.matmul(
                PX[b][0:1, 0:512], ONESK[:, :], h3[:, g, tc * 512:(tc + 1) * 512],
                start=(g == 0), stop=(g == 3)))
        P.wait("tensor", tok_sq)
        for g in range(4):
            tok_st = P.t(lambda e, g=g, tc=tc, b=b: e.matmul(
                PX[b][32:33, 0:512], ONESK[:, :], sq3[:, g, tc * 512:(tc + 1) * 512],
                start=(g == 0), stop=(g == 3)))
        st_toks.append(tok_st)
    for tc in range(4):
        tok_coef = coef_rows(PX[tc], 512, D, st_toks[tc])
        guard[tc] = tok_coef
        repl_toks.append(repl(tc, 4 + tc, 512, tok_coef))
    P.waits("vector", *repl_toks)
    P.wait("vector", tok_st)
    for tc in range(4):
        hns = hn3[:, :, tc * 512:(tc + 1) * 512]
        hs = h3[:, :, tc * 512:(tc + 1) * 512]
        rb = PX[tc][:, :].unsqueeze(1).broadcast_to([128, 4, 512])
        cbk = PX[4 + tc][:, :].unsqueeze(1).broadcast_to([128, 4, 512])
        P.v(lambda e, hns=hns, hs=hs, rb=rb: e.tensor_tensor(hns, hs, rb, ALU.mult))
        P.v(lambda e, hns=hns, cbk=cbk: e.tensor_tensor(hns, hns, cbk, ALU.add))
        for g in range(4):
            tok_hn = P.v(lambda e, g=g, tc=tc: e.tensor_scalar(
                hn3[:, g, tc * 512:(tc + 1) * 512], hn3[:, g, tc * 512:(tc + 1) * 512],
                SM[:, SC["lnl_w"] + g: SC["lnl_w"] + g + 1],
                SM[:, SC["lnl_b"] + g: SC["lnl_b"] + g + 1], ALU.mult, ALU.add))
    for b in range(8):
        guard[b] = tok_hn
    sel0 = SM[:, SC["sel"]:SC["sel"] + 1]
    sel1 = SM[:, SC["sel"] + 1:SC["sel"] + 2]
    P.v(lambda e: e.tensor_scalar(hno3[:, :, :], hn3[:, :, 0:TOK], sel0, None, ALU.mult))
    tok_hno = P.v(lambda e: e.scalar_tensor_tensor(
        hno3[:, :, :], hn3[:, :, TOK:L], sel1, hno3[:, :, :], ALU.mult, ALU.add))

    P.waits("scalar", tok_hno, tok_wol)
    tok_hnh = P.s(lambda e: e.activation(HNHf, A2[:, 8192:12288], AF.Copy))
    P.wait("vector", tok_hnh)
    tok_hnl = P.v(lambda e: e.tensor_tensor(
        HNLf, A2[:, 8192:12288], HNHf, ALU.subtract))

    P.waits("tensor", tok_woh, tok_wol, tok_hnh, tok_hnl)
    tok_adma = [None, None]
    tok_asq = [None, None]
    tok_astat_sl = [None, None]
    tok_astat = None
    for og in range(VG):
        sl = og % 2
        for tc2 in range(2):
            b = sl * 2 + tc2
            bankw(b)
            for kg in range(4):
                for si, (ws, hs2) in enumerate(((WOH, HNH), (WOH, HNL), (WOL, HNH))):
                    tok_u = P.t(lambda e, og=og, kg=kg, tc2=tc2, b=b, ws=ws, hs2=hs2,
                                st=(kg == 0 and si == 0), sp=(kg == 3 and si == 2):
                                e.matmul(PX[b][:, :], ws[:, kg, og * 128:(og + 1) * 128],
                                         hs2[:, kg, tc2 * 512:(tc2 + 1) * 512],
                                         start=st, stop=sp))
            P.waits("vector", tok_u, tok_adma[sl], tok_asq[sl], tok_astat_sl[sl])
            tok_ev = P.v(lambda e, og=og, tc2=tc2, b=b, sl=sl: e.tensor_scalar(
                atile[sl][:, tc2 * 512:(tc2 + 1) * 512], PX[b][:, :],
                SM[:, SC["bout"] + og: SC["bout"] + og + 1], None, ALU.add))
            guard[b] = tok_ev
        P.wait("gpsimd", tok_ev)
        tok_adma[sl] = P.dma("gpsimd", d_ascr[:, og, :], atile[sl])
        P.waits("scalar", tok_ev, tok_astat_sl[sl])
        tok_asq[sl] = P.s(lambda e, sl=sl: e.activation(asq[sl], atile[sl], AF.Square))
        for tc2 in range(2):
            b = 4 + tc2
            if og == 0:
                bankw(b)
            P.wait("tensor", tok_ev)
            P.t(lambda e, og=og, tc2=tc2, b=b, sl=sl: e.matmul(
                PX[b][0:1, 0:512], ONESK[:, :], atile[sl][:, tc2 * 512:(tc2 + 1) * 512],
                start=(og == 0), stop=(og == VG - 1)))
            P.wait("tensor", tok_asq[sl])
            tok_astat = P.t(lambda e, og=og, tc2=tc2, b=b, sl=sl: e.matmul(
                PX[b][32:33, 0:512], ONESK[:, :], asq[sl][:, tc2 * 512:(tc2 + 1) * 512],
                start=(og == 0), stop=(og == VG - 1)))
        tok_astat_sl[sl] = tok_astat
    for tc2 in range(2):
        tok_coef = coef_rows(PX[4 + tc2], 512, V, tok_astat)
        guard[4 + tc2] = tok_coef
        P.v(lambda e, tc2=tc2: e.tensor_copy(RROWB[0:1, tc2 * 512:(tc2 + 1) * 512], RROW[0:1, :]))
        tok_arows = P.v(lambda e, tc2=tc2: e.tensor_copy(CROWB[0:1, tc2 * 512:(tc2 + 1) * 512], MUROW[0:1, :]))

    P.waits("sync", tok_u)
    tok_ats = P.dma("sync", atstg, d_aT)
    P.waits("scalar", tok_ats, tok_hn, st_toks[-1])
    tok_ath = P.s(lambda e: e.activation(ATHf, atstg, AF.Copy))
    P.waits("vector", tok_ath, tok_hn)
    tok_atl = P.v(lambda e: e.tensor_tensor(ATLf, atstg, ATHf, ALU.subtract))

    m8 = RT[:, 0:8]
    lg = RT[:, 8:16]
    qe = RT[:, 16:24]
    msk = RT[:, 24:32]
    den = RT[:, 32:33]
    nm1 = RT[:, 33:34]
    r2 = RT[:, 34:35]

    tok_x1dma = None
    tok_sqstat = None
    tok_vmm = tok_u
    tok_aq_free = None
    tok_vc = tok_vs = None
    tok_wb_mm = [None, None]
    tok_wbh = [None, None]
    tok_wbl = [None, None]
    tok_bth = [None, None]
    tok_btl = [None, None]
    tok_fdma = None
    tok_fsq = None
    tok_fstat = None
    tok_h2h_prev = None
    frows_tok = [None, None]
    tok_base = None
    tok_wbstg_free = [tok_hnh, tok_hnl]
    for th in range(2):
        vsplit = []
        for sub in range(2):
            qc = th * 2 + sub
            P.waits("sync", tok_adma[0], tok_adma[1], tok_astat, tok_x1dma,
                    tok_aq_free, tok_vc, tok_vs)
            tok_aq = P.dma("sync", aq3, d_ascr[:, :, qc * 256:(qc + 1) * 256])
            P.waits("sync", tok_ath, tok_atl, tok_sqstat)
            tok_xq = P.dma("sync", xq3, d_xown[:, :, qc * 256:(qc + 1) * 256])
            tok_r = repl(6, 7, 256, tok_arows,
                         src_r=RROWB[0:1, qc * 256:(qc + 1) * 256],
                         src_c=CROWB[0:1, qc * 256:(qc + 1) * 256])
            P.waits("vector", tok_aq, tok_xq, tok_r)
            rb = PX[6][:, 0:256].unsqueeze(1).broadcast_to([128, VG, 256])
            cbk = PX[7][:, 0:256].unsqueeze(1).broadcast_to([128, VG, 256])
            P.v(lambda e, rb=rb: e.tensor_tensor(aq3, aq3, rb, ALU.mult))
            P.v(lambda e, cbk=cbk: e.tensor_tensor(aq3, aq3, cbk, ALU.add))
            tok_x1 = P.v(lambda e: e.tensor_tensor(aq3, aq3, xq3, ALU.add))
            guard[6] = tok_x1
            guard[7] = tok_x1
            P.wait("gpsimd", tok_x1)
            tok_x1dma = P.dma("gpsimd", d_x1scr[:, :, qc * 256:(qc + 1) * 256], aq3)
            P.waits("scalar", tok_x1, tok_sqstat)
            tok_sq = P.s(lambda e: e.activation(sqv, A2[:, 0:8192], AF.Square))
            bankw(4)
            P.wait("tensor", tok_x1)
            for vg in range(VG):
                P.t(lambda e, vg=vg: e.matmul(
                    PX[4][0:1, 0:256], ONESK[:, :], aq3[:, vg, :],
                    start=(vg == 0), stop=(vg == VG - 1)))
            P.wait("tensor", tok_sq)
            for vg in range(VG):
                tok_st2 = P.t(lambda e, vg=vg: e.matmul(
                    PX[4][32:33, 0:256], ONESK[:, :], sqv3[:, vg, :],
                    start=(vg == 0), stop=(vg == VG - 1)))
            tok_sqstat = tok_st2
            tok_coef = coef_rows(PX[4], 256, V, tok_st2)
            guard[4] = tok_coef
            tok_r2 = repl(5, 0, 256, tok_coef)
            P.wait("vector", tok_r2)
            rb2 = PX[5][:, 0:256].unsqueeze(1).broadcast_to([128, VG, 256])
            cb2 = PX[0][:, 0:256].unsqueeze(1).broadcast_to([128, VG, 256])
            P.v(lambda e, rb2=rb2: e.tensor_tensor(aq3, aq3, rb2, ALU.mult))
            tok_v = P.v(lambda e, cb2=cb2: e.tensor_tensor(aq3, aq3, cb2, ALU.add))
            guard[5] = tok_v
            guard[0] = tok_v
            P.waits("scalar", tok_v, tok_vmm)
            tok_vc = P.s(lambda e, sub=sub: e.activation(
                VH[:, :, sub * 256:(sub + 1) * 256], aq3, AF.Copy))
            P.waits("vector", tok_vc, tok_vmm)
            tok_vs = P.v(lambda e, sub=sub: e.tensor_tensor(
                VL[:, :, sub * 256:(sub + 1) * 256], aq3,
                VH[:, :, sub * 256:(sub + 1) * 256], ALU.subtract))
            vsplit.extend([tok_vc, tok_vs])
        tok_h2 = None
        for tt in range(4):
            bankw(3)
            P.waits("tensor", tok_ath, tok_atl, *vsplit)
            for vg in range(VG):
                for si, (vv, aa) in enumerate(((VH, ATH), (VL, ATH), (VH, ATL))):
                    tok_ha = P.t(lambda e, vg=vg, tt=tt, vv=vv, aa=aa,
                                 st=(vg == 0 and si == 0),
                                 sp=(vg == VG - 1 and si == 2):
                                 e.matmul(PX[3][:, 0:136],
                                          vv[:, vg, tt * 128:(tt + 1) * 128],
                                          aa[:, vg, :], start=st, stop=sp))
            P.wait("vector", tok_ha)
            P.small_mode = True
            P.v(lambda e: e.tensor_copy(lg, PX[3][:, 128:136]))
            tok_lg = P.v(lambda e: e.max(m8, lg))
            tok_nm1 = P.v(lambda e: e.tensor_scalar(nm1, m8[:, 0:1], -1.0, None, ALU.mult))
            P.wait("scalar", tok_nm1)
            tok_q = P.s(lambda e: e.activation(qe, lg, AF.Exp, bias=nm1))
            P.wait("vector", tok_q)
            P.v(lambda e: e.tensor_single_scalar(msk, lg, m8[:, 1:2], ALU.is_ge))
            P.v(lambda e: e.tensor_tensor(qe, qe, msk, ALU.mult))
            P.v(lambda e: e.reduce_sum(den, qe, axis=AXX))
            P.v(lambda e: e.reciprocal(r2, den))
            P.v(lambda e: e.tensor_scalar(r2, r2, 2.0, None, ALU.mult))
            P.v(lambda e: e.tensor_scalar(qe, qe, r2, None, ALU.mult))
            cmb = qe.rearrange("p (a b) -> p a b", b=1).broadcast_to([128, 8, 16])
            tok_cp = P.v(lambda e: e.tensor_copy(H2W, PX[3][:, 0:128]))
            guard[3] = tok_cp
            tok_h2t = P.v(lambda e, cmb=cmb: e.tensor_tensor(
                H2T.rearrange("p (a b) -> p a b", b=16),
                H2W.rearrange("p (a b) -> p a b", b=16), cmb, ALU.mult))
            P.small_mode = False
            bankw(6)
            P.waits("tensor", tok_h2t, tok_id)
            tok_tr = P.t(lambda e: e.transpose(PX[6][:, 0:128], H2T, IDT[:, :]))
            P.waits("vector", tok_tr, tok_h2h_prev)
            tok_h2 = P.v(lambda e, tt=tt: e.tensor_copy(
                h2b[:, tt * 128:(tt + 1) * 128], PX[6][:, 0:128]))
            guard[6] = tok_h2
        P.waits("scalar", tok_h2, tok_wb_mm[0], tok_wb_mm[1])
        tok_h2h = P.s(lambda e: e.activation(H2Hf, h2b, AF.Copy))
        tok_h2h_prev = tok_h2h
        P.waits("vector", tok_h2h, tok_wb_mm[0], tok_wb_mm[1])
        tok_h2l = P.v(lambda e: e.tensor_tensor(H2Lf, h2b, H2Hf, ALU.subtract))
        for og in range(VG):
            sl = og % 2
            P.waits("sync", tok_wbh[1 - sl], tok_wbl[1 - sl],
                    tok_bth[1 - sl], tok_btl[1 - sl], *tok_wbstg_free)
            tok_wbstg_free = []
            tok_sg = P.dma("sync", wbstg, d_wbT[og])
            tok_btg = P.dma("sync", btstg[sl], d_bT[:, og * 128:(og + 1) * 128])
            P.waits("scalar", tok_sg, tok_wb_mm[sl], *vsplit)
            tok_wbh[sl] = P.s(lambda e, sl=sl: e.activation(WBHf[sl], wbstg, AF.Copy))
            P.waits("vector", tok_wbh[sl], tok_wb_mm[sl], *vsplit)
            tok_wbl[sl] = P.v(lambda e, sl=sl: e.tensor_tensor(
                WBLf[sl], wbstg, WBHf[sl], ALU.subtract))
            P.waits("scalar", tok_btg, tok_wb_mm[sl])
            tok_bth[sl] = P.s(lambda e, sl=sl: e.activation(BTH[sl], btstg[sl], AF.Copy))
            P.waits("vector", tok_bth[sl], tok_wb_mm[sl])
            tok_btl[sl] = P.v(lambda e, sl=sl: e.tensor_tensor(
                BTL[sl], btstg[sl], BTH[sl], ALU.subtract))
            b = sl
            bankw(b)
            P.waits("tensor", tok_wbh[sl], tok_wbl[sl], tok_bth[sl], tok_btl[sl],
                    tok_h2h, tok_h2l)
            for kg in range(VG):
                for si, (wv, vv) in enumerate(((WBH[sl], VH), (WBH[sl], VL), (WBL[sl], VH))):
                    P.t(lambda e, kg=kg, wv=wv, vv=vv, b=b,
                        st=(kg == 0 and si == 0):
                        e.matmul(PX[b][:, :], wv[:, kg, :], vv[:, kg, :],
                                 start=st, stop=False))
            for si, (bb, hh2) in enumerate(((BTH[sl], H2Hf), (BTH[sl], H2Lf), (BTL[sl], H2Hf))):
                tok_base = P.t(lambda e, bb=bb, hh2=hh2, b=b, sp=(si == 2):
                               e.matmul(PX[b][:, :], bb, hh2, start=False, stop=sp))
            tok_wb_mm[sl] = tok_base
            P.waits("vector", tok_base, tok_fdma, tok_fsq, tok_fstat)
            tok_ev = P.v(lambda e, og=og, b=b: e.tensor_scalar(
                ftile, PX[b][:, :], SM[:, SC["b_base"] + og: SC["b_base"] + og + 1],
                None, ALU.add))
            guard[b] = tok_ev
            P.wait("gpsimd", tok_ev)
            tok_fdma = P.dma("gpsimd", d_fscr[:, og, th * 512:(th + 1) * 512], ftile)
            P.waits("scalar", tok_ev, tok_fstat)
            tok_fsq = P.s(lambda e: e.activation(fsq, ftile, AF.Square))
            if og == 0:
                bankw(4)
            P.wait("tensor", tok_ev)
            P.t(lambda e, og=og: e.matmul(PX[4][0:1, 0:512], ONESK[:, :], ftile,
                                          start=(og == 0), stop=(og == VG - 1)))
            P.wait("tensor", tok_fsq)
            tok_fstat = P.t(lambda e, og=og: e.matmul(
                PX[4][32:33, 0:512], ONESK[:, :], fsq,
                start=(og == 0), stop=(og == VG - 1)))
        tok_coef = coef_rows(PX[4], 512, V, tok_fstat)
        guard[4] = tok_coef
        P.v(lambda e, th=th: e.tensor_copy(RROWB[0:1, th * 512:(th + 1) * 512], RROW[0:1, :]))
        frows_tok[th] = P.v(lambda e, th=th: e.tensor_copy(
            CROWB[0:1, th * 512:(th + 1) * 512], MUROW[0:1, :]))
        tok_vmm = tok_base
        tok_aq_free = tok_base

    tok_outd = None
    for ck in range(4):
        th2 = ck // 2
        P.waits("sync", tok_base, tok_fdma, tok_outd)
        tok_fc = P.dma("sync", fch3, d_fscr[:, :, ck * 256:(ck + 1) * 256])
        P.wait("sync", tok_x1dma)
        tok_xc = P.dma("sync", xch3, d_x1scr[:, :, ck * 256:(ck + 1) * 256])
        tok_r = repl(6, 7, 256, frows_tok[th2],
                     src_r=RROWB[0:1, ck * 256:(ck + 1) * 256],
                     src_c=CROWB[0:1, ck * 256:(ck + 1) * 256])
        P.waits("vector", tok_fc, tok_xc, tok_r)
        rb = PX[6][:, 0:256].unsqueeze(1).broadcast_to([128, VG, 256])
        cbk = PX[7][:, 0:256].unsqueeze(1).broadcast_to([128, VG, 256])
        P.v(lambda e, rb=rb: e.tensor_tensor(fch3, fch3, rb, ALU.mult))
        P.v(lambda e, cbk=cbk: e.tensor_tensor(fch3, fch3, cbk, ALU.add))
        tok_o = P.v(lambda e: e.tensor_tensor(fch3, fch3, xch3, ALU.add))
        guard[6] = tok_o
        guard[7] = tok_o
        P.wait("gpsimd", tok_o)
        tok_outd = P.dma("gpsimd", d_out[:, :, ck * 256:(ck + 1) * 256], fch3)
    P.wait("gpsimd", tok_outd)
    return nc, P

    tok_vq = None
    tok_x1dma = None
    vstat_rows = []
    for qc in range(4):
        P.waits("sync", tok_adma[0], tok_adma[1], tok_astat, tok_vq, tok_x1dma)
        tok_aq = P.dma("sync", aq3, d_ascr[:, :, qc * 256:(qc + 1) * 256])
        vslot = AR[:, qc * 8192:(qc + 1) * 8192]
        vslot3 = vslot.rearrange("p (v t) -> p v t", t=256)
        P.wait("sync", tok_u)
        tok_xq = P.dma("sync", vslot3, d_xown[:, :, qc * 256:(qc + 1) * 256])
        tok_r = repl(6, 7, 256, tok_arows,
                     src_r=RROWB[0:1, qc * 256:(qc + 1) * 256],
                     src_c=CROWB[0:1, qc * 256:(qc + 1) * 256])
        P.waits("vector", tok_aq, tok_xq, tok_r)
        rb = PX[6][:, 0:256].unsqueeze(1).broadcast_to([128, VG, 256])
        cbk = PX[7][:, 0:256].unsqueeze(1).broadcast_to([128, VG, 256])
        P.v(lambda e, rb=rb: e.tensor_tensor(aq3, aq3, rb, ALU.mult))
        P.v(lambda e, cbk=cbk: e.tensor_tensor(aq3, aq3, cbk, ALU.add))
        tok_x1 = P.v(lambda e, vslot3=vslot3: e.tensor_tensor(aq3, aq3, vslot3, ALU.add))
        guard[6] = tok_x1
        guard[7] = tok_x1
        P.wait("gpsimd", tok_x1)
        tok_x1dma = P.dma("gpsimd", d_x1scr[:, :, qc * 256:(qc + 1) * 256], aq3)
        P.wait("scalar", tok_x1)
        tok_sq = P.s(lambda e, vslot=vslot: e.activation(vslot, A2[:, 0:8192], AF.Square))
        b = 4
        bankw(b)
        P.wait("tensor", tok_x1)
        for vg in range(VG):
            P.t(lambda e, vg=vg: e.matmul(PX[4][0:1, 0:256], ONESK[:, :], aq3[:, vg, :],
                                          start=(vg == 0), stop=(vg == VG - 1)))
        P.wait("tensor", tok_sq)
        for vg in range(VG):
            tok_st = P.t(lambda e, vg=vg, vslot3=vslot3: e.matmul(
                PX[4][32:33, 0:256], ONESK[:, :], vslot3[:, vg, :],
                start=(vg == 0), stop=(vg == VG - 1)))
        tok_coef = coef_rows(PX[4], 256, V, tok_st)
        guard[4] = tok_coef
        tok_r = repl(5, 0, 256, tok_coef)
        P.wait("vector", tok_r)
        rb = PX[5][:, 0:256].unsqueeze(1).broadcast_to([128, VG, 256])
        cbk = PX[0][:, 0:256].unsqueeze(1).broadcast_to([128, VG, 256])
        P.v(lambda e, vslot3=vslot3, rb=rb: e.tensor_tensor(vslot3, aq3, rb, ALU.mult))
        tok_vq = P.v(lambda e, vslot3=vslot3, cbk=cbk: e.tensor_tensor(vslot3, vslot3, cbk, ALU.add))
        guard[5] = tok_vq
        guard[0] = tok_vq
        vstat_rows.append(tok_vq)

    if stop == "x1v":
        P.wait("gpsimd", tok_vq)
        tg = P.dma("gpsimd", d_tap[:, :], AR[:, :])
        P.wait("gpsimd", tg)
        return nc, P

    P.waits("sync", tok_vq)
    tok_at = P.dma("sync", aTS, d_aT)
    m8 = RT[:, 0:8]
    lg = RT[:, 8:16]
    qe = RT[:, 16:24]
    msk = RT[:, 24:32]
    den = RT[:, 32:33]
    nm1 = RT[:, 33:34]
    r2 = RT[:, 34:35]
    tok_h2 = None
    for tt in range(8):
        qc, th = tt // 2, tt % 2
        bankw(3)
        P.waits("tensor", tok_at, vstat_rows[qc])
        for vg in range(VG):
            lhs = AR4[:, qc, vg, th * 128:(th + 1) * 128]
            tok_ha = P.t(lambda e, vg=vg, lhs=lhs: e.matmul(
                PX[3][:, 0:136], lhs, aTS[:, vg, :],
                start=(vg == 0), stop=(vg == VG - 1)))
        tok_rt = tok_ha
        if stop == "p5a" and tt == 0:
            P.wait("gpsimd", tok_rt)
            tg = P.dma("gpsimd", d_tap[:, 0:1024], h2b)
            P.wait("gpsimd", tg)
            return nc, P
        P.wait("vector", tok_rt)
        P.small_mode = True
        P.v(lambda e: e.tensor_copy(lg, PX[3][:, 128:136]))
        tok_lg = P.v(lambda e: e.max(m8, lg))
        tok_nm1 = P.v(lambda e: e.tensor_scalar(nm1, m8[:, 0:1], -1.0, None, ALU.mult))
        P.wait("scalar", tok_nm1)
        tok_q = P.s(lambda e: e.activation(qe, lg, AF.Exp, bias=nm1))
        P.wait("vector", tok_q)
        P.v(lambda e: e.tensor_single_scalar(msk, lg, m8[:, 1:2], ALU.is_ge))
        P.v(lambda e: e.tensor_tensor(qe, qe, msk, ALU.mult))
        P.v(lambda e: e.reduce_sum(den, qe, axis=AXX))
        P.v(lambda e: e.reciprocal(r2, den))
        P.v(lambda e: e.tensor_scalar(r2, r2, 2.0, None, ALU.mult))
        P.v(lambda e: e.tensor_scalar(qe, qe, r2, None, ALU.mult))
        if stop == "p5b" and tt == 0:
            P.wait("gpsimd", tok_ha)
            tg = P.dma("gpsimd", d_tap[:, 0:1024], h2b)
            P.wait("gpsimd", tg)
            return nc, P
        cmb = qe.rearrange("p (a b) -> p a b", b=1).broadcast_to([128, 8, 16])
        P.wait("vector", tok_ha)
        tok_cp = P.v(lambda e: e.tensor_copy(H2W, PX[3][:, 0:128]))
        if stop == "p5c0" and tt == 0:
            P.wait("gpsimd", tok_cp)
            tg = P.dma("gpsimd", d_tap[:, 0:128], H2W)
            P.wait("gpsimd", tg)
            return nc, P
        guard[3] = tok_cp
        tok_h2t = P.v(lambda e, cmb=cmb: e.tensor_tensor(
            H2T.rearrange("p (a b) -> p a b", b=16),
            H2W.rearrange("p (a b) -> p a b", b=16), cmb, ALU.mult))
        P.small_mode = False
        if stop == "p5c" and tt == 0:
            P.wait("gpsimd", tok_h2t)
            tg = P.dma("gpsimd", d_tap[:, 0:128], H2T)
            P.wait("gpsimd", tg)
            return nc, P
        bankw(6)
        P.waits("tensor", tok_h2t, tok_id)
        tok_tr = P.t(lambda e: e.transpose(PX[6][:, 0:128], H2T, IDT[:, :]))
        P.wait("vector", tok_tr)
        tok_h2 = P.v(lambda e, tt=tt: e.tensor_copy(h2b[:, tt * 128:(tt + 1) * 128], PX[6][:, 0:128]))
        guard[6] = tok_h2
        if stop == "ffn_h2" and tt == 1:
            P.wait("gpsimd", tok_h2)
            tg = P.dma("gpsimd", d_tap[:, 0:1024], h2b)
            P.wait("gpsimd", tg)
            return nc, P

    P.wait("sync", tok_h2)
    tok_wb = [tok_at, tok_at]
    tok_fdma = None
    tok_fsq = None
    tok_fstat = None
    tok_evprev = None
    for og in range(VG):
        sl = og % 2
        P.wait("sync", tok_wb[sl])
        if sl == 1:
            P.wait("sync", tok_ha)
        tok_wbd = P.dma("sync", wbslot[sl], d_wbT[og])
        tok_btd = P.dma("sync", btslot[sl], d_bT[:, og * 128:(og + 1) * 128])
        evt = []
        for tc in range(2):
            b = (og % 2) * 2 + tc
            bankw(b)
            P.waits("tensor", tok_wbd, tok_btd, tok_h2)
            for kg in range(VG):
                rhs = AR4[:, 2 * tc:2 * tc + 2, kg, :]
                P.t(lambda e, kg=kg, sl=sl, rhs=rhs, b=b: e.matmul(
                    PX[b][:, :], wbslot[sl][:, kg, :], rhs, start=(kg == 0), stop=False))
            tok_base = P.t(lambda e, sl=sl, tc=tc, b=b: e.matmul(
                PX[b][:, :], btslot[sl], h2b[:, tc * 512:(tc + 1) * 512], start=False, stop=True))
            tok_wb[sl] = tok_base
            P.waits("vector", tok_base, tok_fdma, tok_fsq, tok_fstat)
            tok_ev = P.v(lambda e, og=og, tc=tc, b=b: e.tensor_scalar(
                ftile[:, tc * 512:(tc + 1) * 512], PX[b][:, :],
                SM[:, SC["b_base"] + og: SC["b_base"] + og + 1], None, ALU.add))
            guard[b] = tok_ev
            evt.append(tok_ev)
        P.wait("gpsimd", evt[-1])
        tok_fdma = P.dma("gpsimd", d_fscr[:, og, :], ftile)
        P.waits("scalar", evt[-1], tok_fstat)
        tok_fsq = P.s(lambda e: e.activation(fsq, ftile, AF.Square))
        for tc in range(2):
            b = 4 + tc
            if og == 0:
                bankw(b)
            P.wait("tensor", evt[tc])
            P.t(lambda e, og=og, tc=tc, b=b: e.matmul(
                PX[b][0:1, 0:512], ONESK[:, :], ftile[:, tc * 512:(tc + 1) * 512],
                start=(og == 0), stop=(og == VG - 1)))
            P.wait("tensor", tok_fsq)
            tok_fstat = P.t(lambda e, og=og, tc=tc, b=b: e.matmul(
                PX[b][32:33, 0:512], ONESK[:, :], fsq[:, tc * 512:(tc + 1) * 512],
                start=(og == 0), stop=(og == VG - 1)))
        if stop == "ffn_f" and og == 1:
            P.wait("gpsimd", tok_fdma)
            tg = P.dma("gpsimd", d_tap[:, 0:1024], ftile)
            P.wait("gpsimd", tg)
            return nc, P
    for tc in range(2):
        tok_coef = coef_rows(PX[4 + tc], 512, V, tok_fstat)
        guard[4 + tc] = tok_coef
        P.v(lambda e, tc=tc: e.tensor_copy(RROWB[0:1, tc * 512:(tc + 1) * 512], RROW[0:1, :]))
        tok_frows = P.v(lambda e, tc=tc: e.tensor_copy(CROWB[0:1, tc * 512:(tc + 1) * 512], MUROW[0:1, :]))

    tok_outd = None
    for tc in range(2):
        fch = AR[:, 0:16384].rearrange("p (v t) -> p v t", t=512)
        xch = AR[:, 16384:32768].rearrange("p (v t) -> p v t", t=512)
        P.waits("sync", tok_base, tok_fdma, tok_outd)
        tok_fc = P.dma("sync", fch, d_fscr[:, :, tc * 512:(tc + 1) * 512])
        P.wait("sync", tok_x1dma)
        tok_xc = P.dma("sync", xch, d_x1scr[:, :, tc * 512:(tc + 1) * 512])
        tok_r = repl(6, 7, 512, tok_frows,
                     src_r=RROWB[0:1, tc * 512:(tc + 1) * 512],
                     src_c=CROWB[0:1, tc * 512:(tc + 1) * 512])
        P.waits("vector", tok_fc, tok_xc, tok_r)
        rb = PX[6][:, :].unsqueeze(1).broadcast_to([128, VG, 512])
        cbk = PX[7][:, :].unsqueeze(1).broadcast_to([128, VG, 512])
        P.v(lambda e, fch=fch, rb=rb: e.tensor_tensor(fch, fch, rb, ALU.mult))
        P.v(lambda e, fch=fch, cbk=cbk: e.tensor_tensor(fch, fch, cbk, ALU.add))
        tok_o = P.v(lambda e, fch=fch, xch=xch: e.tensor_tensor(fch, fch, xch, ALU.add))
        guard[6] = tok_o
        guard[7] = tok_o
        P.wait("gpsimd", tok_o)
        tok_outd = P.dma("gpsimd", d_out[:, :, tc * 512:(tc + 1) * 512], fch)
    P.wait("gpsimd", tok_outd)
    return nc, P


def finalize(nc, P):
    with (
        nc.semaphore("sD") as sD,
        nc.semaphore("sG") as sG,
        nc.semaphore("sT") as sT,
        nc.semaphore("sV") as sV,
        nc.semaphore("sS") as sS,
        nc.Block() as block,
    ):
        sems = dict(D=sD, G=sG, T=sT, V=sV, S=sS)

        def runner(ename):
            def run(e):
                for item in P.q[ename]:
                    if item[0] == "w":
                        e.wait_ge(sems[item[1]], item[2])
                    else:
                        _, fn, sem, amt = item
                        ins = fn(e)
                        if sem is not None:
                            ins.then_inc(sems[sem], amt)
            return run
        block.sync(runner("sync"))
        block.tensor(runner("tensor"))
        block.vector(runner("vector"))
        block.scalar(runner("scalar"))
        block.gpsimd(runner("gpsimd"))
    return nc


def build(stop=None):
    nc, P = build_program(stop)
    return finalize(nc, P)


def _pack_weights(inputs):
    f32 = np.float32
    g = lambda k: np.asarray(inputs[k], dtype=f32)
    w = {}
    w["winT"] = np.ascontiguousarray(g("L0_attn_Win").reshape(D, VG, 128).transpose(2, 1, 0))
    w["w1T"] = np.ascontiguousarray(g("L0_attn_W1").reshape(DEPTH, FF, DG, 128).transpose(0, 3, 2, 1))
    w["w2T"] = np.ascontiguousarray(g("L0_attn_W2").reshape(DEPTH, D, FG, 128).transpose(0, 3, 2, 1))
    w["woutT"] = np.ascontiguousarray(g("L0_attn_Wout").reshape(V, DG, 128).transpose(2, 1, 0))
    w["wbT"] = np.ascontiguousarray(g("L0_ffn_W_base").reshape(VG, 128, VG, 128).transpose(0, 3, 2, 1))
    at = g("L0_ffn_A_stack").reshape(128, VG, 128).transpose(2, 1, 0)
    wrt = g("L0_ffn_W_router").reshape(8, VG, 128).transpose(2, 1, 0)
    w["aT"] = np.ascontiguousarray(np.concatenate([at, wrt], axis=2))
    w["bT"] = np.ascontiguousarray(g("L0_ffn_B_stack").transpose(0, 2, 1).reshape(128, V))
    w["id128"] = np.eye(128, dtype=f32)

    def chg(vec):
        return np.asarray(vec, f32).reshape(4, 128).T

    sm = np.zeros((128, NS), f32)
    sm[:, SC["bin"]:SC["bin"] + 4] = chg(g("L0_attn_bin"))
    for name, key in (("ln1_w", "L0_attn_ln1_w"), ("ln1_b", "L0_attn_ln1_b"),
                      ("ln2_w", "L0_attn_ln2_w"), ("ln2_b", "L0_attn_ln2_b"),
                      ("phz_re", "L0_attn_phz_re"), ("phz_im", "L0_attn_phz_im"),
                      ("lc_re", "L0_attn_lc_re"), ("lc_im", "L0_attn_lc_im"),
                      ("sc_lin", "L0_attn_sc_lin"), ("b2", "L0_attn_b2")):
        arr = g(key)
        for i in range(DEPTH):
            sm[:, SC[name] + i * 4: SC[name] + (i + 1) * 4] = chg(arr[i])
    sm[:, SC["lnl_w"]:SC["lnl_w"] + 4] = chg(g("L0_attn_lnl_w"))
    sm[:, SC["lnl_b"]:SC["lnl_b"] + 4] = chg(g("L0_attn_lnl_b"))
    b1 = g("L0_attn_b1")
    for i in range(DEPTH):
        sm[:, SC["b1"] + i * 16: SC["b1"] + (i + 1) * 16] = b1[i].reshape(16, 128).T
    sm[:, SC["bout"]:SC["bout"] + 32] = g("L0_attn_bout").reshape(32, 128).T
    sm[:, SC["b_base"]:SC["b_base"] + 32] = g("L0_ffn_b_base").reshape(32, 128).T
    w["smalls_base"] = sm
    return w


_CACHE = {}


def kernel(**inputs):
    key = "full"
    if key not in _CACHE:
        _CACHE[key] = build()
    nc = _CACHE[key]
    w = _pack_weights(inputs)
    x = np.asarray(inputs["L0_attn_x"], np.float32)
    in_maps = []
    for c in range(8):
        b, m = c // 2, c % 2
        xT = np.ascontiguousarray(x[b].reshape(L, VG, 128).transpose(2, 1, 0))
        xown = np.ascontiguousarray(xT[:, :, m * TOK:(m + 1) * TOK])
        sm = w["smalls_base"].copy()
        sm[:, SC["sel"]] = 1.0 if m == 0 else 0.0
        sm[:, SC["sel"] + 1] = 0.0 if m == 0 else 1.0
        in_maps.append({
            "xT": xT, "xown": xown, "winT": w["winT"], "w1T": w["w1T"],
            "w2T": w["w2T"], "woutT": w["woutT"], "wbT": w["wbT"],
            "aT": w["aT"], "bT": w["bT"],
            "smalls": sm, "id128": w["id128"],
        })
    res = run_bass_kernel_spmd(nc, in_maps, core_ids=list(range(8)))
    out = np.empty((4, L, V), np.float32)
    for c in range(8):
        b, m = c // 2, c % 2
        o = res.results[c]["OUT"]
        out[b, m * TOK:(m + 1) * TOK] = np.ascontiguousarray(
            np.transpose(o, (2, 1, 0)).reshape(TOK, V))
    return out


# version 09
# speedup vs baseline: 1.0798x; 1.0798x over previous
import math
import numpy as np
import concourse.bass as bass
import concourse.mybir as mybir
from concourse.bass_utils import run_bass_kernel_spmd

F32 = mybir.dt.float32
BF16 = mybir.dt.bfloat16
AF = mybir.ActivationFunctionType
ALU = mybir.AluOpType
AXX = mybir.AxisListType.X

V, D, FF, L, TOK = 4096, 512, 2048, 2048, 1024
VG, DG, FG = 32, 4, 16
DEPTH, Q, NCH = 4, 128, 16
EPS = 1e-5
EC = 256
NECK = L // EC

SC = {}
_c = 0
def _col(name, n):
    global _c
    SC[name] = _c
    _c += n
_col("bin", 4); _col("ln1_w", 16); _col("ln1_b", 16); _col("ln2_w", 16)
_col("ln2_b", 16); _col("phz_re", 16); _col("phz_im", 16); _col("lc_re", 16)
_col("lc_im", 16); _col("sc_lin", 16); _col("b2", 16); _col("lnl_w", 4)
_col("lnl_b", 4); _col("b1", 64); _col("bout", 32); _col("b_base", 32)
_col("sel", 2)
NS = _c


class Plan:
    ENGS = ("sync", "tensor", "vector", "scalar", "gpsimd")
    OWN = {"tensor": "T", "vector": "V", "scalar": "S"}

    def __init__(self):
        self.q = {e: [] for e in self.ENGS}
        self.cnt = dict(D=0, G=0, T=0, V=0, S=0)
        self.last_wait = {e: {} for e in self.ENGS}
        self.small_mode = False

    def wait(self, eng, tok):
        if tok is None:
            return
        s, v = tok
        if v <= 0:
            return
        if self.OWN.get(eng) == s:
            return
        lw = self.last_wait[eng]
        if lw.get(s, -1) >= v:
            return
        lw[s] = v
        self.q[eng].append(("w", s, v))

    def waits(self, eng, *toks):
        for t in toks:
            self.wait(eng, t)

    def do(self, eng, fn, sem=None, amt=1):
        tok = None
        if sem is not None:
            self.cnt[sem] += amt
            tok = (sem, self.cnt[sem])
            self.last_wait[eng][sem] = self.cnt[sem] if self.OWN.get(eng) == sem else self.last_wait[eng].get(sem, -1)
        self.q[eng].append(("i", fn, sem, amt))
        return tok

    def t(self, fn, sem="T"):
        return self.do("tensor", fn, sem)

    def v(self, fn, sem="V"):
        tok = self.do("vector", fn, sem)
        if self.small_mode:
            self.q["vector"].append(("w", "V", tok[1]))
        return tok

    def s(self, fn, sem="S"):
        tok = self.do("scalar", fn, sem)
        if self.small_mode:
            self.q["scalar"].append(("w", "S", tok[1]))
        return tok

    def g(self, fn):
        return self.do("gpsimd", fn, "G", 1)

    def vbar(self):
        self.q["vector"].append(("w", "V", self.cnt["V"]))

    def dma(self, eng, out_ap, in_ap):
        sem = "D" if eng == "sync" else "G"
        return self.do(eng, lambda e, o=out_ap, i=in_ap: e.dma_start(out=o, in_=i), sem, 16)


def build_program(stop=None):
    nc = bass.Bass("TRN2", target_bir_lowering=False)

    d_xT = nc.dram_tensor("xT", [128, VG, L], F32, kind="ExternalInput").ap()
    d_xown = nc.dram_tensor("xown", [128, VG, TOK], F32, kind="ExternalInput").ap()
    d_winT = nc.dram_tensor("winT", [128, VG, D], F32, kind="ExternalInput").ap()
    d_w1T = nc.dram_tensor("w1T", [DEPTH, 128, DG, FF], F32, kind="ExternalInput").ap()
    d_w2T = nc.dram_tensor("w2T", [DEPTH, 128, FG, D], F32, kind="ExternalInput").ap()
    d_woutT = nc.dram_tensor("woutT", [128, DG, V], F32, kind="ExternalInput").ap()
    d_wbT = nc.dram_tensor("wbT", [VG, 128, VG, 128], F32, kind="ExternalInput").ap()
    d_aT = nc.dram_tensor("aT", [128, VG, 136], F32, kind="ExternalInput").ap()
    d_bT = nc.dram_tensor("bT", [128, V], F32, kind="ExternalInput").ap()
    d_sm = nc.dram_tensor("smalls", [128, NS], F32, kind="ExternalInput").ap()
    d_id = nc.dram_tensor("id128", [128, 128], F32, kind="ExternalInput").ap()
    d_ascr = nc.dram_tensor("a_scr", [128, VG, TOK], F32, kind="Internal").ap()
    d_x1scr = nc.dram_tensor("x1_scr", [128, VG, TOK], F32, kind="Internal").ap()
    d_fscr = nc.dram_tensor("f_scr", [128, VG, TOK], F32, kind="Internal").ap()
    d_out = nc.dram_tensor("OUT", [128, VG, TOK], F32, kind="ExternalOutput").ap()
    d_tap = None
    if stop is not None:
        d_tap = nc.dram_tensor("TAP", [128, 32768], F32, kind="ExternalOutput").ap()

    AR_h = nc.alloc_sbuf_tensor("ARENA", [128, 32768], F32)
    AR = AR_h.ap()
    A2_h = nc.alloc_sbuf_tensor("ARENA2", [128, 12288], F32)
    A2 = A2_h.ap()
    ar_addr = nc.lookup_mloc(AR_h).addr
    a2_addr = nc.lookup_mloc(A2_h).addr

    def _alias(name, n_bf16, f32_off, base=None):
        return nc.alloc_sbuf_tensor_at(
            name, [128, n_bf16], BF16,
            offset=(ar_addr if base is None else base) + 4 * f32_off).ap()
    tCB = nc.alloc_sbuf_tensor("tCB", [128, 4 * 129], F32).ap()
    tSB = nc.alloc_sbuf_tensor("tSB", [128, 4 * 129], F32).ap()
    tNSB = nc.alloc_sbuf_tensor("tNSB", [128, 4 * 129], F32).ap()
    tRP = nc.alloc_sbuf_tensor("tRP", [128, 4 * 128], F32).ap()
    tCQ = nc.alloc_sbuf_tensor("tCQ", [128, 4 * 17], F32).ap()
    tSQ = nc.alloc_sbuf_tensor("tSQ", [128, 4 * 17], F32).ap()
    tS1 = nc.alloc_sbuf_tensor("tS1", [128, 4 * 64], F32).ap()
    tS2 = nc.alloc_sbuf_tensor("tS2", [128, 4 * 64], F32).ap()
    SC4 = nc.alloc_sbuf_tensor("SC4", [128, 32], F32).ap()
    S16 = nc.alloc_sbuf_tensor("S16", [128, 16 * 16], F32).ap()
    SM = nc.alloc_sbuf_tensor("SM", [128, NS], F32).ap()
    IDT = nc.alloc_sbuf_tensor("IDT", [128, 128], F32).ap()
    ONESK = nc.alloc_sbuf_tensor("ONESK", [128, 1], F32).ap()
    ONES1 = nc.alloc_sbuf_tensor("ONES1", [1, 128], F32).ap()
    MUROW = nc.alloc_sbuf_tensor("MUROW", [1, 512], F32).ap()
    RROW = nc.alloc_sbuf_tensor("RROW", [1, 512], F32).ap()
    SROW = nc.alloc_sbuf_tensor("SROW", [1, 512], F32).ap()
    RROWB = nc.alloc_sbuf_tensor("RROWB", [1, TOK], F32).ap()
    CROWB = nc.alloc_sbuf_tensor("CROWB", [1, TOK], F32).ap()
    WCS = nc.alloc_sbuf_tensor("WCS", [128, 4], F32).ap()

    PX = [nc.alloc_psum_tensor(f"PX{i}", [128, 512], F32).ap() for i in range(8)]

    h3 = AR[:, 24576:32768].rearrange("p (g t) -> p g t", t=L)
    hf = AR[:, 24576:32768]
    winTS = AR[:, 0:16384].rearrange("p (v d) -> p v d", d=D)
    woutTS = AR[:, 0:16384].rearrange("p (k o) -> p k o", o=V)
    AR4 = AR.rearrange("p (q v t) -> p q v t", v=VG, t=256)
    w1stg = AR[:, 0:8192].rearrange("p (k f) -> p k f", f=FF)
    w2stg = AR[:, 0:8192].rearrange("p (k d) -> p k d", d=D)
    W1Hf = _alias("W1H", 8192, 8192)
    W1Lf = _alias("W1L", 8192, 12288)
    W2Hf = _alias("W2H", 8192, 16384)
    W2Lf = _alias("W2L", 8192, 20480)
    MIDHf = _alias("MIDH", 8192, 0)
    MIDLf = _alias("MIDL", 8192, 4096)
    ZHf = _alias("ZH", 8192, 0, base=a2_addr)
    ZLf = _alias("ZL", 8192, 4096, base=a2_addr)
    W1H = W1Hf.rearrange("p (k f) -> p k f", f=FF)
    W1L = W1Lf.rearrange("p (k f) -> p k f", f=FF)
    W2H = W2Hf.rearrange("p (k d) -> p k d", d=D)
    W2L = W2Lf.rearrange("p (k d) -> p k d", d=D)
    MIDH = MIDHf.rearrange("p (k t) -> p k t", t=512)
    MIDL = MIDLf.rearrange("p (k t) -> p k t", t=512)
    ZH = ZHf.rearrange("p (k t) -> p k t", t=L)
    ZL = ZLf.rearrange("p (k t) -> p k t", t=L)
    ztmp3 = A2[:, 8192:10240].rearrange("p (k t) -> p k t", t=512)
    mtmp = [A2[:, 10240:10752], A2[:, 10752:11264]]

    R = [A2[:, j * 2048:(j + 1) * 2048] for j in range(6)]
    sqb = A2[:, 0:8192]
    hn3 = A2[:, 0:8192].rearrange("p (g t) -> p g t", t=L)
    hno3 = A2[:, 8192:12288].rearrange("p (g t) -> p g t", t=TOK)
    atile = [A2[:, 0:1024], A2[:, 1024:2048]]
    asq = [A2[:, 2048:3072], A2[:, 3072:4096]]
    aq3 = A2[:, 0:8192].rearrange("p (v t) -> p v t", t=256)
    EV = A2[:, 8192:8704]

    wostg = AR[:, 16384:24576].rearrange("p (k o) -> p k o", o=V)
    WOHf = _alias("WOH", 16384, 0)
    WOLf = _alias("WOL", 16384, 8192)
    WOH = WOHf.rearrange("p (k o) -> p k o", o=V)
    WOL = WOLf.rearrange("p (k o) -> p k o", o=V)
    HNHf = _alias("HNH", 4096, 16384)
    HNLf = _alias("HNL", 4096, 18432)
    HNH = HNHf.rearrange("p (k t) -> p k t", t=TOK)
    HNL = HNLf.rearrange("p (k t) -> p k t", t=TOK)
    VHf = _alias("VH", 16384, 0)
    VLf = _alias("VL", 16384, 8192)
    VH = VHf.rearrange("p (v t) -> p v t", t=512)
    VL = VLf.rearrange("p (v t) -> p v t", t=512)
    xq3 = AR[:, 16384:24576].rearrange("p (v t) -> p v t", t=256)
    sqv = AR[:, 16384:24576]
    sqv3 = xq3
    atstg = AR[:, 16384:20736]
    ATHf = _alias("ATH", 4352, 24576)
    ATLf = _alias("ATL", 4352, 26752)
    ATH = ATHf.rearrange("p (v e) -> p v e", e=136)
    ATL = ATLf.rearrange("p (v e) -> p v e", e=136)
    h2b = AR[:, 28928:29440]
    H2T = AR[:, 29440:29568]
    RT = AR[:, 29568:29616]
    H2W = AR[:, 29616:29744]
    ftile = AR[:, 29744:30256]
    fsq = AR[:, 30256:30768]
    H2Hf = _alias("H2H", 512, 30768)
    H2Lf = _alias("H2L", 512, 31024)
    btstg = [AR[:, 31280:31408], AR[:, 31408:31536]]
    BTH = [_alias("BT0H", 128, 31616), _alias("BT1H", 128, 31744)]
    BTL = [_alias("BT0L", 128, 31680), _alias("BT1L", 128, 31808)]
    wbstg = [A2[:, 8192:12288], AR[:, 20480:24576]]
    WBHf = [_alias("WBH0", 4096, 0, base=a2_addr),
            _alias("WBH1", 4096, 4096, base=a2_addr)]
    WBLf = [_alias("WBL0", 4096, 2048, base=a2_addr),
            _alias("WBL1", 4096, 6144, base=a2_addr)]
    WBH = [t.rearrange("p (k o) -> p k o", o=128) for t in WBHf]
    WBL = [t.rearrange("p (k o) -> p k o", o=128) for t in WBLf]
    fch3 = AR[:, 0:8192].rearrange("p (v t) -> p v t", t=256)
    xch3 = AR[:, 8192:16384].rearrange("p (v t) -> p v t", t=256)

    P = Plan()
    guard = [None] * 8
    last_repl = [None]

    def bankw(i):
        P.wait("tensor", guard[i])

    def coef_rows(ps_bank, w, n, tok_stats, qoff=0):
        P.wait("vector", tok_stats)
        P.wait("vector", last_repl[0])
        P.small_mode = True
        P.v(lambda e: e.tensor_scalar(MUROW[0:1, 0:w], ps_bank[0:1, qoff:qoff + w], 1.0 / n, None, ALU.mult))
        P.v(lambda e: e.tensor_scalar(SROW[0:1, 0:w], ps_bank[32:33, qoff:qoff + w], 1.0 / n, None, ALU.mult))
        P.v(lambda e: e.tensor_tensor(RROW[0:1, 0:w], MUROW[0:1, 0:w], MUROW[0:1, 0:w], ALU.mult))
        P.v(lambda e: e.tensor_tensor(SROW[0:1, 0:w], SROW[0:1, 0:w], RROW[0:1, 0:w], ALU.subtract))
        t1 = P.v(lambda e: e.tensor_scalar(SROW[0:1, 0:w], SROW[0:1, 0:w], EPS, None, ALU.add))
        P.wait("scalar", t1)
        t2 = P.s(lambda e: e.activation(SROW[0:1, 0:w], SROW[0:1, 0:w], AF.Sqrt))
        P.wait("vector", t2)
        P.v(lambda e: e.reciprocal(RROW[0:1, 0:w], SROW[0:1, 0:w]))
        P.v(lambda e: e.tensor_tensor(MUROW[0:1, 0:w], MUROW[0:1, 0:w], RROW[0:1, 0:w], ALU.mult))
        tok = P.v(lambda e: e.tensor_scalar(MUROW[0:1, 0:w], MUROW[0:1, 0:w], -1.0, None, ALU.mult))
        P.small_mode = False
        return tok

    def repl(bank_r, bank_c, w, tok_coef, src_r=None, src_c=None, qoff=0):
        sr = RROW[0:1, 0:w] if src_r is None else src_r
        scr = MUROW[0:1, 0:w] if src_c is None else src_c
        P.wait("tensor", tok_coef)
        bankw(bank_r)
        P.t(lambda e, b=bank_r, sr=sr, w=w, qoff=qoff: e.matmul(
            PX[b][:, qoff:qoff + w], ONES1, sr, start=True, stop=True))
        bankw(bank_c)
        tok = P.t(lambda e, b=bank_c, scr=scr, w=w, qoff=qoff: e.matmul(
            PX[b][:, qoff:qoff + w], ONES1, scr, start=True, stop=True))
        last_repl[0] = tok
        return tok

    tok_sm = P.dma("sync", SM[:, :], d_sm)
    tok_id = P.dma("sync", IDT[:, :], d_id)
    tok_win = P.dma("sync", winTS, d_winT)
    P.v(lambda e: e.memset(ONESK[:, :], 1.0))
    tok_ones = P.v(lambda e: e.memset(ONES1[:, :], 1.0))

    P.waits("tensor", tok_win, tok_ones)
    for dg in range(DG):
        for vg in range(VG):
            tk = P.t(lambda e, vg=vg, dg=dg: e.matmul(
                PX[6][:, dg:dg + 1], winTS[:, vg, dg * 128:(dg + 1) * 128], ONESK[:, :],
                start=(vg == 0), stop=(vg == VG - 1)))
    P.wait("vector", tk)
    P.small_mode = True
    tok_wcs = P.v(lambda e: e.tensor_copy(WCS[:, :], PX[6][:, 0:4]))
    P.small_mode = False
    guard[6] = tok_wcs

    xcbuf = [AR[:, 16384:24576], A2[:, 0:8192]]
    xcfree = [None, None]
    P.wait("vector", tok_sm)
    for ck in range(NECK):
        buf = xcbuf[ck % 2]
        buf3 = buf.rearrange("p (v t) -> p v t", t=EC)
        P.wait("sync", xcfree[ck % 2])
        tok_xc = P.dma("sync", buf3, d_xT[:, :, ck * EC:(ck + 1) * EC])
        P.wait("tensor", tok_xc)
        P.wait("tensor", tok_sm)
        bankw(0)
        for vg in range(VG):
            P.t(lambda e, vg=vg, buf3=buf3: e.matmul(
                PX[0][0:1, 0:EC], ONESK[:, :], buf3[:, vg, :],
                start=(vg == 0), stop=(vg == VG - 1)))
        for dg in range(DG):
            bankw(2 + dg)
            for vg in range(VG):
                tok_mm = P.t(lambda e, vg=vg, dg=dg, buf3=buf3: e.matmul(
                    PX[2 + dg][:, 0:EC], winTS[:, vg, dg * 128:(dg + 1) * 128], buf3[:, vg, :],
                    start=(vg == 0), stop=(vg == VG - 1)))
        P.wait("scalar", tok_mm)
        tok_sq = P.s(lambda e, buf=buf: e.activation(buf, buf, AF.Square))
        P.wait("tensor", tok_sq)
        for vg in range(VG):
            tok_st = P.t(lambda e, vg=vg, buf3=buf3: e.matmul(
                PX[0][32:33, 0:EC], ONESK[:, :], buf3[:, vg, :],
                start=(vg == 0), stop=(vg == VG - 1)))
        xcfree[ck % 2] = tok_st
        tok_coef = coef_rows(PX[0], EC, V, tok_st)
        guard[0] = tok_coef
        tok_repl = repl(6, 7, EC, tok_coef)
        P.wait("vector", tok_repl)
        P.wait("vector", tok_wcs)
        P.v(lambda e: e.tensor_copy(EV[:, 0:EC], PX[6][:, 0:EC]))
        for dg in range(DG):
            hsl = h3[:, dg, ck * EC:(ck + 1) * EC]
            P.v(lambda e, hsl=hsl, dg=dg: e.tensor_tensor(hsl, PX[2 + dg][:, 0:EC], EV[:, 0:EC], ALU.mult))
            P.v(lambda e, hsl=hsl, dg=dg: e.scalar_tensor_tensor(
                hsl, PX[7][:, 0:EC], WCS[:, dg:dg + 1], hsl, ALU.mult, ALU.add))
            tok_ev = P.v(lambda e, hsl=hsl, dg=dg: e.tensor_scalar(
                hsl, hsl, SM[:, SC["bin"] + dg:SC["bin"] + dg + 1], None, ALU.add))
        for i in range(2, 8):
            guard[i] = tok_ev
    tok_h = tok_ev
    tok_winfree = tok_ev

    if stop == "embed":
        P.wait("gpsimd", tok_h)
        tg = P.dma("gpsimd", d_tap[:, 0:8192], hf)
        P.wait("gpsimd", tg)
        return nc, P

    tok_w1_used = tok_winfree
    tok_w2_used = tok_winfree
    tok_stage_free = tok_winfree
    tok_scanbufs_free = None
    tok_mid_free = None

    def emit_tables(i):
        c = lambda n: SM[:, SC[n] + i * 4: SC[n] + (i + 1) * 4]
        RR, RHO, CS1, SN1, T4a, T4b = (SC4[:, j * 4:(j + 1) * 4] for j in range(6))
        P.wait("vector", tok_sm)
        P.small_mode = True
        P.v(lambda e: e.tensor_tensor(RR, c("phz_re"), c("phz_re"), ALU.mult))
        P.v(lambda e: e.tensor_tensor(T4a, c("phz_im"), c("phz_im"), ALU.mult))
        t = P.v(lambda e: e.tensor_tensor(RR, RR, T4a, ALU.add))
        P.wait("scalar", t)
        t = P.s(lambda e: e.activation(T4a, RR, AF.Sqrt))
        P.wait("vector", t)
        P.v(lambda e: e.reciprocal(T4b, T4a))
        P.v(lambda e: e.tensor_tensor(CS1, RR, T4b, ALU.mult))
        P.v(lambda e: e.tensor_tensor(CS1, CS1, T4b, ALU.mult))
        P.v(lambda e: e.tensor_scalar(CS1, CS1, -0.5, 1.5, ALU.mult, ALU.add))
        P.v(lambda e: e.tensor_tensor(T4b, T4b, CS1, ALU.mult))
        t = P.v(lambda e: e.tensor_tensor(RR, RR, T4b, ALU.mult))
        P.wait("scalar", t)
        t2 = P.s(lambda e: e.activation(RHO, RR, AF.Tanh))
        P.wait("vector", t2)
        P.v(lambda e: e.tensor_tensor(CS1, c("phz_re"), T4b, ALU.mult))
        P.v(lambda e: e.tensor_tensor(SN1, c("phz_im"), T4b, ALU.mult))
        cb = tCB.rearrange("p (g r) -> p g r", r=129)
        sb = tSB.rearrange("p (g r) -> p g r", r=129)
        s1 = tS1.rearrange("p (g r) -> p g r", r=64)
        s2 = tS2.rearrange("p (g r) -> p g r", r=64)
        cs3 = CS1.unsqueeze(2)
        sn3 = SN1.unsqueeze(2)
        P.v(lambda e: e.memset(cb[:, :, 0:1], 1.0))
        P.v(lambda e: e.memset(sb[:, :, 0:1], 0.0))
        P.v(lambda e: e.tensor_copy(cb[:, :, 1:2], cs3))
        P.v(lambda e: e.tensor_copy(sb[:, :, 1:2], sn3))
        n = 2
        while n < 129:
            m = min(n, 129 - n)
            cf, sf = T4a.unsqueeze(2), T4b.unsqueeze(2)
            P.v(lambda e, n=n: e.tensor_tensor(s1[:, :, 0:1], cb[:, :, n - 1:n], cs3, ALU.mult))
            P.v(lambda e, n=n: e.tensor_tensor(s2[:, :, 0:1], sb[:, :, n - 1:n], sn3, ALU.mult))
            P.v(lambda e: e.tensor_tensor(cf, s1[:, :, 0:1], s2[:, :, 0:1], ALU.subtract))
            P.v(lambda e, n=n: e.tensor_tensor(s1[:, :, 0:1], sb[:, :, n - 1:n], cs3, ALU.mult))
            P.v(lambda e, n=n: e.tensor_tensor(s2[:, :, 0:1], cb[:, :, n - 1:n], sn3, ALU.mult))
            P.v(lambda e: e.tensor_tensor(sf, s1[:, :, 0:1], s2[:, :, 0:1], ALU.add))
            cfb = cf.broadcast_to([128, 4, m])
            sfb = sf.broadcast_to([128, 4, m])
            P.v(lambda e, m=m, cfb=cfb: e.tensor_tensor(s1[:, :, 0:m], cb[:, :, 0:m], cfb, ALU.mult))
            P.v(lambda e, m=m, sfb=sfb: e.tensor_tensor(s2[:, :, 0:m], sb[:, :, 0:m], sfb, ALU.mult))
            P.v(lambda e, n=n, m=m: e.tensor_tensor(cb[:, :, n:n + m], s1[:, :, 0:m], s2[:, :, 0:m], ALU.subtract))
            P.v(lambda e, m=m, cfb=cfb: e.tensor_tensor(s1[:, :, 0:m], sb[:, :, 0:m], cfb, ALU.mult))
            P.v(lambda e, m=m, sfb=sfb: e.tensor_tensor(s2[:, :, 0:m], cb[:, :, 0:m], sfb, ALU.mult))
            P.v(lambda e, n=n, m=m: e.tensor_tensor(sb[:, :, n:n + m], s1[:, :, 0:m], s2[:, :, 0:m], ALU.add))
            n *= 2
        P.v(lambda e: e.tensor_scalar(tNSB[:, :], tSB[:, :], -1.0, None, ALU.mult))
        rp = tRP.rearrange("p (g r) -> p g r", r=128)
        P.v(lambda e: e.tensor_copy(rp[:, :, 0:1], RHO.unsqueeze(2)))
        n = 1
        while n < 128:
            m = min(n, 128 - n)
            fb = rp[:, :, n - 1:n].broadcast_to([128, 4, m])
            P.v(lambda e, n=n, m=m, fb=fb: e.tensor_tensor(rp[:, :, n:n + m], rp[:, :, 0:m], fb, ALU.mult))
            n *= 2
        cq = tCQ.rearrange("p (g r) -> p g r", r=17)
        sq = tSQ.rearrange("p (g r) -> p g r", r=17)
        P.v(lambda e: e.memset(cq[:, :, 0:1], 1.0))
        P.v(lambda e: e.memset(sq[:, :, 0:1], 0.0))
        P.v(lambda e: e.tensor_copy(cq[:, :, 1:2], cb[:, :, Q:Q + 1]))
        P.v(lambda e: e.tensor_copy(sq[:, :, 1:2], sb[:, :, Q:Q + 1]))
        n = 2
        while n < 17:
            m = min(n, 17 - n)
            cf, sf = T4a.unsqueeze(2), T4b.unsqueeze(2)
            P.v(lambda e, n=n: e.tensor_tensor(s1[:, :, 0:1], cq[:, :, n - 1:n], cq[:, :, 1:2], ALU.mult))
            P.v(lambda e, n=n: e.tensor_tensor(s2[:, :, 0:1], sq[:, :, n - 1:n], sq[:, :, 1:2], ALU.mult))
            P.v(lambda e: e.tensor_tensor(cf, s1[:, :, 0:1], s2[:, :, 0:1], ALU.subtract))
            P.v(lambda e, n=n: e.tensor_tensor(s1[:, :, 0:1], sq[:, :, n - 1:n], cq[:, :, 1:2], ALU.mult))
            P.v(lambda e, n=n: e.tensor_tensor(s2[:, :, 0:1], cq[:, :, n - 1:n], sq[:, :, 1:2], ALU.mult))
            P.v(lambda e: e.tensor_tensor(sf, s1[:, :, 0:1], s2[:, :, 0:1], ALU.add))
            cfb = cf.broadcast_to([128, 4, m])
            sfb = sf.broadcast_to([128, 4, m])
            P.v(lambda e, m=m, cfb=cfb: e.tensor_tensor(s1[:, :, 0:m], cq[:, :, 0:m], cfb, ALU.mult))
            P.v(lambda e, m=m, sfb=sfb: e.tensor_tensor(s2[:, :, 0:m], sq[:, :, 0:m], sfb, ALU.mult))
            P.v(lambda e, n=n, m=m: e.tensor_tensor(cq[:, :, n:n + m], s1[:, :, 0:m], s2[:, :, 0:m], ALU.subtract))
            P.v(lambda e, m=m, cfb=cfb: e.tensor_tensor(s1[:, :, 0:m], sq[:, :, 0:m], cfb, ALU.mult))
            P.v(lambda e, m=m, sfb=sfb: e.tensor_tensor(s2[:, :, 0:m], cq[:, :, 0:m], sfb, ALU.mult))
            tk = P.v(lambda e, n=n, m=m: e.tensor_tensor(sq[:, :, n:n + m], s1[:, :, 0:m], s2[:, :, 0:m], ALU.add))
            n *= 2
        P.small_mode = False
        return tk

    def inner_layer(i):
        nonlocal tok_w1_used, tok_w2_used, tok_scanbufs_free, tok_mid_free, tok_h
        nonlocal tok_stage_free
        P.waits("sync", tok_stage_free)
        tok_w1s = P.dma("sync", w1stg, d_w1T[i])
        P.waits("scalar", tok_w1s, tok_w1_used)
        tok_w1h = P.s(lambda e: e.activation(W1Hf, AR[:, 0:8192], AF.Copy))
        P.waits("gpsimd", tok_w1h, tok_w1_used)
        tok_w1l = P.g(lambda e: e.tensor_tensor(
            W1Lf, AR[:, 0:8192], W1Hf, ALU.subtract))
        P.waits("sync", tok_w1h, tok_w1l)
        tok_w2s = P.dma("sync", w2stg, d_w2T[i])
        P.waits("scalar", tok_w2s, tok_w2_used)
        tok_w2h = P.s(lambda e: e.activation(W2Hf, AR[:, 0:8192], AF.Copy))
        P.waits("gpsimd", tok_w2h, tok_w2_used)
        tok_w2l = P.g(lambda e: e.tensor_tensor(
            W2Lf, AR[:, 0:8192], W2Hf, ALU.subtract))
        tok_tab = emit_tables(i)
        P.waits("scalar", tok_h, tok_scanbufs_free)
        tok_sq = P.s(lambda e: e.activation(sqb, hf, AF.Square))
        sq3 = A2[:, 0:8192].rearrange("p (g t) -> p g t", g=4)
        repl_toks = []
        st_toks = []
        for tc in range(4):
            b = tc
            bankw(b)
            P.wait("tensor", tok_h)
            for g in range(4):
                P.t(lambda e, g=g, tc=tc, b=b: e.matmul(
                    PX[b][0:1, 0:512], ONESK[:, :], h3[:, g, tc * 512:(tc + 1) * 512],
                    start=(g == 0), stop=(g == 3)))
            P.wait("tensor", tok_sq)
            for g in range(4):
                tok_st = P.t(lambda e, g=g, tc=tc, b=b: e.matmul(
                    PX[b][32:33, 0:512], ONESK[:, :], sq3[:, g, tc * 512:(tc + 1) * 512],
                    start=(g == 0), stop=(g == 3)))
            st_toks.append(tok_st)
        for tc in range(4):
            tok_coef = coef_rows(PX[tc], 512, D, st_toks[tc])
            guard[tc] = tok_coef
            tok_r = repl(tc, 4 + tc, 512, tok_coef)
            repl_toks.append(tok_r)
        P.waits("vector", tok_tab, *repl_toks)
        lcre = SM[:, SC["lc_re"] + i * 4: SC["lc_re"] + i * 4 + 4]
        lcim = SM[:, SC["lc_im"] + i * 4: SC["lc_im"] + i * 4 + 4]
        cb = tCB.rearrange("p (g r) -> p g r", r=129)
        sb = tSB.rearrange("p (g r) -> p g r", r=129)
        nsb = tNSB.rearrange("p (g r) -> p g r", r=129)
        rp = tRP.rearrange("p (g r) -> p g r", r=128)
        cq = tCQ.rearrange("p (g r) -> p g r", r=17)
        sq = tSQ.rearrange("p (g r) -> p g r", r=17)
        RHO = SC4[:, 4:8]
        CS1, SN1 = SC4[:, 8:12], SC4[:, 12:16]
        sl = lambda j: S16[:, j * 16:(j + 1) * 16]
        ER, EI, ETR, ETI, VR, VI, VSR, VSI, GR, GI, HR, HI, D0C, TMPA = (sl(j) for j in range(14))
        for g in range(4):
            r0, r1, r2, r3, r4, r5 = R[0], R[1], R[2], R[3], R[4], R[5]
            r0v = r0.rearrange("p (n q) -> p n q", q=Q)
            r1v = r1.rearrange("p (n q) -> p n q", q=Q)
            r2v = r2.rearrange("p (n q) -> p n q", q=Q)
            r3v = r3.rearrange("p (n q) -> p n q", q=Q)
            r4v = r4.rearrange("p (n q) -> p n q", q=Q)
            r5v = r5.rearrange("p (n q) -> p n q", q=Q)
            rho_b = RHO[:, g:g + 1].unsqueeze(1).broadcast_to([128, NCH, Q])
            P.small_mode = True
            P.v(lambda e, rho_b=rho_b, r5v=r5v: e.tensor_copy(r5v, rho_b))
            P.v(lambda e, r5v=r5v: e.memset(r5v[:, :, 0:1], 0.0))
            P.small_mode = False
            for tc in range(4):
                hsl = h3[:, g, tc * 512:(tc + 1) * 512]
                P.v(lambda e, hsl=hsl, tc=tc, r0=r0: e.tensor_tensor(
                    r0[:, tc * 512:(tc + 1) * 512], hsl, PX[tc][:, :], ALU.mult))
                tk = P.v(lambda e, tc=tc, r0=r0: e.tensor_tensor(
                    r0[:, tc * 512:(tc + 1) * 512], r0[:, tc * 512:(tc + 1) * 512],
                    PX[4 + tc][:, :], ALU.add))
            P.wait("scalar", tk)
            tk = P.s(lambda e, g=g, r0=r0: e.activation(
                r0, r0, AF.Silu,
                bias=SM[:, SC["ln1_b"] + i * 4 + g: SC["ln1_b"] + i * 4 + g + 1],
                scale=SM[:, SC["ln1_w"] + i * 4 + g: SC["ln1_w"] + i * 4 + g + 1]))
            P.wait("vector", tk)
            cbb = cb[:, g, 0:Q].unsqueeze(1).broadcast_to([128, NCH, Q])
            nsbb = nsb[:, g, 0:Q].unsqueeze(1).broadcast_to([128, NCH, Q])
            sbb = sb[:, g, 0:Q].unsqueeze(1).broadcast_to([128, NCH, Q])
            P.v(lambda e, r1v=r1v, r0v=r0v, cbb=cbb: e.tensor_tensor(r1v, r0v, cbb, ALU.mult))
            P.v(lambda e, r2v=r2v, r0v=r0v, nsbb=nsbb: e.tensor_tensor(r2v, r0v, nsbb, ALU.mult))
            P.small_mode = True
            P.v(lambda e, r3=r3, r5=r5, r1=r1: e.tensor_tensor_scan(r3, r5, r1, 0.0, ALU.mult, ALU.add))
            P.v(lambda e, r4=r4, r5=r5, r2=r2: e.tensor_tensor_scan(r4, r5, r2, 0.0, ALU.mult, ALU.add))
            urE = r3v[:, :, Q - 1]
            uiE = r4v[:, :, Q - 1]
            cbq = cb[:, g, Q - 1:Q]
            sbq = sb[:, g, Q - 1:Q]
            P.v(lambda e, urE=urE, cbq=cbq: e.tensor_scalar(ER, urE, cbq, None, ALU.mult))
            P.v(lambda e, uiE=uiE, sbq=sbq: e.tensor_scalar(TMPA, uiE, sbq, None, ALU.mult))
            P.v(lambda e: e.tensor_tensor(ER, ER, TMPA, ALU.subtract))
            P.v(lambda e, urE=urE, sbq=sbq: e.tensor_scalar(EI, urE, sbq, None, ALU.mult))
            P.v(lambda e, uiE=uiE, cbq=cbq: e.tensor_scalar(TMPA, uiE, cbq, None, ALU.mult))
            P.v(lambda e: e.tensor_tensor(EI, EI, TMPA, ALU.add))
            cq1 = cq[:, g, 1:17]
            sq1 = sq[:, g, 1:17]
            P.v(lambda e, cq1=cq1: e.tensor_tensor(ETR, ER, cq1, ALU.mult))
            P.v(lambda e, sq1=sq1: e.tensor_tensor(TMPA, EI, sq1, ALU.mult))
            P.v(lambda e: e.tensor_tensor(ETR, ETR, TMPA, ALU.add))
            P.v(lambda e, cq1=cq1: e.tensor_tensor(ETI, EI, cq1, ALU.mult))
            P.v(lambda e, sq1=sq1: e.tensor_tensor(TMPA, ER, sq1, ALU.mult))
            P.v(lambda e: e.tensor_tensor(ETI, ETI, TMPA, ALU.subtract))
            rq = rp[:, g, Q - 1:Q]
            P.v(lambda e, g=g, rq=rq, lcre=lcre: e.scalar_tensor_tensor(
                ETR[:, 0:1], lcre[:, g:g + 1], rq, ETR[:, 0:1], ALU.mult, ALU.add))
            P.v(lambda e, g=g, rq=rq, lcim=lcim: e.scalar_tensor_tensor(
                ETI[:, 0:1], lcim[:, g:g + 1], rq, ETI[:, 0:1], ALU.mult, ALU.add))
            P.v(lambda e, rq=rq: e.tensor_copy(D0C, rq.broadcast_to([128, 16])))
            P.v(lambda e: e.memset(D0C[:, 0:1], 0.0))
            P.v(lambda e: e.tensor_tensor_scan(VR, D0C, ETR, 0.0, ALU.mult, ALU.add))
            P.v(lambda e: e.tensor_tensor_scan(VI, D0C, ETI, 0.0, ALU.mult, ALU.add))
            P.v(lambda e: e.tensor_copy(VSR[:, 1:16], VR[:, 0:15]))
            P.v(lambda e, g=g, lcre=lcre: e.tensor_copy(VSR[:, 0:1], lcre[:, g:g + 1]))
            P.v(lambda e: e.tensor_copy(VSI[:, 1:16], VI[:, 0:15]))
            P.v(lambda e, g=g, lcim=lcim: e.tensor_copy(VSI[:, 0:1], lcim[:, g:g + 1]))
            cq0 = cq[:, g, 0:16]
            sq0 = sq[:, g, 0:16]
            P.v(lambda e, cq0=cq0: e.tensor_tensor(GR, VSR, cq0, ALU.mult))
            P.v(lambda e, sq0=sq0: e.tensor_tensor(TMPA, VSI, sq0, ALU.mult))
            P.v(lambda e: e.tensor_tensor(GR, GR, TMPA, ALU.subtract))
            P.v(lambda e, sq0=sq0: e.tensor_tensor(GI, VSR, sq0, ALU.mult))
            P.v(lambda e, cq0=cq0: e.tensor_tensor(TMPA, VSI, cq0, ALU.mult))
            P.v(lambda e: e.tensor_tensor(GI, GI, TMPA, ALU.add))
            cs1g = CS1[:, g:g + 1]
            sn1g = SN1[:, g:g + 1]
            P.v(lambda e, cs1g=cs1g: e.tensor_scalar(HR, GR, cs1g, None, ALU.mult))
            P.v(lambda e, sn1g=sn1g: e.tensor_scalar(TMPA, GI, sn1g, None, ALU.mult))
            P.v(lambda e: e.tensor_tensor(HR, HR, TMPA, ALU.subtract))
            P.v(lambda e, sn1g=sn1g: e.tensor_scalar(HI, GR, sn1g, None, ALU.mult))
            P.v(lambda e, cs1g=cs1g: e.tensor_scalar(TMPA, GI, cs1g, None, ALU.mult))
            P.v(lambda e: e.tensor_tensor(HI, HI, TMPA, ALU.add))
            P.small_mode = False
            rpb = rp[:, g, :].unsqueeze(1).broadcast_to([128, NCH, Q])
            hrb = HR.unsqueeze(2).broadcast_to([128, NCH, Q])
            hib = HI.unsqueeze(2).broadcast_to([128, NCH, Q])
            P.v(lambda e, r0v=r0v, rpb=rpb, hrb=hrb: e.tensor_tensor(r0v, rpb, hrb, ALU.mult))
            P.v(lambda e, r1=r1, r3=r3, r0=r0: e.tensor_tensor(r1, r3, r0, ALU.add))
            P.v(lambda e, r0v=r0v, rpb=rpb, hib=hib: e.tensor_tensor(r0v, rpb, hib, ALU.mult))
            P.v(lambda e, r2=r2, r4=r4, r0=r0: e.tensor_tensor(r2, r4, r0, ALU.add))
            P.v(lambda e, r0v=r0v, r1v=r1v, cbb=cbb: e.tensor_tensor(r0v, r1v, cbb, ALU.mult))
            P.v(lambda e, r1v=r1v, r2v=r2v, sbb=sbb: e.tensor_tensor(r1v, r2v, sbb, ALU.mult))
            P.v(lambda e, r0=r0, r1=r1: e.tensor_tensor(r0, r0, r1, ALU.subtract))
            tok_hg = P.v(lambda e, g=g, r0=r0: e.scalar_tensor_tensor(
                h3[:, g, :], r0, SM[:, SC["sc_lin"] + i * 4 + g: SC["sc_lin"] + i * 4 + g + 1],
                h3[:, g, :], ALU.mult, ALU.add))
            if stop == "dbgl0" and i == 0 and g == 0:
                P.wait("gpsimd", tok_hg)
                for off, src in ((0, tCB), (516, tSB), (1032, tRP), (1544, SC4),
                                 (1576, S16), (2048, R[0]), (4096, R[1]),
                                 (6144, R[2]), (8192, R[3]), (10240, R[4]),
                                 (12288, R[5]), (14336, h3[:, 0, :])):
                    n = src.shape[-1] if src.ndim == 2 else src.free_size()
                    tg = P.dma("gpsimd", d_tap[:, off:off + n], src)
                P.wait("gpsimd", tg)
                return None
        for b in range(8):
            guard[b] = tok_hg
        tok_h = tok_hg
        P.wait("scalar", tok_h)
        tok_sq = P.s(lambda e: e.activation(sqb, hf, AF.Square))
        repl_toks = []
        st_toks = []
        for tc in range(4):
            b = tc
            bankw(b)
            P.wait("tensor", tok_h)
            for g in range(4):
                P.t(lambda e, g=g, tc=tc, b=b: e.matmul(
                    PX[b][0:1, 0:512], ONESK[:, :], h3[:, g, tc * 512:(tc + 1) * 512],
                    start=(g == 0), stop=(g == 3)))
            P.wait("tensor", tok_sq)
            for g in range(4):
                tok_st = P.t(lambda e, g=g, tc=tc, b=b: e.matmul(
                    PX[b][32:33, 0:512], ONESK[:, :], sq3[:, g, tc * 512:(tc + 1) * 512],
                    start=(g == 0), stop=(g == 3)))
            st_toks.append(tok_st)
        for tc in range(4):
            tok_coef = coef_rows(PX[tc], 512, D, st_toks[tc])
            guard[tc] = tok_coef
            repl_toks.append(repl(tc, 4 + tc, 512, tok_coef))
        P.waits("vector", *repl_toks)
        P.wait("vector", tok_st)
        tok_zh = tok_zl = None
        for tc in range(4):
            hs = h3[:, :, tc * 512:(tc + 1) * 512]
            rb = PX[tc][:, :].unsqueeze(1).broadcast_to([128, 4, 512])
            cbk = PX[4 + tc][:, :].unsqueeze(1).broadcast_to([128, 4, 512])
            P.v(lambda e, hs=hs, rb=rb: e.tensor_tensor(ztmp3, hs, rb, ALU.mult))
            P.v(lambda e, cbk=cbk: e.tensor_tensor(ztmp3, ztmp3, cbk, ALU.add))
            for g in range(4):
                tok_zt = P.v(lambda e, g=g: e.tensor_scalar(
                    ztmp3[:, g, :], ztmp3[:, g, :],
                    SM[:, SC["ln2_w"] + i * 4 + g: SC["ln2_w"] + i * 4 + g + 1],
                    SM[:, SC["ln2_b"] + i * 4 + g: SC["ln2_b"] + i * 4 + g + 1],
                    ALU.mult, ALU.add))
            P.waits("scalar", tok_zt, tok_st)
            tok_zh = P.s(lambda e, tc=tc: e.activation(
                ZH[:, :, tc * 512:(tc + 1) * 512], ztmp3, AF.Copy))
            P.wait("vector", tok_zh)
            tok_zl = P.v(lambda e, tc=tc: e.tensor_tensor(
                ZL[:, :, tc * 512:(tc + 1) * 512], ztmp3,
                ZH[:, :, tc * 512:(tc + 1) * 512], ALU.subtract))
        for b in range(8):
            guard[b] = tok_zl
        P.waits("tensor", tok_w1h, tok_w1l, tok_w2h, tok_w2l, tok_zh, tok_zl)
        tok_w1d = None
        tok_w2d = None
        mtmp_free = [None, None]
        for tc in range(4):
            tok_mh = tok_ml = None
            for fg in range(FG):
                b = fg % 2
                bankw(b)
                P.wait("tensor", tok_mid_free)
                for kg in range(4):
                    for si, (ws, zs2) in enumerate(((W1H, ZH), (W1H, ZL), (W1L, ZH))):
                        tok_w1d = P.t(lambda e, fg=fg, kg=kg, tc=tc, b=b, ws=ws, zs2=zs2,
                                      st=(kg == 0 and si == 0), sp=(kg == 3 and si == 2):
                                      e.matmul(PX[b][:, :], ws[:, kg, fg * 128:(fg + 1) * 128],
                                               zs2[:, kg, tc * 512:(tc + 1) * 512],
                                               start=st, stop=sp))
                mt = mtmp[fg % 2]
                P.waits("scalar", tok_w1d, tok_mid_free, mtmp_free[fg % 2])
                tok_silu = P.s(lambda e, fg=fg, b=b, mt=mt: e.activation(
                    mt, PX[b][:, :], AF.Silu,
                    bias=SM[:, SC["b1"] + i * 16 + fg: SC["b1"] + i * 16 + fg + 1]))
                guard[b] = tok_silu
                tok_mh = P.s(lambda e, fg=fg, mt=mt: e.activation(
                    MIDH[:, fg, :], mt, AF.Copy))
                P.wait("vector", tok_mh)
                tok_ml = P.v(lambda e, fg=fg, mt=mt: e.tensor_tensor(
                    MIDL[:, fg, :], mt, MIDH[:, fg, :], ALU.subtract))
                mtmp_free[fg % 2] = tok_ml
            P.waits("tensor", tok_mh, tok_ml)
            for dg in range(4):
                b = 2 + dg % 2
                bankw(b)
                for kg in range(FG):
                    for si, (ws, ms) in enumerate(((W2H, MIDH), (W2H, MIDL), (W2L, MIDH))):
                        tok_w2d = P.t(lambda e, dg=dg, kg=kg, b=b, ws=ws, ms=ms,
                                      st=(kg == 0 and si == 0), sp=(kg == FG - 1 and si == 2):
                                      e.matmul(PX[b][:, :], ws[:, kg, dg * 128:(dg + 1) * 128],
                                               ms[:, kg, :], start=st, stop=sp))
                P.wait("vector", tok_w2d)
                tok_h = P.v(lambda e, dg=dg, tc=tc, b=b: e.scalar_tensor_tensor(
                    h3[:, dg, tc * 512:(tc + 1) * 512], PX[b][:, :],
                    SM[:, SC["b2"] + i * 4 + dg: SC["b2"] + i * 4 + dg + 1],
                    h3[:, dg, tc * 512:(tc + 1) * 512], ALU.add, ALU.add))
                guard[b] = tok_h
            tok_mid_free = tok_w2d
        tok_w1_used = tok_w1d
        tok_w2_used = tok_w2d
        tok_stage_free = tok_w2d
        tok_scanbufs_free = tok_w1d

    for i in range(DEPTH):
        rr_ = inner_layer(i)
        if stop == "dbgl0" and i == 0:
            return nc, P
        if stop == f"l{i}":
            P.wait("gpsimd", tok_h)
            tg = P.dma("gpsimd", d_tap[:, 0:8192], hf)
            P.wait("gpsimd", tg)
            return nc, P

    tok_woh = tok_wol = None
    for hh in range(2):
        P.waits("sync", tok_w2_used, tok_woh, tok_wol)
        tok_wst = P.dma("sync", wostg, d_woutT[:, 2 * hh:2 * hh + 2, :])
        P.waits("scalar", tok_wst, tok_w2_used)
        tok_woh = P.s(lambda e, hh=hh: e.activation(
            WOH[:, 2 * hh:2 * hh + 2, :], wostg, AF.Copy))
        P.waits("gpsimd", tok_woh, tok_w1_used)
        tok_wol = P.g(lambda e, hh=hh: e.tensor_tensor(
            WOL[:, 2 * hh:2 * hh + 2, :], wostg, WOH[:, 2 * hh:2 * hh + 2, :],
            ALU.subtract))
    P.waits("scalar", tok_h, tok_scanbufs_free)
    tok_sq = P.s(lambda e: e.activation(sqb, hf, AF.Square))
    sq3 = A2[:, 0:8192].rearrange("p (g t) -> p g t", g=4)
    repl_toks = []
    st_toks = []
    for tc in range(4):
        b = tc
        bankw(b)
        P.wait("tensor", tok_h)
        for g in range(4):
            P.t(lambda e, g=g, tc=tc, b=b: e.matmul(
                PX[b][0:1, 0:512], ONESK[:, :], h3[:, g, tc * 512:(tc + 1) * 512],
                start=(g == 0), stop=(g == 3)))
        P.wait("tensor", tok_sq)
        for g in range(4):
            tok_st = P.t(lambda e, g=g, tc=tc, b=b: e.matmul(
                PX[b][32:33, 0:512], ONESK[:, :], sq3[:, g, tc * 512:(tc + 1) * 512],
                start=(g == 0), stop=(g == 3)))
        st_toks.append(tok_st)
    for tc in range(4):
        tok_coef = coef_rows(PX[tc], 512, D, st_toks[tc])
        guard[tc] = tok_coef
        repl_toks.append(repl(tc, 4 + tc, 512, tok_coef))
    P.waits("vector", *repl_toks)
    P.wait("vector", tok_st)
    for tc in range(4):
        hns = hn3[:, :, tc * 512:(tc + 1) * 512]
        hs = h3[:, :, tc * 512:(tc + 1) * 512]
        rb = PX[tc][:, :].unsqueeze(1).broadcast_to([128, 4, 512])
        cbk = PX[4 + tc][:, :].unsqueeze(1).broadcast_to([128, 4, 512])
        P.v(lambda e, hns=hns, hs=hs, rb=rb: e.tensor_tensor(hns, hs, rb, ALU.mult))
        P.v(lambda e, hns=hns, cbk=cbk: e.tensor_tensor(hns, hns, cbk, ALU.add))
        for g in range(4):
            tok_hn = P.v(lambda e, g=g, tc=tc: e.tensor_scalar(
                hn3[:, g, tc * 512:(tc + 1) * 512], hn3[:, g, tc * 512:(tc + 1) * 512],
                SM[:, SC["lnl_w"] + g: SC["lnl_w"] + g + 1],
                SM[:, SC["lnl_b"] + g: SC["lnl_b"] + g + 1], ALU.mult, ALU.add))
    for b in range(8):
        guard[b] = tok_hn
    sel0 = SM[:, SC["sel"]:SC["sel"] + 1]
    sel1 = SM[:, SC["sel"] + 1:SC["sel"] + 2]
    P.v(lambda e: e.tensor_scalar(hno3[:, :, :], hn3[:, :, 0:TOK], sel0, None, ALU.mult))
    tok_hno = P.v(lambda e: e.scalar_tensor_tensor(
        hno3[:, :, :], hn3[:, :, TOK:L], sel1, hno3[:, :, :], ALU.mult, ALU.add))

    P.waits("scalar", tok_hno, tok_wol)
    tok_hnh = P.s(lambda e: e.activation(HNHf, A2[:, 8192:12288], AF.Copy))
    P.wait("vector", tok_hnh)
    tok_hnl = P.v(lambda e: e.tensor_tensor(
        HNLf, A2[:, 8192:12288], HNHf, ALU.subtract))

    P.waits("tensor", tok_woh, tok_wol, tok_hnh, tok_hnl)
    tok_adma = [None, None]
    tok_asq = [None, None]
    tok_astat_sl = [None, None]
    tok_astat = None
    for og in range(VG):
        sl = og % 2
        for tc2 in range(2):
            b = sl * 2 + tc2
            bankw(b)
            for kg in range(4):
                for si, (ws, hs2) in enumerate(((WOH, HNH), (WOH, HNL), (WOL, HNH))):
                    tok_u = P.t(lambda e, og=og, kg=kg, tc2=tc2, b=b, ws=ws, hs2=hs2,
                                st=(kg == 0 and si == 0), sp=(kg == 3 and si == 2):
                                e.matmul(PX[b][:, :], ws[:, kg, og * 128:(og + 1) * 128],
                                         hs2[:, kg, tc2 * 512:(tc2 + 1) * 512],
                                         start=st, stop=sp))
            P.waits("vector", tok_u, tok_adma[sl], tok_asq[sl], tok_astat_sl[sl])
            tok_ev = P.v(lambda e, og=og, tc2=tc2, b=b, sl=sl: e.tensor_scalar(
                atile[sl][:, tc2 * 512:(tc2 + 1) * 512], PX[b][:, :],
                SM[:, SC["bout"] + og: SC["bout"] + og + 1], None, ALU.add))
            guard[b] = tok_ev
        P.wait("gpsimd", tok_ev)
        tok_adma[sl] = P.dma("gpsimd", d_ascr[:, og, :], atile[sl])
        P.waits("scalar", tok_ev, tok_astat_sl[sl])
        tok_asq[sl] = P.s(lambda e, sl=sl: e.activation(asq[sl], atile[sl], AF.Square))
        for tc2 in range(2):
            b = 4 + tc2
            if og == 0:
                bankw(b)
            P.wait("tensor", tok_ev)
            P.t(lambda e, og=og, tc2=tc2, b=b, sl=sl: e.matmul(
                PX[b][0:1, 0:512], ONESK[:, :], atile[sl][:, tc2 * 512:(tc2 + 1) * 512],
                start=(og == 0), stop=(og == VG - 1)))
            P.wait("tensor", tok_asq[sl])
            tok_astat = P.t(lambda e, og=og, tc2=tc2, b=b, sl=sl: e.matmul(
                PX[b][32:33, 0:512], ONESK[:, :], asq[sl][:, tc2 * 512:(tc2 + 1) * 512],
                start=(og == 0), stop=(og == VG - 1)))
        tok_astat_sl[sl] = tok_astat
    for tc2 in range(2):
        tok_coef = coef_rows(PX[4 + tc2], 512, V, tok_astat)
        guard[4 + tc2] = tok_coef
        P.v(lambda e, tc2=tc2: e.tensor_copy(RROWB[0:1, tc2 * 512:(tc2 + 1) * 512], RROW[0:1, :]))
        tok_arows = P.v(lambda e, tc2=tc2: e.tensor_copy(CROWB[0:1, tc2 * 512:(tc2 + 1) * 512], MUROW[0:1, :]))

    P.waits("sync", tok_u)
    tok_ats = P.dma("sync", atstg, d_aT)
    P.waits("scalar", tok_ats, tok_hn, st_toks[-1])
    tok_ath = P.s(lambda e: e.activation(ATHf, atstg, AF.Copy))
    P.waits("vector", tok_ath, tok_hn)
    tok_atl = P.v(lambda e: e.tensor_tensor(ATLf, atstg, ATHf, ALU.subtract))

    m8 = RT[:, 0:8]
    lg = RT[:, 8:16]
    qe = RT[:, 16:24]
    msk = RT[:, 24:32]
    den = RT[:, 32:33]
    nm1 = RT[:, 33:34]
    r2 = RT[:, 34:35]

    tok_x1dma = None
    tok_sqstat = None
    tok_vmm = tok_u
    tok_aq_free = None
    tok_vc = tok_vs = None
    tok_wb_mm = [None, None]
    tok_wbh = [None, None]
    tok_wbl = [None, None]
    tok_bth = [None, None]
    tok_btl = [None, None]
    tok_fdma = None
    tok_fsq = None
    tok_fstat = None
    tok_h2h_prev = None
    frows_tok = [None, None]
    tok_base = None
    tok_wbstg_free = [tok_hnh, tok_hnl]
    for th in range(2):
        vsplit = []
        for sub in range(2):
            qc = th * 2 + sub
            P.waits("sync", tok_adma[0], tok_adma[1], tok_astat, tok_x1dma,
                    tok_aq_free, tok_vc, tok_vs)
            tok_aq = P.dma("sync", aq3, d_ascr[:, :, qc * 256:(qc + 1) * 256])
            P.waits("sync", tok_ath, tok_atl, tok_sqstat, tok_wbh[1], tok_wbl[1])
            tok_xq = P.dma("sync", xq3, d_xown[:, :, qc * 256:(qc + 1) * 256])
            tok_r = repl(6, 7, 256, tok_arows,
                         src_r=RROWB[0:1, qc * 256:(qc + 1) * 256],
                         src_c=CROWB[0:1, qc * 256:(qc + 1) * 256])
            P.waits("vector", tok_aq, tok_xq, tok_r)
            rb = PX[6][:, 0:256].unsqueeze(1).broadcast_to([128, VG, 256])
            cbk = PX[7][:, 0:256].unsqueeze(1).broadcast_to([128, VG, 256])
            P.v(lambda e, rb=rb: e.tensor_tensor(aq3, aq3, rb, ALU.mult))
            P.v(lambda e, cbk=cbk: e.tensor_tensor(aq3, aq3, cbk, ALU.add))
            tok_x1 = P.v(lambda e: e.tensor_tensor(aq3, aq3, xq3, ALU.add))
            guard[6] = tok_x1
            guard[7] = tok_x1
            P.wait("gpsimd", tok_x1)
            tok_x1dma = P.dma("gpsimd", d_x1scr[:, :, qc * 256:(qc + 1) * 256], aq3)
            P.waits("scalar", tok_x1, tok_sqstat)
            tok_sq = P.s(lambda e: e.activation(sqv, A2[:, 0:8192], AF.Square))
            bankw(4)
            P.wait("tensor", tok_x1)
            for vg in range(VG):
                P.t(lambda e, vg=vg: e.matmul(
                    PX[4][0:1, 0:256], ONESK[:, :], aq3[:, vg, :],
                    start=(vg == 0), stop=(vg == VG - 1)))
            P.wait("tensor", tok_sq)
            for vg in range(VG):
                tok_st2 = P.t(lambda e, vg=vg: e.matmul(
                    PX[4][32:33, 0:256], ONESK[:, :], sqv3[:, vg, :],
                    start=(vg == 0), stop=(vg == VG - 1)))
            tok_sqstat = tok_st2
            tok_coef = coef_rows(PX[4], 256, V, tok_st2)
            guard[4] = tok_coef
            tok_r2 = repl(5, 0, 256, tok_coef)
            P.wait("vector", tok_r2)
            rb2 = PX[5][:, 0:256].unsqueeze(1).broadcast_to([128, VG, 256])
            cb2 = PX[0][:, 0:256].unsqueeze(1).broadcast_to([128, VG, 256])
            P.v(lambda e, rb2=rb2: e.tensor_tensor(aq3, aq3, rb2, ALU.mult))
            tok_v = P.v(lambda e, cb2=cb2: e.tensor_tensor(aq3, aq3, cb2, ALU.add))
            guard[5] = tok_v
            guard[0] = tok_v
            P.waits("scalar", tok_v, tok_vmm)
            tok_vc = P.s(lambda e, sub=sub: e.activation(
                VH[:, :, sub * 256:(sub + 1) * 256], aq3, AF.Copy))
            P.waits("vector", tok_vc, tok_vmm)
            tok_vs = P.v(lambda e, sub=sub: e.tensor_tensor(
                VL[:, :, sub * 256:(sub + 1) * 256], aq3,
                VH[:, :, sub * 256:(sub + 1) * 256], ALU.subtract))
            vsplit.extend([tok_vc, tok_vs])
        def stage_wb(og):
            sl = og % 2
            P.waits("sync", tok_wbh[sl], tok_wbl[sl],
                    tok_bth[sl], tok_btl[sl], tok_sqstat, *tok_wbstg_free)
            tok_sg = P.dma("sync", wbstg[sl], d_wbT[og])
            tok_btg = P.dma("sync", btstg[sl], d_bT[:, og * 128:(og + 1) * 128])
            P.waits("scalar", tok_sg, tok_wb_mm[sl], *vsplit)
            tok_wbh[sl] = P.s(lambda e, sl=sl: e.activation(WBHf[sl], wbstg[sl], AF.Copy))
            P.waits("vector", tok_wbh[sl], tok_wb_mm[sl], *vsplit)
            tok_wbl[sl] = P.v(lambda e, sl=sl: e.tensor_tensor(
                WBLf[sl], wbstg[sl], WBHf[sl], ALU.subtract))
            P.waits("scalar", tok_btg, tok_wb_mm[sl])
            tok_bth[sl] = P.s(lambda e, sl=sl: e.activation(BTH[sl], btstg[sl], AF.Copy))
            P.waits("vector", tok_bth[sl], tok_wb_mm[sl])
            tok_btl[sl] = P.v(lambda e, sl=sl: e.tensor_tensor(
                BTL[sl], btstg[sl], BTH[sl], ALU.subtract))

        stage_wb(0)
        tok_wbstg_free = []
        stage_wb(1)
        tok_h2 = None
        for tt in range(4):
            bankw(3)
            P.waits("tensor", tok_ath, tok_atl, *vsplit)
            for vg in range(VG):
                for si, (vv, aa) in enumerate(((VH, ATH), (VL, ATH), (VH, ATL))):
                    tok_ha = P.t(lambda e, vg=vg, tt=tt, vv=vv, aa=aa,
                                 st=(vg == 0 and si == 0),
                                 sp=(vg == VG - 1 and si == 2):
                                 e.matmul(PX[3][:, 0:136],
                                          vv[:, vg, tt * 128:(tt + 1) * 128],
                                          aa[:, vg, :], start=st, stop=sp))
            P.wait("vector", tok_ha)
            P.small_mode = True
            P.v(lambda e: e.tensor_copy(lg, PX[3][:, 128:136]))
            tok_lg = P.v(lambda e: e.max(m8, lg))
            tok_nm1 = P.v(lambda e: e.tensor_scalar(nm1, m8[:, 0:1], -1.0, None, ALU.mult))
            P.wait("scalar", tok_nm1)
            tok_q = P.s(lambda e: e.activation(qe, lg, AF.Exp, bias=nm1))
            P.wait("vector", tok_q)
            P.v(lambda e: e.tensor_single_scalar(msk, lg, m8[:, 1:2], ALU.is_ge))
            P.v(lambda e: e.tensor_tensor(qe, qe, msk, ALU.mult))
            P.v(lambda e: e.reduce_sum(den, qe, axis=AXX))
            P.v(lambda e: e.reciprocal(r2, den))
            P.v(lambda e: e.tensor_scalar(r2, r2, 2.0, None, ALU.mult))
            P.v(lambda e: e.tensor_scalar(qe, qe, r2, None, ALU.mult))
            cmb = qe.rearrange("p (a b) -> p a b", b=1).broadcast_to([128, 8, 16])
            tok_cp = P.v(lambda e: e.tensor_copy(H2W, PX[3][:, 0:128]))
            guard[3] = tok_cp
            tok_h2t = P.v(lambda e, cmb=cmb: e.tensor_tensor(
                H2T.rearrange("p (a b) -> p a b", b=16),
                H2W.rearrange("p (a b) -> p a b", b=16), cmb, ALU.mult))
            P.small_mode = False
            bankw(6)
            P.waits("tensor", tok_h2t, tok_id)
            tok_tr = P.t(lambda e: e.transpose(PX[6][:, 0:128], H2T, IDT[:, :]))
            P.waits("vector", tok_tr, tok_h2h_prev)
            tok_h2 = P.v(lambda e, tt=tt: e.tensor_copy(
                h2b[:, tt * 128:(tt + 1) * 128], PX[6][:, 0:128]))
            guard[6] = tok_h2
        P.waits("scalar", tok_h2, tok_wb_mm[0], tok_wb_mm[1])
        tok_h2h = P.s(lambda e: e.activation(H2Hf, h2b, AF.Copy))
        tok_h2h_prev = tok_h2h
        P.waits("vector", tok_h2h, tok_wb_mm[0], tok_wb_mm[1])
        tok_h2l = P.v(lambda e: e.tensor_tensor(H2Lf, h2b, H2Hf, ALU.subtract))
        for og in range(VG):
            sl = og % 2
            b = sl
            bankw(b)
            P.waits("tensor", tok_wbh[sl], tok_wbl[sl], tok_bth[sl], tok_btl[sl],
                    tok_h2h, tok_h2l)
            for kg in range(VG):
                for si, (wv, vv) in enumerate(((WBH[sl], VH), (WBH[sl], VL), (WBL[sl], VH))):
                    P.t(lambda e, kg=kg, wv=wv, vv=vv, b=b,
                        st=(kg == 0 and si == 0):
                        e.matmul(PX[b][:, :], wv[:, kg, :], vv[:, kg, :],
                                 start=st, stop=False))
            for si, (bb, hh2) in enumerate(((BTH[sl], H2Hf), (BTH[sl], H2Lf), (BTL[sl], H2Hf))):
                tok_base = P.t(lambda e, bb=bb, hh2=hh2, b=b, sp=(si == 2):
                               e.matmul(PX[b][:, :], bb, hh2, start=False, stop=sp))
            tok_wb_mm[sl] = tok_base
            P.waits("vector", tok_base, tok_fdma, tok_fsq, tok_fstat)
            tok_ev = P.v(lambda e, og=og, b=b: e.tensor_scalar(
                ftile, PX[b][:, :], SM[:, SC["b_base"] + og: SC["b_base"] + og + 1],
                None, ALU.add))
            guard[b] = tok_ev
            P.wait("gpsimd", tok_ev)
            tok_fdma = P.dma("gpsimd", d_fscr[:, og, th * 512:(th + 1) * 512], ftile)
            P.waits("scalar", tok_ev, tok_fstat)
            tok_fsq = P.s(lambda e: e.activation(fsq, ftile, AF.Square))
            if og == 0:
                bankw(4)
            P.wait("tensor", tok_ev)
            P.t(lambda e, og=og: e.matmul(PX[4][0:1, 0:512], ONESK[:, :], ftile,
                                          start=(og == 0), stop=(og == VG - 1)))
            P.wait("tensor", tok_fsq)
            tok_fstat = P.t(lambda e, og=og: e.matmul(
                PX[4][32:33, 0:512], ONESK[:, :], fsq,
                start=(og == 0), stop=(og == VG - 1)))
            if og + 2 < VG:
                stage_wb(og + 2)
        tok_coef = coef_rows(PX[4], 512, V, tok_fstat)
        guard[4] = tok_coef
        P.v(lambda e, th=th: e.tensor_copy(RROWB[0:1, th * 512:(th + 1) * 512], RROW[0:1, :]))
        frows_tok[th] = P.v(lambda e, th=th: e.tensor_copy(
            CROWB[0:1, th * 512:(th + 1) * 512], MUROW[0:1, :]))
        tok_vmm = tok_base
        tok_aq_free = tok_base

    fchp = [AR[:, 0:4096].rearrange("p (v t) -> p v t", t=128),
            AR[:, 4096:8192].rearrange("p (v t) -> p v t", t=128)]
    xchp = [AR[:, 8192:12288].rearrange("p (v t) -> p v t", t=128),
            AR[:, 12288:16384].rearrange("p (v t) -> p v t", t=128)]
    p6bk = [(6, 7), (2, 3)]
    tok_outd = None
    tok_os = [tok_base, tok_base]
    tok_od = [None, None]
    for ck in range(8):
        th2 = ck // 4
        sl = ck % 2
        fc3, xc3 = fchp[sl], xchp[sl]
        br, bc = p6bk[sl]
        P.waits("sync", tok_os[sl], tok_od[sl], tok_fdma, tok_x1dma)
        tok_fc = P.dma("sync", fc3, d_fscr[:, :, ck * 128:(ck + 1) * 128])
        tok_xc = P.dma("sync", xc3, d_x1scr[:, :, ck * 128:(ck + 1) * 128])
        tok_r = repl(br, bc, 128, frows_tok[th2],
                     src_r=RROWB[0:1, ck * 128:(ck + 1) * 128],
                     src_c=CROWB[0:1, ck * 128:(ck + 1) * 128])
        P.waits("vector", tok_fc, tok_xc, tok_r)
        rb = PX[br][:, 0:128].unsqueeze(1).broadcast_to([128, VG, 128])
        cbk = PX[bc][:, 0:128].unsqueeze(1).broadcast_to([128, VG, 128])
        P.v(lambda e, fc3=fc3, rb=rb: e.tensor_tensor(fc3, fc3, rb, ALU.mult))
        P.v(lambda e, fc3=fc3, cbk=cbk: e.tensor_tensor(fc3, fc3, cbk, ALU.add))
        tok_o = P.v(lambda e, fc3=fc3, xc3=xc3: e.tensor_tensor(fc3, fc3, xc3, ALU.add))
        guard[br] = tok_o
        guard[bc] = tok_o
        tok_os[sl] = tok_o
        P.wait("gpsimd", tok_o)
        tok_od[sl] = tok_outd = P.dma("gpsimd", d_out[:, :, ck * 128:(ck + 1) * 128], fc3)
    P.wait("gpsimd", tok_outd)
    return nc, P

    tok_vq = None
    tok_x1dma = None
    vstat_rows = []
    for qc in range(4):
        P.waits("sync", tok_adma[0], tok_adma[1], tok_astat, tok_vq, tok_x1dma)
        tok_aq = P.dma("sync", aq3, d_ascr[:, :, qc * 256:(qc + 1) * 256])
        vslot = AR[:, qc * 8192:(qc + 1) * 8192]
        vslot3 = vslot.rearrange("p (v t) -> p v t", t=256)
        P.wait("sync", tok_u)
        tok_xq = P.dma("sync", vslot3, d_xown[:, :, qc * 256:(qc + 1) * 256])
        tok_r = repl(6, 7, 256, tok_arows,
                     src_r=RROWB[0:1, qc * 256:(qc + 1) * 256],
                     src_c=CROWB[0:1, qc * 256:(qc + 1) * 256])
        P.waits("vector", tok_aq, tok_xq, tok_r)
        rb = PX[6][:, 0:256].unsqueeze(1).broadcast_to([128, VG, 256])
        cbk = PX[7][:, 0:256].unsqueeze(1).broadcast_to([128, VG, 256])
        P.v(lambda e, rb=rb: e.tensor_tensor(aq3, aq3, rb, ALU.mult))
        P.v(lambda e, cbk=cbk: e.tensor_tensor(aq3, aq3, cbk, ALU.add))
        tok_x1 = P.v(lambda e, vslot3=vslot3: e.tensor_tensor(aq3, aq3, vslot3, ALU.add))
        guard[6] = tok_x1
        guard[7] = tok_x1
        P.wait("gpsimd", tok_x1)
        tok_x1dma = P.dma("gpsimd", d_x1scr[:, :, qc * 256:(qc + 1) * 256], aq3)
        P.wait("scalar", tok_x1)
        tok_sq = P.s(lambda e, vslot=vslot: e.activation(vslot, A2[:, 0:8192], AF.Square))
        b = 4
        bankw(b)
        P.wait("tensor", tok_x1)
        for vg in range(VG):
            P.t(lambda e, vg=vg: e.matmul(PX[4][0:1, 0:256], ONESK[:, :], aq3[:, vg, :],
                                          start=(vg == 0), stop=(vg == VG - 1)))
        P.wait("tensor", tok_sq)
        for vg in range(VG):
            tok_st = P.t(lambda e, vg=vg, vslot3=vslot3: e.matmul(
                PX[4][32:33, 0:256], ONESK[:, :], vslot3[:, vg, :],
                start=(vg == 0), stop=(vg == VG - 1)))
        tok_coef = coef_rows(PX[4], 256, V, tok_st)
        guard[4] = tok_coef
        tok_r = repl(5, 0, 256, tok_coef)
        P.wait("vector", tok_r)
        rb = PX[5][:, 0:256].unsqueeze(1).broadcast_to([128, VG, 256])
        cbk = PX[0][:, 0:256].unsqueeze(1).broadcast_to([128, VG, 256])
        P.v(lambda e, vslot3=vslot3, rb=rb: e.tensor_tensor(vslot3, aq3, rb, ALU.mult))
        tok_vq = P.v(lambda e, vslot3=vslot3, cbk=cbk: e.tensor_tensor(vslot3, vslot3, cbk, ALU.add))
        guard[5] = tok_vq
        guard[0] = tok_vq
        vstat_rows.append(tok_vq)

    if stop == "x1v":
        P.wait("gpsimd", tok_vq)
        tg = P.dma("gpsimd", d_tap[:, :], AR[:, :])
        P.wait("gpsimd", tg)
        return nc, P

    P.waits("sync", tok_vq)
    tok_at = P.dma("sync", aTS, d_aT)
    m8 = RT[:, 0:8]
    lg = RT[:, 8:16]
    qe = RT[:, 16:24]
    msk = RT[:, 24:32]
    den = RT[:, 32:33]
    nm1 = RT[:, 33:34]
    r2 = RT[:, 34:35]
    tok_h2 = None
    for tt in range(8):
        qc, th = tt // 2, tt % 2
        bankw(3)
        P.waits("tensor", tok_at, vstat_rows[qc])
        for vg in range(VG):
            lhs = AR4[:, qc, vg, th * 128:(th + 1) * 128]
            tok_ha = P.t(lambda e, vg=vg, lhs=lhs: e.matmul(
                PX[3][:, 0:136], lhs, aTS[:, vg, :],
                start=(vg == 0), stop=(vg == VG - 1)))
        tok_rt = tok_ha
        if stop == "p5a" and tt == 0:
            P.wait("gpsimd", tok_rt)
            tg = P.dma("gpsimd", d_tap[:, 0:1024], h2b)
            P.wait("gpsimd", tg)
            return nc, P
        P.wait("vector", tok_rt)
        P.small_mode = True
        P.v(lambda e: e.tensor_copy(lg, PX[3][:, 128:136]))
        tok_lg = P.v(lambda e: e.max(m8, lg))
        tok_nm1 = P.v(lambda e: e.tensor_scalar(nm1, m8[:, 0:1], -1.0, None, ALU.mult))
        P.wait("scalar", tok_nm1)
        tok_q = P.s(lambda e: e.activation(qe, lg, AF.Exp, bias=nm1))
        P.wait("vector", tok_q)
        P.v(lambda e: e.tensor_single_scalar(msk, lg, m8[:, 1:2], ALU.is_ge))
        P.v(lambda e: e.tensor_tensor(qe, qe, msk, ALU.mult))
        P.v(lambda e: e.reduce_sum(den, qe, axis=AXX))
        P.v(lambda e: e.reciprocal(r2, den))
        P.v(lambda e: e.tensor_scalar(r2, r2, 2.0, None, ALU.mult))
        P.v(lambda e: e.tensor_scalar(qe, qe, r2, None, ALU.mult))
        if stop == "p5b" and tt == 0:
            P.wait("gpsimd", tok_ha)
            tg = P.dma("gpsimd", d_tap[:, 0:1024], h2b)
            P.wait("gpsimd", tg)
            return nc, P
        cmb = qe.rearrange("p (a b) -> p a b", b=1).broadcast_to([128, 8, 16])
        P.wait("vector", tok_ha)
        tok_cp = P.v(lambda e: e.tensor_copy(H2W, PX[3][:, 0:128]))
        if stop == "p5c0" and tt == 0:
            P.wait("gpsimd", tok_cp)
            tg = P.dma("gpsimd", d_tap[:, 0:128], H2W)
            P.wait("gpsimd", tg)
            return nc, P
        guard[3] = tok_cp
        tok_h2t = P.v(lambda e, cmb=cmb: e.tensor_tensor(
            H2T.rearrange("p (a b) -> p a b", b=16),
            H2W.rearrange("p (a b) -> p a b", b=16), cmb, ALU.mult))
        P.small_mode = False
        if stop == "p5c" and tt == 0:
            P.wait("gpsimd", tok_h2t)
            tg = P.dma("gpsimd", d_tap[:, 0:128], H2T)
            P.wait("gpsimd", tg)
            return nc, P
        bankw(6)
        P.waits("tensor", tok_h2t, tok_id)
        tok_tr = P.t(lambda e: e.transpose(PX[6][:, 0:128], H2T, IDT[:, :]))
        P.wait("vector", tok_tr)
        tok_h2 = P.v(lambda e, tt=tt: e.tensor_copy(h2b[:, tt * 128:(tt + 1) * 128], PX[6][:, 0:128]))
        guard[6] = tok_h2
        if stop == "ffn_h2" and tt == 1:
            P.wait("gpsimd", tok_h2)
            tg = P.dma("gpsimd", d_tap[:, 0:1024], h2b)
            P.wait("gpsimd", tg)
            return nc, P

    P.wait("sync", tok_h2)
    tok_wb = [tok_at, tok_at]
    tok_fdma = None
    tok_fsq = None
    tok_fstat = None
    tok_evprev = None
    for og in range(VG):
        sl = og % 2
        P.wait("sync", tok_wb[sl])
        if sl == 1:
            P.wait("sync", tok_ha)
        tok_wbd = P.dma("sync", wbslot[sl], d_wbT[og])
        tok_btd = P.dma("sync", btslot[sl], d_bT[:, og * 128:(og + 1) * 128])
        evt = []
        for tc in range(2):
            b = (og % 2) * 2 + tc
            bankw(b)
            P.waits("tensor", tok_wbd, tok_btd, tok_h2)
            for kg in range(VG):
                rhs = AR4[:, 2 * tc:2 * tc + 2, kg, :]
                P.t(lambda e, kg=kg, sl=sl, rhs=rhs, b=b: e.matmul(
                    PX[b][:, :], wbslot[sl][:, kg, :], rhs, start=(kg == 0), stop=False))
            tok_base = P.t(lambda e, sl=sl, tc=tc, b=b: e.matmul(
                PX[b][:, :], btslot[sl], h2b[:, tc * 512:(tc + 1) * 512], start=False, stop=True))
            tok_wb[sl] = tok_base
            P.waits("vector", tok_base, tok_fdma, tok_fsq, tok_fstat)
            tok_ev = P.v(lambda e, og=og, tc=tc, b=b: e.tensor_scalar(
                ftile[:, tc * 512:(tc + 1) * 512], PX[b][:, :],
                SM[:, SC["b_base"] + og: SC["b_base"] + og + 1], None, ALU.add))
            guard[b] = tok_ev
            evt.append(tok_ev)
        P.wait("gpsimd", evt[-1])
        tok_fdma = P.dma("gpsimd", d_fscr[:, og, :], ftile)
        P.waits("scalar", evt[-1], tok_fstat)
        tok_fsq = P.s(lambda e: e.activation(fsq, ftile, AF.Square))
        for tc in range(2):
            b = 4 + tc
            if og == 0:
                bankw(b)
            P.wait("tensor", evt[tc])
            P.t(lambda e, og=og, tc=tc, b=b: e.matmul(
                PX[b][0:1, 0:512], ONESK[:, :], ftile[:, tc * 512:(tc + 1) * 512],
                start=(og == 0), stop=(og == VG - 1)))
            P.wait("tensor", tok_fsq)
            tok_fstat = P.t(lambda e, og=og, tc=tc, b=b: e.matmul(
                PX[b][32:33, 0:512], ONESK[:, :], fsq[:, tc * 512:(tc + 1) * 512],
                start=(og == 0), stop=(og == VG - 1)))
        if stop == "ffn_f" and og == 1:
            P.wait("gpsimd", tok_fdma)
            tg = P.dma("gpsimd", d_tap[:, 0:1024], ftile)
            P.wait("gpsimd", tg)
            return nc, P
    for tc in range(2):
        tok_coef = coef_rows(PX[4 + tc], 512, V, tok_fstat)
        guard[4 + tc] = tok_coef
        P.v(lambda e, tc=tc: e.tensor_copy(RROWB[0:1, tc * 512:(tc + 1) * 512], RROW[0:1, :]))
        tok_frows = P.v(lambda e, tc=tc: e.tensor_copy(CROWB[0:1, tc * 512:(tc + 1) * 512], MUROW[0:1, :]))

    tok_outd = None
    for tc in range(2):
        fch = AR[:, 0:16384].rearrange("p (v t) -> p v t", t=512)
        xch = AR[:, 16384:32768].rearrange("p (v t) -> p v t", t=512)
        P.waits("sync", tok_base, tok_fdma, tok_outd)
        tok_fc = P.dma("sync", fch, d_fscr[:, :, tc * 512:(tc + 1) * 512])
        P.wait("sync", tok_x1dma)
        tok_xc = P.dma("sync", xch, d_x1scr[:, :, tc * 512:(tc + 1) * 512])
        tok_r = repl(6, 7, 512, tok_frows,
                     src_r=RROWB[0:1, tc * 512:(tc + 1) * 512],
                     src_c=CROWB[0:1, tc * 512:(tc + 1) * 512])
        P.waits("vector", tok_fc, tok_xc, tok_r)
        rb = PX[6][:, :].unsqueeze(1).broadcast_to([128, VG, 512])
        cbk = PX[7][:, :].unsqueeze(1).broadcast_to([128, VG, 512])
        P.v(lambda e, fch=fch, rb=rb: e.tensor_tensor(fch, fch, rb, ALU.mult))
        P.v(lambda e, fch=fch, cbk=cbk: e.tensor_tensor(fch, fch, cbk, ALU.add))
        tok_o = P.v(lambda e, fch=fch, xch=xch: e.tensor_tensor(fch, fch, xch, ALU.add))
        guard[6] = tok_o
        guard[7] = tok_o
        P.wait("gpsimd", tok_o)
        tok_outd = P.dma("gpsimd", d_out[:, :, tc * 512:(tc + 1) * 512], fch)
    P.wait("gpsimd", tok_outd)
    return nc, P


def finalize(nc, P):
    with (
        nc.semaphore("sD") as sD,
        nc.semaphore("sG") as sG,
        nc.semaphore("sT") as sT,
        nc.semaphore("sV") as sV,
        nc.semaphore("sS") as sS,
        nc.Block() as block,
    ):
        sems = dict(D=sD, G=sG, T=sT, V=sV, S=sS)

        def runner(ename):
            def run(e):
                for item in P.q[ename]:
                    if item[0] == "w":
                        e.wait_ge(sems[item[1]], item[2])
                    else:
                        _, fn, sem, amt = item
                        ins = fn(e)
                        if sem is not None:
                            ins.then_inc(sems[sem], amt)
            return run
        block.sync(runner("sync"))
        block.tensor(runner("tensor"))
        block.vector(runner("vector"))
        block.scalar(runner("scalar"))
        block.gpsimd(runner("gpsimd"))
    return nc


def build(stop=None):
    nc, P = build_program(stop)
    return finalize(nc, P)


def _pack_weights(inputs):
    f32 = np.float32
    g = lambda k: np.asarray(inputs[k], dtype=f32)
    w = {}
    w["winT"] = np.ascontiguousarray(g("L0_attn_Win").reshape(D, VG, 128).transpose(2, 1, 0))
    w["w1T"] = np.ascontiguousarray(g("L0_attn_W1").reshape(DEPTH, FF, DG, 128).transpose(0, 3, 2, 1))
    w["w2T"] = np.ascontiguousarray(g("L0_attn_W2").reshape(DEPTH, D, FG, 128).transpose(0, 3, 2, 1))
    w["woutT"] = np.ascontiguousarray(g("L0_attn_Wout").reshape(V, DG, 128).transpose(2, 1, 0))
    w["wbT"] = np.ascontiguousarray(g("L0_ffn_W_base").reshape(VG, 128, VG, 128).transpose(0, 3, 2, 1))
    at = g("L0_ffn_A_stack").reshape(128, VG, 128).transpose(2, 1, 0)
    wrt = g("L0_ffn_W_router").reshape(8, VG, 128).transpose(2, 1, 0)
    w["aT"] = np.ascontiguousarray(np.concatenate([at, wrt], axis=2))
    w["bT"] = np.ascontiguousarray(g("L0_ffn_B_stack").transpose(0, 2, 1).reshape(128, V))
    w["id128"] = np.eye(128, dtype=f32)

    def chg(vec):
        return np.asarray(vec, f32).reshape(4, 128).T

    sm = np.zeros((128, NS), f32)
    sm[:, SC["bin"]:SC["bin"] + 4] = chg(g("L0_attn_bin"))
    for name, key in (("ln1_w", "L0_attn_ln1_w"), ("ln1_b", "L0_attn_ln1_b"),
                      ("ln2_w", "L0_attn_ln2_w"), ("ln2_b", "L0_attn_ln2_b"),
                      ("phz_re", "L0_attn_phz_re"), ("phz_im", "L0_attn_phz_im"),
                      ("lc_re", "L0_attn_lc_re"), ("lc_im", "L0_attn_lc_im"),
                      ("sc_lin", "L0_attn_sc_lin"), ("b2", "L0_attn_b2")):
        arr = g(key)
        for i in range(DEPTH):
            sm[:, SC[name] + i * 4: SC[name] + (i + 1) * 4] = chg(arr[i])
    sm[:, SC["lnl_w"]:SC["lnl_w"] + 4] = chg(g("L0_attn_lnl_w"))
    sm[:, SC["lnl_b"]:SC["lnl_b"] + 4] = chg(g("L0_attn_lnl_b"))
    b1 = g("L0_attn_b1")
    for i in range(DEPTH):
        sm[:, SC["b1"] + i * 16: SC["b1"] + (i + 1) * 16] = b1[i].reshape(16, 128).T
    sm[:, SC["bout"]:SC["bout"] + 32] = g("L0_attn_bout").reshape(32, 128).T
    sm[:, SC["b_base"]:SC["b_base"] + 32] = g("L0_ffn_b_base").reshape(32, 128).T
    w["smalls_base"] = sm
    return w


_CACHE = {}


def kernel(**inputs):
    key = "full"
    if key not in _CACHE:
        _CACHE[key] = build()
    nc = _CACHE[key]
    w = _pack_weights(inputs)
    x = np.asarray(inputs["L0_attn_x"], np.float32)
    in_maps = []
    for c in range(8):
        b, m = c // 2, c % 2
        xT = np.ascontiguousarray(x[b].reshape(L, VG, 128).transpose(2, 1, 0))
        xown = np.ascontiguousarray(xT[:, :, m * TOK:(m + 1) * TOK])
        sm = w["smalls_base"].copy()
        sm[:, SC["sel"]] = 1.0 if m == 0 else 0.0
        sm[:, SC["sel"] + 1] = 0.0 if m == 0 else 1.0
        in_maps.append({
            "xT": xT, "xown": xown, "winT": w["winT"], "w1T": w["w1T"],
            "w2T": w["w2T"], "woutT": w["woutT"], "wbT": w["wbT"],
            "aT": w["aT"], "bT": w["bT"],
            "smalls": sm, "id128": w["id128"],
        })
    res = run_bass_kernel_spmd(nc, in_maps, core_ids=list(range(8)))
    out = np.empty((4, L, V), np.float32)
    for c in range(8):
        b, m = c // 2, c % 2
        o = res.results[c]["OUT"]
        out[b, m * TOK:(m + 1) * TOK] = np.ascontiguousarray(
            np.transpose(o, (2, 1, 0)).reshape(TOK, V))
    return out


# version 12
# speedup vs baseline: 1.1032x; 1.0217x over previous
import math
import numpy as np
import concourse.bass as bass
import concourse.mybir as mybir
from concourse.bass_utils import run_bass_kernel_spmd

F32 = mybir.dt.float32
BF16 = mybir.dt.bfloat16
AF = mybir.ActivationFunctionType
ALU = mybir.AluOpType
AXX = mybir.AxisListType.X

V, D, FF, L, TOK = 4096, 512, 2048, 2048, 1024
VG, DG, FG = 32, 4, 16
DEPTH, Q, NCH = 4, 128, 16
EPS = 1e-5
EC = 256
NECK = L // EC

SC = {}
_c = 0
def _col(name, n):
    global _c
    SC[name] = _c
    _c += n
_col("bin", 4); _col("ln1_w", 16); _col("ln1_b", 16); _col("ln2_w", 16)
_col("ln2_b", 16); _col("phz_re", 16); _col("phz_im", 16); _col("lc_re", 16)
_col("lc_im", 16); _col("sc_lin", 16); _col("b2", 16); _col("lnl_w", 4)
_col("lnl_b", 4); _col("b1", 64); _col("bout", 32); _col("b_base", 32)
_col("sel", 2)
NS = _c


class Plan:
    ENGS = ("sync", "tensor", "vector", "scalar", "gpsimd")
    OWN = {"tensor": "T", "vector": "V", "scalar": "S"}

    def __init__(self):
        self.q = {e: [] for e in self.ENGS}
        self.cnt = dict(D=0, G=0, T=0, V=0, S=0)
        self.last_wait = {e: {} for e in self.ENGS}
        self.small_mode = False

    def wait(self, eng, tok):
        if tok is None:
            return
        s, v = tok
        if v <= 0:
            return
        if self.OWN.get(eng) == s:
            return
        lw = self.last_wait[eng]
        if lw.get(s, -1) >= v:
            return
        lw[s] = v
        self.q[eng].append(("w", s, v))

    def waits(self, eng, *toks):
        for t in toks:
            self.wait(eng, t)

    def do(self, eng, fn, sem=None, amt=1):
        tok = None
        if sem is not None:
            self.cnt[sem] += amt
            tok = (sem, self.cnt[sem])
            self.last_wait[eng][sem] = self.cnt[sem] if self.OWN.get(eng) == sem else self.last_wait[eng].get(sem, -1)
        self.q[eng].append(("i", fn, sem, amt))
        return tok

    def t(self, fn, sem="T"):
        return self.do("tensor", fn, sem)

    def v(self, fn, sem="V"):
        tok = self.do("vector", fn, sem)
        if self.small_mode:
            self.q["vector"].append(("w", "V", tok[1]))
        return tok

    def s(self, fn, sem="S"):
        tok = self.do("scalar", fn, sem)
        if self.small_mode:
            self.q["scalar"].append(("w", "S", tok[1]))
        return tok

    def g(self, fn):
        return self.do("gpsimd", fn, "G", 1)

    def vbar(self):
        self.q["vector"].append(("w", "V", self.cnt["V"]))

    def dma(self, eng, out_ap, in_ap):
        sem = "D" if eng == "sync" else "G"
        return self.do(eng, lambda e, o=out_ap, i=in_ap: e.dma_start(out=o, in_=i), sem, 16)


def build_program(stop=None):
    nc = bass.Bass("TRN2", target_bir_lowering=False)

    d_xT = nc.dram_tensor("xT", [128, VG, L], F32, kind="ExternalInput").ap()
    d_xown = nc.dram_tensor("xown", [128, VG, TOK], F32, kind="ExternalInput").ap()
    d_winT = nc.dram_tensor("winT", [128, VG, D], F32, kind="ExternalInput").ap()
    d_w1T = nc.dram_tensor("w1T", [DEPTH, 128, DG, FF], F32, kind="ExternalInput").ap()
    d_w2T = nc.dram_tensor("w2T", [DEPTH, 128, FG, D], F32, kind="ExternalInput").ap()
    d_woutT = nc.dram_tensor("woutT", [128, DG, V], F32, kind="ExternalInput").ap()
    d_wbT = nc.dram_tensor("wbT", [VG, 128, VG, 128], F32, kind="ExternalInput").ap()
    d_aT = nc.dram_tensor("aT", [128, VG, 136], F32, kind="ExternalInput").ap()
    d_bT = nc.dram_tensor("bT", [128, V], F32, kind="ExternalInput").ap()
    d_sm = nc.dram_tensor("smalls", [128, NS], F32, kind="ExternalInput").ap()
    d_id = nc.dram_tensor("id128", [128, 128], F32, kind="ExternalInput").ap()
    d_ascr = nc.dram_tensor("a_scr", [128, VG, TOK], F32, kind="Internal").ap()
    d_x1scr = nc.dram_tensor("x1_scr", [128, VG, TOK], F32, kind="Internal").ap()
    d_fscr = nc.dram_tensor("f_scr", [128, VG, TOK], F32, kind="Internal").ap()
    d_out = nc.dram_tensor("OUT", [128, VG, TOK], F32, kind="ExternalOutput").ap()
    d_tap = None
    if stop is not None:
        d_tap = nc.dram_tensor("TAP", [128, 32768], F32, kind="ExternalOutput").ap()

    AR_h = nc.alloc_sbuf_tensor("ARENA", [128, 32768], F32)
    AR = AR_h.ap()
    A2_h = nc.alloc_sbuf_tensor("ARENA2", [128, 12288], F32)
    A2 = A2_h.ap()
    ar_addr = nc.lookup_mloc(AR_h).addr
    a2_addr = nc.lookup_mloc(A2_h).addr

    def _alias(name, n_bf16, f32_off, base=None):
        return nc.alloc_sbuf_tensor_at(
            name, [128, n_bf16], BF16,
            offset=(ar_addr if base is None else base) + 4 * f32_off).ap()
    tCB = nc.alloc_sbuf_tensor("tCB", [128, 4 * 129], F32).ap()
    tSB = nc.alloc_sbuf_tensor("tSB", [128, 4 * 129], F32).ap()
    tNSB = nc.alloc_sbuf_tensor("tNSB", [128, 4 * 129], F32).ap()
    tRP = nc.alloc_sbuf_tensor("tRP", [128, 4 * 128], F32).ap()
    tCQ = nc.alloc_sbuf_tensor("tCQ", [128, 4 * 17], F32).ap()
    tSQ = nc.alloc_sbuf_tensor("tSQ", [128, 4 * 17], F32).ap()
    tS1 = nc.alloc_sbuf_tensor("tS1", [128, 4 * 64], F32).ap()
    tS2 = nc.alloc_sbuf_tensor("tS2", [128, 4 * 64], F32).ap()
    SC4 = nc.alloc_sbuf_tensor("SC4", [128, 32], F32).ap()
    S16 = nc.alloc_sbuf_tensor("S16", [128, 16 * 16], F32).ap()
    SM = nc.alloc_sbuf_tensor("SM", [128, NS], F32).ap()
    IDT = nc.alloc_sbuf_tensor("IDT", [128, 128], F32).ap()
    ONESK = nc.alloc_sbuf_tensor("ONESK", [128, 1], F32).ap()
    ONES1 = nc.alloc_sbuf_tensor("ONES1", [1, 128], F32).ap()
    MUROW = nc.alloc_sbuf_tensor("MUROW", [1, 512], F32).ap()
    RROW = nc.alloc_sbuf_tensor("RROW", [1, 512], F32).ap()
    SROW = nc.alloc_sbuf_tensor("SROW", [1, 512], F32).ap()
    RROWB = nc.alloc_sbuf_tensor("RROWB", [1, TOK], F32).ap()
    CROWB = nc.alloc_sbuf_tensor("CROWB", [1, TOK], F32).ap()
    WCS = nc.alloc_sbuf_tensor("WCS", [128, 4], F32).ap()

    PX = [nc.alloc_psum_tensor(f"PX{i}", [128, 512], F32).ap() for i in range(8)]

    h3 = AR[:, 24576:32768].rearrange("p (g t) -> p g t", t=L)
    hf = AR[:, 24576:32768]
    winTS = AR[:, 0:16384].rearrange("p (v d) -> p v d", d=D)
    woutTS = AR[:, 0:16384].rearrange("p (k o) -> p k o", o=V)
    AR4 = AR.rearrange("p (q v t) -> p q v t", v=VG, t=256)
    w1stg = AR[:, 0:8192].rearrange("p (k f) -> p k f", f=FF)
    w2stg = AR[:, 0:8192].rearrange("p (k d) -> p k d", d=D)
    W1Hf = _alias("W1H", 8192, 8192)
    W1Lf = _alias("W1L", 8192, 12288)
    W2Hf = _alias("W2H", 8192, 16384)
    W2Lf = _alias("W2L", 8192, 20480)
    MIDHf = _alias("MIDH", 8192, 0)
    MIDLf = _alias("MIDL", 8192, 4096)
    ZHf = _alias("ZH", 8192, 0, base=a2_addr)
    ZLf = _alias("ZL", 8192, 4096, base=a2_addr)
    W1H = W1Hf.rearrange("p (k f) -> p k f", f=FF)
    W1L = W1Lf.rearrange("p (k f) -> p k f", f=FF)
    W2H = W2Hf.rearrange("p (k d) -> p k d", d=D)
    W2L = W2Lf.rearrange("p (k d) -> p k d", d=D)
    MIDH = MIDHf.rearrange("p (k t) -> p k t", t=512)
    MIDL = MIDLf.rearrange("p (k t) -> p k t", t=512)
    ZH = ZHf.rearrange("p (k t) -> p k t", t=L)
    ZL = ZLf.rearrange("p (k t) -> p k t", t=L)
    ztmp3 = A2[:, 8192:10240].rearrange("p (k t) -> p k t", t=512)
    mtmp = [A2[:, 10240:10752], A2[:, 10752:11264]]

    R = [A2[:, j * 2048:(j + 1) * 2048] for j in range(6)]
    sqb = A2[:, 0:8192]
    hn3 = A2[:, 0:8192].rearrange("p (g t) -> p g t", t=L)
    hno3 = A2[:, 8192:12288].rearrange("p (g t) -> p g t", t=TOK)
    atile = [A2[:, 0:1024], A2[:, 1024:2048]]
    asq = [A2[:, 2048:3072], A2[:, 3072:4096]]
    aq3 = A2[:, 0:8192].rearrange("p (v t) -> p v t", t=256)
    EV = A2[:, 8192:8704]

    wostg = AR[:, 16384:24576].rearrange("p (k o) -> p k o", o=V)
    WOHf = _alias("WOH", 16384, 0)
    WOLf = _alias("WOL", 16384, 8192)
    WOH = WOHf.rearrange("p (k o) -> p k o", o=V)
    WOL = WOLf.rearrange("p (k o) -> p k o", o=V)
    HNHf = _alias("HNH", 4096, 16384)
    HNLf = _alias("HNL", 4096, 18432)
    HNH = HNHf.rearrange("p (k t) -> p k t", t=TOK)
    HNL = HNLf.rearrange("p (k t) -> p k t", t=TOK)
    VHf = _alias("VH", 16384, 0)
    VLf = _alias("VL", 16384, 8192)
    VH = VHf.rearrange("p (v t) -> p v t", t=512)
    VL = VLf.rearrange("p (v t) -> p v t", t=512)
    xq3 = AR[:, 16384:24576].rearrange("p (v t) -> p v t", t=256)
    sqv = AR[:, 16384:24576]
    sqv3 = xq3
    atstg = AR[:, 16384:20736]
    ATHf = _alias("ATH", 4352, 24576)
    ATLf = _alias("ATL", 4352, 26752)
    ATH = ATHf.rearrange("p (v e) -> p v e", e=136)
    ATL = ATLf.rearrange("p (v e) -> p v e", e=136)
    h2b = AR[:, 28928:29440]
    H2T = AR[:, 29440:29568]
    RT = AR[:, 29568:29616]
    H2W = AR[:, 29616:29744]
    ftile = AR[:, 29744:30256]
    fsq = AR[:, 30256:30768]
    H2Hf = _alias("H2H", 512, 30768)
    H2Lf = _alias("H2L", 512, 31024)
    btstg = [AR[:, 31280:31408], AR[:, 31408:31536]]
    BTH = [_alias("BT0H", 128, 31616), _alias("BT1H", 128, 31744)]
    BTL = [_alias("BT0L", 128, 31680), _alias("BT1L", 128, 31808)]
    wbstg = [A2[:, 8192:12288], AR[:, 20480:24576]]
    WBHf = [_alias("WBH0", 4096, 0, base=a2_addr),
            _alias("WBH1", 4096, 4096, base=a2_addr)]
    WBLf = [_alias("WBL0", 4096, 2048, base=a2_addr),
            _alias("WBL1", 4096, 6144, base=a2_addr)]
    WBH = [t.rearrange("p (k o) -> p k o", o=128) for t in WBHf]
    WBL = [t.rearrange("p (k o) -> p k o", o=128) for t in WBLf]
    fch3 = AR[:, 0:8192].rearrange("p (v t) -> p v t", t=256)
    xch3 = AR[:, 8192:16384].rearrange("p (v t) -> p v t", t=256)

    P = Plan()
    guard = [None] * 8
    last_repl = [None]

    def bankw(i):
        P.wait("tensor", guard[i])

    def coef_rows(ps_bank, w, n, tok_stats, qoff=0):
        P.wait("vector", tok_stats)
        P.wait("vector", last_repl[0])
        P.small_mode = True
        P.v(lambda e: e.tensor_scalar(MUROW[0:1, 0:w], ps_bank[0:1, qoff:qoff + w], 1.0 / n, None, ALU.mult))
        P.v(lambda e: e.tensor_scalar(SROW[0:1, 0:w], ps_bank[32:33, qoff:qoff + w], 1.0 / n, None, ALU.mult))
        P.v(lambda e: e.tensor_tensor(RROW[0:1, 0:w], MUROW[0:1, 0:w], MUROW[0:1, 0:w], ALU.mult))
        P.v(lambda e: e.tensor_tensor(SROW[0:1, 0:w], SROW[0:1, 0:w], RROW[0:1, 0:w], ALU.subtract))
        t1 = P.v(lambda e: e.tensor_scalar(SROW[0:1, 0:w], SROW[0:1, 0:w], EPS, None, ALU.add))
        P.wait("scalar", t1)
        t2 = P.s(lambda e: e.activation(SROW[0:1, 0:w], SROW[0:1, 0:w], AF.Sqrt))
        P.wait("vector", t2)
        P.v(lambda e: e.reciprocal(RROW[0:1, 0:w], SROW[0:1, 0:w]))
        P.v(lambda e: e.tensor_tensor(MUROW[0:1, 0:w], MUROW[0:1, 0:w], RROW[0:1, 0:w], ALU.mult))
        tok = P.v(lambda e: e.tensor_scalar(MUROW[0:1, 0:w], MUROW[0:1, 0:w], -1.0, None, ALU.mult))
        P.small_mode = False
        return tok

    def repl(bank_r, bank_c, w, tok_coef, src_r=None, src_c=None, qoff=0):
        sr = RROW[0:1, 0:w] if src_r is None else src_r
        scr = MUROW[0:1, 0:w] if src_c is None else src_c
        P.wait("tensor", tok_coef)
        bankw(bank_r)
        P.t(lambda e, b=bank_r, sr=sr, w=w, qoff=qoff: e.matmul(
            PX[b][:, qoff:qoff + w], ONES1, sr, start=True, stop=True))
        bankw(bank_c)
        tok = P.t(lambda e, b=bank_c, scr=scr, w=w, qoff=qoff: e.matmul(
            PX[b][:, qoff:qoff + w], ONES1, scr, start=True, stop=True))
        last_repl[0] = tok
        return tok

    tok_sm = P.dma("sync", SM[:, :], d_sm)
    tok_id = P.dma("sync", IDT[:, :], d_id)
    tok_win = P.dma("sync", winTS, d_winT)
    P.v(lambda e: e.memset(ONESK[:, :], 1.0))
    tok_ones = P.v(lambda e: e.memset(ONES1[:, :], 1.0))

    P.waits("tensor", tok_win, tok_ones)
    for dg in range(DG):
        for vg in range(VG):
            tk = P.t(lambda e, vg=vg, dg=dg: e.matmul(
                PX[6][:, dg:dg + 1], winTS[:, vg, dg * 128:(dg + 1) * 128], ONESK[:, :],
                start=(vg == 0), stop=(vg == VG - 1)))
    P.wait("vector", tk)
    P.small_mode = True
    tok_wcs = P.v(lambda e: e.tensor_copy(WCS[:, :], PX[6][:, 0:4]))
    P.small_mode = False
    guard[6] = tok_wcs

    xcbuf = [AR[:, 16384:24576], A2[:, 0:8192]]
    xcfree = [None, None]
    P.wait("vector", tok_sm)
    for ck in range(NECK):
        buf = xcbuf[ck % 2]
        buf3 = buf.rearrange("p (v t) -> p v t", t=EC)
        P.wait("sync", xcfree[ck % 2])
        tok_xc = P.dma("sync", buf3, d_xT[:, :, ck * EC:(ck + 1) * EC])
        P.wait("tensor", tok_xc)
        P.wait("tensor", tok_sm)
        bankw(0)
        for vg in range(VG):
            P.t(lambda e, vg=vg, buf3=buf3: e.matmul(
                PX[0][0:1, 0:EC], ONESK[:, :], buf3[:, vg, :],
                start=(vg == 0), stop=(vg == VG - 1)))
        for dg in range(DG):
            bankw(2 + dg)
            for vg in range(VG):
                tok_mm = P.t(lambda e, vg=vg, dg=dg, buf3=buf3: e.matmul(
                    PX[2 + dg][:, 0:EC], winTS[:, vg, dg * 128:(dg + 1) * 128], buf3[:, vg, :],
                    start=(vg == 0), stop=(vg == VG - 1)))
        P.wait("scalar", tok_mm)
        tok_sq = P.s(lambda e, buf=buf: e.activation(buf, buf, AF.Square))
        P.wait("tensor", tok_sq)
        for vg in range(VG):
            tok_st = P.t(lambda e, vg=vg, buf3=buf3: e.matmul(
                PX[0][32:33, 0:EC], ONESK[:, :], buf3[:, vg, :],
                start=(vg == 0), stop=(vg == VG - 1)))
        xcfree[ck % 2] = tok_st
        tok_coef = coef_rows(PX[0], EC, V, tok_st)
        guard[0] = tok_coef
        tok_repl = repl(6, 7, EC, tok_coef)
        P.wait("vector", tok_repl)
        P.wait("vector", tok_wcs)
        P.v(lambda e: e.tensor_copy(EV[:, 0:EC], PX[6][:, 0:EC]))
        for dg in range(DG):
            hsl = h3[:, dg, ck * EC:(ck + 1) * EC]
            P.v(lambda e, hsl=hsl, dg=dg: e.tensor_tensor(hsl, PX[2 + dg][:, 0:EC], EV[:, 0:EC], ALU.mult))
            P.v(lambda e, hsl=hsl, dg=dg: e.scalar_tensor_tensor(
                hsl, PX[7][:, 0:EC], WCS[:, dg:dg + 1], hsl, ALU.mult, ALU.add))
            tok_ev = P.v(lambda e, hsl=hsl, dg=dg: e.tensor_scalar(
                hsl, hsl, SM[:, SC["bin"] + dg:SC["bin"] + dg + 1], None, ALU.add))
        for i in range(2, 8):
            guard[i] = tok_ev
    tok_h = tok_ev
    tok_winfree = tok_ev

    if stop == "embed":
        P.wait("gpsimd", tok_h)
        tg = P.dma("gpsimd", d_tap[:, 0:8192], hf)
        P.wait("gpsimd", tg)
        return nc, P

    tok_w1_used = tok_winfree
    tok_w2_used = tok_winfree
    tok_stage_free = tok_winfree
    tok_scanbufs_free = None
    tok_mid_free = None

    def emit_tables(i):
        c = lambda n: SM[:, SC[n] + i * 4: SC[n] + (i + 1) * 4]
        RR, RHO, CS1, SN1, T4a, T4b = (SC4[:, j * 4:(j + 1) * 4] for j in range(6))
        P.wait("vector", tok_sm)
        P.small_mode = True
        P.v(lambda e: e.tensor_tensor(RR, c("phz_re"), c("phz_re"), ALU.mult))
        P.v(lambda e: e.tensor_tensor(T4a, c("phz_im"), c("phz_im"), ALU.mult))
        t = P.v(lambda e: e.tensor_tensor(RR, RR, T4a, ALU.add))
        P.wait("scalar", t)
        t = P.s(lambda e: e.activation(T4a, RR, AF.Sqrt))
        P.wait("vector", t)
        P.v(lambda e: e.reciprocal(T4b, T4a))
        P.v(lambda e: e.tensor_tensor(CS1, RR, T4b, ALU.mult))
        P.v(lambda e: e.tensor_tensor(CS1, CS1, T4b, ALU.mult))
        P.v(lambda e: e.tensor_scalar(CS1, CS1, -0.5, 1.5, ALU.mult, ALU.add))
        P.v(lambda e: e.tensor_tensor(T4b, T4b, CS1, ALU.mult))
        t = P.v(lambda e: e.tensor_tensor(RR, RR, T4b, ALU.mult))
        P.wait("scalar", t)
        t2 = P.s(lambda e: e.activation(RHO, RR, AF.Tanh))
        P.wait("vector", t2)
        P.v(lambda e: e.tensor_tensor(CS1, c("phz_re"), T4b, ALU.mult))
        P.v(lambda e: e.tensor_tensor(SN1, c("phz_im"), T4b, ALU.mult))
        cb = tCB.rearrange("p (g r) -> p g r", r=129)
        sb = tSB.rearrange("p (g r) -> p g r", r=129)
        s1 = tS1.rearrange("p (g r) -> p g r", r=64)
        s2 = tS2.rearrange("p (g r) -> p g r", r=64)
        cs3 = CS1.unsqueeze(2)
        sn3 = SN1.unsqueeze(2)
        P.v(lambda e: e.memset(cb[:, :, 0:1], 1.0))
        P.v(lambda e: e.memset(sb[:, :, 0:1], 0.0))
        P.v(lambda e: e.tensor_copy(cb[:, :, 1:2], cs3))
        P.v(lambda e: e.tensor_copy(sb[:, :, 1:2], sn3))
        n = 2
        while n < 129:
            m = min(n, 129 - n)
            cf, sf = T4a.unsqueeze(2), T4b.unsqueeze(2)
            P.v(lambda e, n=n: e.tensor_tensor(s1[:, :, 0:1], cb[:, :, n - 1:n], cs3, ALU.mult))
            P.v(lambda e, n=n: e.tensor_tensor(s2[:, :, 0:1], sb[:, :, n - 1:n], sn3, ALU.mult))
            P.v(lambda e: e.tensor_tensor(cf, s1[:, :, 0:1], s2[:, :, 0:1], ALU.subtract))
            P.v(lambda e, n=n: e.tensor_tensor(s1[:, :, 0:1], sb[:, :, n - 1:n], cs3, ALU.mult))
            P.v(lambda e, n=n: e.tensor_tensor(s2[:, :, 0:1], cb[:, :, n - 1:n], sn3, ALU.mult))
            P.v(lambda e: e.tensor_tensor(sf, s1[:, :, 0:1], s2[:, :, 0:1], ALU.add))
            cfb = cf.broadcast_to([128, 4, m])
            sfb = sf.broadcast_to([128, 4, m])
            P.v(lambda e, m=m, cfb=cfb: e.tensor_tensor(s1[:, :, 0:m], cb[:, :, 0:m], cfb, ALU.mult))
            P.v(lambda e, m=m, sfb=sfb: e.tensor_tensor(s2[:, :, 0:m], sb[:, :, 0:m], sfb, ALU.mult))
            P.v(lambda e, n=n, m=m: e.tensor_tensor(cb[:, :, n:n + m], s1[:, :, 0:m], s2[:, :, 0:m], ALU.subtract))
            P.v(lambda e, m=m, cfb=cfb: e.tensor_tensor(s1[:, :, 0:m], sb[:, :, 0:m], cfb, ALU.mult))
            P.v(lambda e, m=m, sfb=sfb: e.tensor_tensor(s2[:, :, 0:m], cb[:, :, 0:m], sfb, ALU.mult))
            P.v(lambda e, n=n, m=m: e.tensor_tensor(sb[:, :, n:n + m], s1[:, :, 0:m], s2[:, :, 0:m], ALU.add))
            n *= 2
        P.v(lambda e: e.tensor_scalar(tNSB[:, :], tSB[:, :], -1.0, None, ALU.mult))
        rp = tRP.rearrange("p (g r) -> p g r", r=128)
        P.v(lambda e: e.tensor_copy(rp[:, :, 0:1], RHO.unsqueeze(2)))
        n = 1
        while n < 128:
            m = min(n, 128 - n)
            fb = rp[:, :, n - 1:n].broadcast_to([128, 4, m])
            P.v(lambda e, n=n, m=m, fb=fb: e.tensor_tensor(rp[:, :, n:n + m], rp[:, :, 0:m], fb, ALU.mult))
            n *= 2
        cq = tCQ.rearrange("p (g r) -> p g r", r=17)
        sq = tSQ.rearrange("p (g r) -> p g r", r=17)
        P.v(lambda e: e.memset(cq[:, :, 0:1], 1.0))
        P.v(lambda e: e.memset(sq[:, :, 0:1], 0.0))
        P.v(lambda e: e.tensor_copy(cq[:, :, 1:2], cb[:, :, Q:Q + 1]))
        P.v(lambda e: e.tensor_copy(sq[:, :, 1:2], sb[:, :, Q:Q + 1]))
        n = 2
        while n < 17:
            m = min(n, 17 - n)
            cf, sf = T4a.unsqueeze(2), T4b.unsqueeze(2)
            P.v(lambda e, n=n: e.tensor_tensor(s1[:, :, 0:1], cq[:, :, n - 1:n], cq[:, :, 1:2], ALU.mult))
            P.v(lambda e, n=n: e.tensor_tensor(s2[:, :, 0:1], sq[:, :, n - 1:n], sq[:, :, 1:2], ALU.mult))
            P.v(lambda e: e.tensor_tensor(cf, s1[:, :, 0:1], s2[:, :, 0:1], ALU.subtract))
            P.v(lambda e, n=n: e.tensor_tensor(s1[:, :, 0:1], sq[:, :, n - 1:n], cq[:, :, 1:2], ALU.mult))
            P.v(lambda e, n=n: e.tensor_tensor(s2[:, :, 0:1], cq[:, :, n - 1:n], sq[:, :, 1:2], ALU.mult))
            P.v(lambda e: e.tensor_tensor(sf, s1[:, :, 0:1], s2[:, :, 0:1], ALU.add))
            cfb = cf.broadcast_to([128, 4, m])
            sfb = sf.broadcast_to([128, 4, m])
            P.v(lambda e, m=m, cfb=cfb: e.tensor_tensor(s1[:, :, 0:m], cq[:, :, 0:m], cfb, ALU.mult))
            P.v(lambda e, m=m, sfb=sfb: e.tensor_tensor(s2[:, :, 0:m], sq[:, :, 0:m], sfb, ALU.mult))
            P.v(lambda e, n=n, m=m: e.tensor_tensor(cq[:, :, n:n + m], s1[:, :, 0:m], s2[:, :, 0:m], ALU.subtract))
            P.v(lambda e, m=m, cfb=cfb: e.tensor_tensor(s1[:, :, 0:m], sq[:, :, 0:m], cfb, ALU.mult))
            P.v(lambda e, m=m, sfb=sfb: e.tensor_tensor(s2[:, :, 0:m], cq[:, :, 0:m], sfb, ALU.mult))
            tk = P.v(lambda e, n=n, m=m: e.tensor_tensor(sq[:, :, n:n + m], s1[:, :, 0:m], s2[:, :, 0:m], ALU.add))
            n *= 2
        P.small_mode = False
        return tk

    def inner_layer(i):
        nonlocal tok_w1_used, tok_w2_used, tok_scanbufs_free, tok_mid_free, tok_h
        nonlocal tok_stage_free
        P.waits("sync", tok_stage_free)
        tok_w1s = P.dma("sync", w1stg, d_w1T[i])
        P.waits("scalar", tok_w1s, tok_w1_used)
        tok_w1h = P.s(lambda e: e.activation(W1Hf, AR[:, 0:8192], AF.Copy))
        P.waits("gpsimd", tok_w1h, tok_w1_used)
        tok_w1l = P.g(lambda e: e.tensor_tensor(
            W1Lf, AR[:, 0:8192], W1Hf, ALU.subtract))
        P.waits("sync", tok_w1h, tok_w1l)
        tok_w2s = P.dma("sync", w2stg, d_w2T[i])
        P.waits("scalar", tok_w2s, tok_w2_used)
        tok_w2h = P.s(lambda e: e.activation(W2Hf, AR[:, 0:8192], AF.Copy))
        P.waits("gpsimd", tok_w2h, tok_w2_used)
        tok_w2l = P.g(lambda e: e.tensor_tensor(
            W2Lf, AR[:, 0:8192], W2Hf, ALU.subtract))
        tok_tab = emit_tables(i)
        P.waits("scalar", tok_h, tok_scanbufs_free)
        tok_sq = P.s(lambda e: e.activation(sqb, hf, AF.Square))
        sq3 = A2[:, 0:8192].rearrange("p (g t) -> p g t", g=4)
        repl_toks = []
        st_toks = []
        for tc in range(4):
            b = tc
            bankw(b)
            P.wait("tensor", tok_h)
            for g in range(4):
                P.t(lambda e, g=g, tc=tc, b=b: e.matmul(
                    PX[b][0:1, 0:512], ONESK[:, :], h3[:, g, tc * 512:(tc + 1) * 512],
                    start=(g == 0), stop=(g == 3)))
            P.wait("tensor", tok_sq)
            for g in range(4):
                tok_st = P.t(lambda e, g=g, tc=tc, b=b: e.matmul(
                    PX[b][32:33, 0:512], ONESK[:, :], sq3[:, g, tc * 512:(tc + 1) * 512],
                    start=(g == 0), stop=(g == 3)))
            st_toks.append(tok_st)
        for tc in range(4):
            tok_coef = coef_rows(PX[tc], 512, D, st_toks[tc])
            guard[tc] = tok_coef
            tok_r = repl(tc, 4 + tc, 512, tok_coef)
            repl_toks.append(tok_r)
        P.waits("vector", tok_tab, *repl_toks)
        lcre = SM[:, SC["lc_re"] + i * 4: SC["lc_re"] + i * 4 + 4]
        lcim = SM[:, SC["lc_im"] + i * 4: SC["lc_im"] + i * 4 + 4]
        cb = tCB.rearrange("p (g r) -> p g r", r=129)
        sb = tSB.rearrange("p (g r) -> p g r", r=129)
        nsb = tNSB.rearrange("p (g r) -> p g r", r=129)
        rp = tRP.rearrange("p (g r) -> p g r", r=128)
        cq = tCQ.rearrange("p (g r) -> p g r", r=17)
        sq = tSQ.rearrange("p (g r) -> p g r", r=17)
        RHO = SC4[:, 4:8]
        CS1, SN1 = SC4[:, 8:12], SC4[:, 12:16]
        sl = lambda j: S16[:, j * 16:(j + 1) * 16]
        ER, EI, ETR, ETI, VR, VI, VSR, VSI, GR, GI, HR, HI, D0C, TMPA = (sl(j) for j in range(14))
        tok_gfree = tok_h
        tok_sqg = [None] * 4
        for g in range(4):
            r0, r1, r2, r3, r4, r5 = R[0], R[1], R[2], R[3], R[4], R[5]
            r0v = r0.rearrange("p (n q) -> p n q", q=Q)
            r1v = r1.rearrange("p (n q) -> p n q", q=Q)
            r2v = r2.rearrange("p (n q) -> p n q", q=Q)
            r3v = r3.rearrange("p (n q) -> p n q", q=Q)
            r4v = r4.rearrange("p (n q) -> p n q", q=Q)
            r5v = r5.rearrange("p (n q) -> p n q", q=Q)
            rho_b = RHO[:, g:g + 1].unsqueeze(1).broadcast_to([128, NCH, Q])
            P.small_mode = True
            P.v(lambda e, rho_b=rho_b, r5v=r5v: e.tensor_copy(r5v, rho_b))
            tok_r5 = P.v(lambda e, r5v=r5v: e.memset(r5v[:, :, 0:1], 0.0))
            P.small_mode = False
            for tc in range(4):
                hsl = h3[:, g, tc * 512:(tc + 1) * 512]
                P.v(lambda e, hsl=hsl, tc=tc, r0=r0: e.tensor_tensor(
                    r0[:, tc * 512:(tc + 1) * 512], hsl, PX[tc][:, :], ALU.mult))
                tk = P.v(lambda e, tc=tc, r0=r0: e.tensor_tensor(
                    r0[:, tc * 512:(tc + 1) * 512], r0[:, tc * 512:(tc + 1) * 512],
                    PX[4 + tc][:, :], ALU.add))
            P.wait("scalar", tk)
            tk = P.s(lambda e, g=g, r0=r0: e.activation(
                r0, r0, AF.Silu,
                bias=SM[:, SC["ln1_b"] + i * 4 + g: SC["ln1_b"] + i * 4 + g + 1],
                scale=SM[:, SC["ln1_w"] + i * 4 + g: SC["ln1_w"] + i * 4 + g + 1]))
            P.wait("vector", tk)
            cbb = cb[:, g, 0:Q].unsqueeze(1).broadcast_to([128, NCH, Q])
            nsbb = nsb[:, g, 0:Q].unsqueeze(1).broadcast_to([128, NCH, Q])
            sbb = sb[:, g, 0:Q].unsqueeze(1).broadcast_to([128, NCH, Q])
            P.v(lambda e, r1v=r1v, r0v=r0v, cbb=cbb: e.tensor_tensor(r1v, r0v, cbb, ALU.mult))
            P.v(lambda e, r2v=r2v, r0v=r0v, nsbb=nsbb: e.tensor_tensor(
                r2v, r0v, nsbb, ALU.mult))
            P.small_mode = True
            P.v(lambda e, r3=r3, r5=r5, r1=r1: e.tensor_tensor_scan(r3, r5, r1, 0.0, ALU.mult, ALU.add))
            P.v(lambda e, r4=r4, r5=r5, r2=r2: e.tensor_tensor_scan(r4, r5, r2, 0.0, ALU.mult, ALU.add))
            urE = r3v[:, :, Q - 1]
            uiE = r4v[:, :, Q - 1]
            cbq = cb[:, g, Q - 1:Q]
            sbq = sb[:, g, Q - 1:Q]
            P.v(lambda e, urE=urE, cbq=cbq: e.tensor_scalar(ER, urE, cbq, None, ALU.mult))
            P.v(lambda e, uiE=uiE, sbq=sbq: e.tensor_scalar(TMPA, uiE, sbq, None, ALU.mult))
            P.v(lambda e: e.tensor_tensor(ER, ER, TMPA, ALU.subtract))
            P.v(lambda e, urE=urE, sbq=sbq: e.tensor_scalar(EI, urE, sbq, None, ALU.mult))
            P.v(lambda e, uiE=uiE, cbq=cbq: e.tensor_scalar(TMPA, uiE, cbq, None, ALU.mult))
            P.v(lambda e: e.tensor_tensor(EI, EI, TMPA, ALU.add))
            cq1 = cq[:, g, 1:17]
            sq1 = sq[:, g, 1:17]
            P.v(lambda e, cq1=cq1: e.tensor_tensor(ETR, ER, cq1, ALU.mult))
            P.v(lambda e, sq1=sq1: e.tensor_tensor(TMPA, EI, sq1, ALU.mult))
            P.v(lambda e: e.tensor_tensor(ETR, ETR, TMPA, ALU.add))
            P.v(lambda e, cq1=cq1: e.tensor_tensor(ETI, EI, cq1, ALU.mult))
            P.v(lambda e, sq1=sq1: e.tensor_tensor(TMPA, ER, sq1, ALU.mult))
            P.v(lambda e: e.tensor_tensor(ETI, ETI, TMPA, ALU.subtract))
            rq = rp[:, g, Q - 1:Q]
            P.v(lambda e, g=g, rq=rq, lcre=lcre: e.scalar_tensor_tensor(
                ETR[:, 0:1], lcre[:, g:g + 1], rq, ETR[:, 0:1], ALU.mult, ALU.add))
            P.v(lambda e, g=g, rq=rq, lcim=lcim: e.scalar_tensor_tensor(
                ETI[:, 0:1], lcim[:, g:g + 1], rq, ETI[:, 0:1], ALU.mult, ALU.add))
            P.v(lambda e, rq=rq: e.tensor_copy(D0C, rq.broadcast_to([128, 16])))
            P.v(lambda e: e.memset(D0C[:, 0:1], 0.0))
            P.v(lambda e: e.tensor_tensor_scan(VR, D0C, ETR, 0.0, ALU.mult, ALU.add))
            P.v(lambda e: e.tensor_tensor_scan(VI, D0C, ETI, 0.0, ALU.mult, ALU.add))
            P.v(lambda e: e.tensor_copy(VSR[:, 1:16], VR[:, 0:15]))
            P.v(lambda e, g=g, lcre=lcre: e.tensor_copy(VSR[:, 0:1], lcre[:, g:g + 1]))
            P.v(lambda e: e.tensor_copy(VSI[:, 1:16], VI[:, 0:15]))
            P.v(lambda e, g=g, lcim=lcim: e.tensor_copy(VSI[:, 0:1], lcim[:, g:g + 1]))
            cq0 = cq[:, g, 0:16]
            sq0 = sq[:, g, 0:16]
            P.v(lambda e, cq0=cq0: e.tensor_tensor(GR, VSR, cq0, ALU.mult))
            P.v(lambda e, sq0=sq0: e.tensor_tensor(TMPA, VSI, sq0, ALU.mult))
            P.v(lambda e: e.tensor_tensor(GR, GR, TMPA, ALU.subtract))
            P.v(lambda e, sq0=sq0: e.tensor_tensor(GI, VSR, sq0, ALU.mult))
            P.v(lambda e, cq0=cq0: e.tensor_tensor(TMPA, VSI, cq0, ALU.mult))
            P.v(lambda e: e.tensor_tensor(GI, GI, TMPA, ALU.add))
            cs1g = CS1[:, g:g + 1]
            sn1g = SN1[:, g:g + 1]
            P.v(lambda e, cs1g=cs1g: e.tensor_scalar(HR, GR, cs1g, None, ALU.mult))
            P.v(lambda e, sn1g=sn1g: e.tensor_scalar(TMPA, GI, sn1g, None, ALU.mult))
            P.v(lambda e: e.tensor_tensor(HR, HR, TMPA, ALU.subtract))
            P.v(lambda e, sn1g=sn1g: e.tensor_scalar(HI, GR, sn1g, None, ALU.mult))
            P.v(lambda e, cs1g=cs1g: e.tensor_scalar(TMPA, GI, cs1g, None, ALU.mult))
            P.v(lambda e: e.tensor_tensor(HI, HI, TMPA, ALU.add))
            P.small_mode = False
            rpb = rp[:, g, :].unsqueeze(1).broadcast_to([128, NCH, Q])
            hrb = HR.unsqueeze(2).broadcast_to([128, NCH, Q])
            hib = HI.unsqueeze(2).broadcast_to([128, NCH, Q])
            P.v(lambda e, r0v=r0v, rpb=rpb, hrb=hrb: e.tensor_tensor(r0v, rpb, hrb, ALU.mult))
            P.v(lambda e, r1=r1, r3=r3, r0=r0: e.tensor_tensor(r1, r3, r0, ALU.add))
            P.v(lambda e, r0v=r0v, rpb=rpb, hib=hib: e.tensor_tensor(r0v, rpb, hib, ALU.mult))
            P.v(lambda e, r2=r2, r4=r4, r0=r0: e.tensor_tensor(r2, r4, r0, ALU.add))
            P.v(lambda e, r0v=r0v, r1v=r1v, cbb=cbb: e.tensor_tensor(r0v, r1v, cbb, ALU.mult))
            P.v(lambda e, r1v=r1v, r2v=r2v, sbb=sbb: e.tensor_tensor(r1v, r2v, sbb, ALU.mult))
            P.v(lambda e, r0=r0, r1=r1: e.tensor_tensor(r0, r0, r1, ALU.subtract))
            tok_hg = P.v(lambda e, g=g, r0=r0: e.scalar_tensor_tensor(
                h3[:, g, :], r0, SM[:, SC["sc_lin"] + i * 4 + g: SC["sc_lin"] + i * 4 + g + 1],
                h3[:, g, :], ALU.mult, ALU.add))
            tok_gfree = tok_hg
            P.waits("scalar", tok_hg, tok_w2l)
            tok_sqg[g] = P.s(lambda e, g=g: e.activation(
                AR[:, g * 2048:(g + 1) * 2048], h3[:, g, :], AF.Square))
            if stop == "dbgl0" and i == 0 and g == 0:
                P.wait("gpsimd", tok_hg)
                for off, src in ((0, tCB), (516, tSB), (1032, tRP), (1544, SC4),
                                 (1576, S16), (2048, R[0]), (4096, R[1]),
                                 (6144, R[2]), (8192, R[3]), (10240, R[4]),
                                 (12288, R[5]), (14336, h3[:, 0, :])):
                    n = src.shape[-1] if src.ndim == 2 else src.free_size()
                    tg = P.dma("gpsimd", d_tap[:, off:off + n], src)
                P.wait("gpsimd", tg)
                return None
        for b in range(8):
            guard[b] = tok_hg
        tok_h = tok_hg
        sq4 = AR[:, 0:8192].rearrange("p (g t) -> p g t", g=4)
        st_toks = []
        for tc in range(4):
            b = tc
            bankw(b)
            P.wait("tensor", tok_h)
            for g in range(4):
                P.t(lambda e, g=g, tc=tc, b=b: e.matmul(
                    PX[b][0:1, 0:512], ONESK[:, :], h3[:, g, tc * 512:(tc + 1) * 512],
                    start=(g == 0), stop=(g == 3)))
            for g in range(4):
                P.wait("tensor", tok_sqg[g])
                tok_st = P.t(lambda e, g=g, tc=tc, b=b: e.matmul(
                    PX[b][32:33, 0:512], ONESK[:, :], sq4[:, g, tc * 512:(tc + 1) * 512],
                    start=(g == 0), stop=(g == 3)))
            st_toks.append(tok_st)
        rowtoks = [None] * 4
        repl0 = [None] * 4
        for tc in range(4):
            tok_coef = coef_rows(PX[tc], 512, D, st_toks[tc])
            guard[tc] = tok_coef
            if tc < 2:
                repl0[tc] = repl(4 if tc == 0 else 6, 5 if tc == 0 else 7,
                                 512, tok_coef)
            else:
                off = (tc - 2) * 512
                P.v(lambda e, off=off: e.tensor_copy(
                    RROWB[0:1, off:off + 512], RROW[0:1, :]))
                rowtoks[tc] = P.v(lambda e, off=off: e.tensor_copy(
                    CROWB[0:1, off:off + 512], MUROW[0:1, :]))
        tok_zh = [None] * 4
        tok_zl = [None] * 4

        def prep_z(tc):
            br, bc = (4, 5) if tc % 2 == 0 else (6, 7)
            if tc < 2:
                tok_r = repl0[tc]
            else:
                off = (tc - 2) * 512
                tok_r = repl(br, bc, 512, rowtoks[tc],
                             src_r=RROWB[0:1, off:off + 512],
                             src_c=CROWB[0:1, off:off + 512])
            hs = h3[:, :, tc * 512:(tc + 1) * 512]
            rb = PX[br][:, :].unsqueeze(1).broadcast_to([128, 4, 512])
            cbk = PX[bc][:, :].unsqueeze(1).broadcast_to([128, 4, 512])
            P.waits("vector", tok_r, tok_zh[tc - 1] if tc else None)
            P.v(lambda e, hs=hs, rb=rb: e.tensor_tensor(ztmp3, hs, rb, ALU.mult))
            P.v(lambda e, cbk=cbk: e.tensor_tensor(ztmp3, ztmp3, cbk, ALU.add))
            for g in range(4):
                tok_zt = P.v(lambda e, g=g: e.tensor_scalar(
                    ztmp3[:, g, :], ztmp3[:, g, :],
                    SM[:, SC["ln2_w"] + i * 4 + g: SC["ln2_w"] + i * 4 + g + 1],
                    SM[:, SC["ln2_b"] + i * 4 + g: SC["ln2_b"] + i * 4 + g + 1],
                    ALU.mult, ALU.add))
            P.waits("scalar", tok_zt)
            tok_zh[tc] = P.s(lambda e, tc=tc: e.activation(
                ZH[:, :, tc * 512:(tc + 1) * 512], ztmp3, AF.Copy))
            P.wait("vector", tok_zh[tc])
            tok_zl[tc] = P.v(lambda e, tc=tc: e.tensor_tensor(
                ZL[:, :, tc * 512:(tc + 1) * 512], ztmp3,
                ZH[:, :, tc * 512:(tc + 1) * 512], ALU.subtract))
            guard[br] = tok_zt
            guard[bc] = tok_zt

        prep_z(0)
        tok_w1d = None
        tok_w2d = None
        mtmp_free = [None, None]
        for tc in range(4):
            if tc + 1 < 4:
                prep_z(tc + 1)
            P.waits("tensor", tok_w1h, tok_w1l, tok_w2h, tok_w2l,
                    tok_zh[tc], tok_zl[tc])
            tok_mh = tok_ml = None
            for fg in range(FG):
                b = fg % 2
                bankw(b)
                P.wait("tensor", tok_mid_free)
                for kg in range(4):
                    for si, (ws, zs2) in enumerate(((W1H, ZH), (W1H, ZL), (W1L, ZH))):
                        tok_w1d = P.t(lambda e, fg=fg, kg=kg, tc=tc, b=b, ws=ws, zs2=zs2,
                                      st=(kg == 0 and si == 0), sp=(kg == 3 and si == 2):
                                      e.matmul(PX[b][:, :], ws[:, kg, fg * 128:(fg + 1) * 128],
                                               zs2[:, kg, tc * 512:(tc + 1) * 512],
                                               start=st, stop=sp))
                mt = mtmp[fg % 2]
                P.waits("scalar", tok_w1d, tok_mid_free, mtmp_free[fg % 2])
                tok_silu = P.s(lambda e, fg=fg, b=b, mt=mt: e.activation(
                    mt, PX[b][:, :], AF.Silu,
                    bias=SM[:, SC["b1"] + i * 16 + fg: SC["b1"] + i * 16 + fg + 1]))
                guard[b] = tok_silu
                tok_mh = P.s(lambda e, fg=fg, mt=mt: e.activation(
                    MIDH[:, fg, :], mt, AF.Copy))
                P.wait("vector", tok_mh)
                tok_ml = P.v(lambda e, fg=fg, mt=mt: e.tensor_tensor(
                    MIDL[:, fg, :], mt, MIDH[:, fg, :], ALU.subtract))
                mtmp_free[fg % 2] = tok_ml
            P.waits("tensor", tok_mh, tok_ml)
            for dg in range(4):
                b = 2 + dg % 2
                bankw(b)
                for kg in range(FG):
                    for si, (ws, ms) in enumerate(((W2H, MIDH), (W2H, MIDL), (W2L, MIDH))):
                        tok_w2d = P.t(lambda e, dg=dg, kg=kg, b=b, ws=ws, ms=ms,
                                      st=(kg == 0 and si == 0), sp=(kg == FG - 1 and si == 2):
                                      e.matmul(PX[b][:, :], ws[:, kg, dg * 128:(dg + 1) * 128],
                                               ms[:, kg, :], start=st, stop=sp))
                P.wait("vector", tok_w2d)
                tok_h = P.v(lambda e, dg=dg, tc=tc, b=b: e.scalar_tensor_tensor(
                    h3[:, dg, tc * 512:(tc + 1) * 512], PX[b][:, :],
                    SM[:, SC["b2"] + i * 4 + dg: SC["b2"] + i * 4 + dg + 1],
                    h3[:, dg, tc * 512:(tc + 1) * 512], ALU.add, ALU.add))
                guard[b] = tok_h
            tok_mid_free = tok_w2d
        tok_w1_used = tok_w1d
        tok_w2_used = tok_w2d
        tok_stage_free = tok_w2d
        tok_scanbufs_free = tok_w1d

    for i in range(DEPTH):
        rr_ = inner_layer(i)
        if stop == "dbgl0" and i == 0:
            return nc, P
        if stop == f"l{i}":
            P.wait("gpsimd", tok_h)
            tg = P.dma("gpsimd", d_tap[:, 0:8192], hf)
            P.wait("gpsimd", tg)
            return nc, P

    tok_woh = tok_wol = None
    for hh in range(2):
        P.waits("sync", tok_w2_used, tok_woh, tok_wol)
        tok_wst = P.dma("sync", wostg, d_woutT[:, 2 * hh:2 * hh + 2, :])
        P.waits("scalar", tok_wst, tok_w2_used)
        tok_woh = P.s(lambda e, hh=hh: e.activation(
            WOH[:, 2 * hh:2 * hh + 2, :], wostg, AF.Copy))
        P.waits("gpsimd", tok_woh, tok_w1_used)
        tok_wol = P.g(lambda e, hh=hh: e.tensor_tensor(
            WOL[:, 2 * hh:2 * hh + 2, :], wostg, WOH[:, 2 * hh:2 * hh + 2, :],
            ALU.subtract))
    P.waits("scalar", tok_h, tok_scanbufs_free)
    tok_sq = P.s(lambda e: e.activation(sqb, hf, AF.Square))
    sq3 = A2[:, 0:8192].rearrange("p (g t) -> p g t", g=4)
    repl_toks = []
    st_toks = []
    for tc in range(4):
        b = tc
        bankw(b)
        P.wait("tensor", tok_h)
        for g in range(4):
            P.t(lambda e, g=g, tc=tc, b=b: e.matmul(
                PX[b][0:1, 0:512], ONESK[:, :], h3[:, g, tc * 512:(tc + 1) * 512],
                start=(g == 0), stop=(g == 3)))
        P.wait("tensor", tok_sq)
        for g in range(4):
            tok_st = P.t(lambda e, g=g, tc=tc, b=b: e.matmul(
                PX[b][32:33, 0:512], ONESK[:, :], sq3[:, g, tc * 512:(tc + 1) * 512],
                start=(g == 0), stop=(g == 3)))
        st_toks.append(tok_st)
    for tc in range(4):
        tok_coef = coef_rows(PX[tc], 512, D, st_toks[tc])
        guard[tc] = tok_coef
        repl_toks.append(repl(tc, 4 + tc, 512, tok_coef))
    P.waits("vector", *repl_toks)
    P.wait("vector", tok_st)
    for tc in range(4):
        hns = hn3[:, :, tc * 512:(tc + 1) * 512]
        hs = h3[:, :, tc * 512:(tc + 1) * 512]
        rb = PX[tc][:, :].unsqueeze(1).broadcast_to([128, 4, 512])
        cbk = PX[4 + tc][:, :].unsqueeze(1).broadcast_to([128, 4, 512])
        P.v(lambda e, hns=hns, hs=hs, rb=rb: e.tensor_tensor(hns, hs, rb, ALU.mult))
        P.v(lambda e, hns=hns, cbk=cbk: e.tensor_tensor(hns, hns, cbk, ALU.add))
        for g in range(4):
            tok_hn = P.v(lambda e, g=g, tc=tc: e.tensor_scalar(
                hn3[:, g, tc * 512:(tc + 1) * 512], hn3[:, g, tc * 512:(tc + 1) * 512],
                SM[:, SC["lnl_w"] + g: SC["lnl_w"] + g + 1],
                SM[:, SC["lnl_b"] + g: SC["lnl_b"] + g + 1], ALU.mult, ALU.add))
    for b in range(8):
        guard[b] = tok_hn
    sel0 = SM[:, SC["sel"]:SC["sel"] + 1]
    sel1 = SM[:, SC["sel"] + 1:SC["sel"] + 2]
    P.v(lambda e: e.tensor_scalar(hno3[:, :, :], hn3[:, :, 0:TOK], sel0, None, ALU.mult))
    tok_hno = P.v(lambda e: e.scalar_tensor_tensor(
        hno3[:, :, :], hn3[:, :, TOK:L], sel1, hno3[:, :, :], ALU.mult, ALU.add))

    P.waits("scalar", tok_hno, tok_wol)
    tok_hnh = P.s(lambda e: e.activation(HNHf, A2[:, 8192:12288], AF.Copy))
    P.wait("vector", tok_hnh)
    tok_hnl = P.v(lambda e: e.tensor_tensor(
        HNLf, A2[:, 8192:12288], HNHf, ALU.subtract))

    P.waits("tensor", tok_woh, tok_wol, tok_hnh, tok_hnl)
    tok_adma = [None, None]
    tok_asq = [None, None]
    tok_astat_sl = [None, None]
    tok_astat = None
    for og in range(VG):
        sl = og % 2
        for tc2 in range(2):
            b = sl * 2 + tc2
            bankw(b)
            for kg in range(4):
                for si, (ws, hs2) in enumerate(((WOH, HNH), (WOH, HNL), (WOL, HNH))):
                    tok_u = P.t(lambda e, og=og, kg=kg, tc2=tc2, b=b, ws=ws, hs2=hs2,
                                st=(kg == 0 and si == 0), sp=(kg == 3 and si == 2):
                                e.matmul(PX[b][:, :], ws[:, kg, og * 128:(og + 1) * 128],
                                         hs2[:, kg, tc2 * 512:(tc2 + 1) * 512],
                                         start=st, stop=sp))
            P.waits("vector", tok_u, tok_adma[sl], tok_asq[sl], tok_astat_sl[sl])
            tok_ev = P.v(lambda e, og=og, tc2=tc2, b=b, sl=sl: e.tensor_scalar(
                atile[sl][:, tc2 * 512:(tc2 + 1) * 512], PX[b][:, :],
                SM[:, SC["bout"] + og: SC["bout"] + og + 1], None, ALU.add))
            guard[b] = tok_ev
        P.wait("gpsimd", tok_ev)
        tok_adma[sl] = P.dma("gpsimd", d_ascr[:, og, :], atile[sl])
        P.waits("scalar", tok_ev, tok_astat_sl[sl])
        tok_asq[sl] = P.s(lambda e, sl=sl: e.activation(asq[sl], atile[sl], AF.Square))
        for tc2 in range(2):
            b = 4 + tc2
            if og == 0:
                bankw(b)
            P.wait("tensor", tok_ev)
            P.t(lambda e, og=og, tc2=tc2, b=b, sl=sl: e.matmul(
                PX[b][0:1, 0:512], ONESK[:, :], atile[sl][:, tc2 * 512:(tc2 + 1) * 512],
                start=(og == 0), stop=(og == VG - 1)))
            P.wait("tensor", tok_asq[sl])
            tok_astat = P.t(lambda e, og=og, tc2=tc2, b=b, sl=sl: e.matmul(
                PX[b][32:33, 0:512], ONESK[:, :], asq[sl][:, tc2 * 512:(tc2 + 1) * 512],
                start=(og == 0), stop=(og == VG - 1)))
        tok_astat_sl[sl] = tok_astat
    for tc2 in range(2):
        tok_coef = coef_rows(PX[4 + tc2], 512, V, tok_astat)
        guard[4 + tc2] = tok_coef
        P.v(lambda e, tc2=tc2: e.tensor_copy(RROWB[0:1, tc2 * 512:(tc2 + 1) * 512], RROW[0:1, :]))
        tok_arows = P.v(lambda e, tc2=tc2: e.tensor_copy(CROWB[0:1, tc2 * 512:(tc2 + 1) * 512], MUROW[0:1, :]))

    P.waits("sync", tok_u)
    tok_ats = P.dma("sync", atstg, d_aT)
    P.waits("scalar", tok_ats, tok_hn, st_toks[-1])
    tok_ath = P.s(lambda e: e.activation(ATHf, atstg, AF.Copy))
    P.waits("vector", tok_ath, tok_hn)
    tok_atl = P.v(lambda e: e.tensor_tensor(ATLf, atstg, ATHf, ALU.subtract))

    m8 = RT[:, 0:8]
    lg = RT[:, 8:16]
    qe = RT[:, 16:24]
    msk = RT[:, 24:32]
    den = RT[:, 32:33]
    nm1 = RT[:, 33:34]
    r2 = RT[:, 34:35]

    tok_x1dma = None
    tok_sqstat = None
    tok_vmm = tok_u
    tok_aq_free = None
    tok_vc = tok_vs = None
    tok_wb_mm = [None, None]
    tok_wbh = [None, None]
    tok_wbl = [None, None]
    tok_bth = [None, None]
    tok_btl = [None, None]
    tok_fdma = None
    tok_fsq = None
    tok_fstat = None
    tok_h2h_prev = None
    frows_tok = [None, None]
    tok_base = None
    tok_wbstg_free = [tok_hnh, tok_hnl]
    for th in range(2):
        vsplit = []
        for sub in range(2):
            qc = th * 2 + sub
            P.waits("sync", tok_adma[0], tok_adma[1], tok_astat, tok_x1dma,
                    tok_aq_free, tok_vc, tok_vs)
            tok_aq = P.dma("sync", aq3, d_ascr[:, :, qc * 256:(qc + 1) * 256])
            P.waits("sync", tok_ath, tok_atl, tok_sqstat, tok_wbh[1], tok_wbl[1])
            tok_xq = P.dma("sync", xq3, d_xown[:, :, qc * 256:(qc + 1) * 256])
            tok_r = repl(6, 7, 256, tok_arows,
                         src_r=RROWB[0:1, qc * 256:(qc + 1) * 256],
                         src_c=CROWB[0:1, qc * 256:(qc + 1) * 256])
            P.waits("vector", tok_aq, tok_xq, tok_r)
            rb = PX[6][:, 0:256].unsqueeze(1).broadcast_to([128, VG, 256])
            cbk = PX[7][:, 0:256].unsqueeze(1).broadcast_to([128, VG, 256])
            P.v(lambda e, rb=rb: e.tensor_tensor(aq3, aq3, rb, ALU.mult))
            P.v(lambda e, cbk=cbk: e.tensor_tensor(aq3, aq3, cbk, ALU.add))
            tok_x1 = P.v(lambda e: e.tensor_tensor(aq3, aq3, xq3, ALU.add))
            guard[6] = tok_x1
            guard[7] = tok_x1
            P.wait("gpsimd", tok_x1)
            tok_x1dma = P.dma("gpsimd", d_x1scr[:, :, qc * 256:(qc + 1) * 256], aq3)
            P.waits("scalar", tok_x1, tok_sqstat)
            tok_sq = P.s(lambda e: e.activation(sqv, A2[:, 0:8192], AF.Square))
            bankw(4)
            P.wait("tensor", tok_x1)
            for vg in range(VG):
                P.t(lambda e, vg=vg: e.matmul(
                    PX[4][0:1, 0:256], ONESK[:, :], aq3[:, vg, :],
                    start=(vg == 0), stop=(vg == VG - 1)))
            P.wait("tensor", tok_sq)
            for vg in range(VG):
                tok_st2 = P.t(lambda e, vg=vg: e.matmul(
                    PX[4][32:33, 0:256], ONESK[:, :], sqv3[:, vg, :],
                    start=(vg == 0), stop=(vg == VG - 1)))
            tok_sqstat = tok_st2
            tok_coef = coef_rows(PX[4], 256, V, tok_st2)
            guard[4] = tok_coef
            tok_r2 = repl(5, 0, 256, tok_coef)
            P.wait("vector", tok_r2)
            rb2 = PX[5][:, 0:256].unsqueeze(1).broadcast_to([128, VG, 256])
            cb2 = PX[0][:, 0:256].unsqueeze(1).broadcast_to([128, VG, 256])
            P.v(lambda e, rb2=rb2: e.tensor_tensor(aq3, aq3, rb2, ALU.mult))
            tok_v = P.v(lambda e, cb2=cb2: e.tensor_tensor(aq3, aq3, cb2, ALU.add))
            guard[5] = tok_v
            guard[0] = tok_v
            P.waits("scalar", tok_v, tok_vmm)
            tok_vc = P.s(lambda e, sub=sub: e.activation(
                VH[:, :, sub * 256:(sub + 1) * 256], aq3, AF.Copy))
            P.waits("vector", tok_vc, tok_vmm)
            tok_vs = P.v(lambda e, sub=sub: e.tensor_tensor(
                VL[:, :, sub * 256:(sub + 1) * 256], aq3,
                VH[:, :, sub * 256:(sub + 1) * 256], ALU.subtract))
            vsplit.extend([tok_vc, tok_vs])
        def stage_wb(og):
            sl = og % 2
            P.waits("sync", tok_wbh[sl], tok_wbl[sl],
                    tok_bth[sl], tok_btl[sl], tok_sqstat, *tok_wbstg_free)
            tok_sg = P.dma("sync", wbstg[sl], d_wbT[og])
            tok_btg = P.dma("sync", btstg[sl], d_bT[:, og * 128:(og + 1) * 128])
            P.waits("scalar", tok_sg, tok_wb_mm[sl], *vsplit)
            tok_wbh[sl] = P.s(lambda e, sl=sl: e.activation(WBHf[sl], wbstg[sl], AF.Copy))
            P.waits("vector", tok_wbh[sl], tok_wb_mm[sl], *vsplit)
            tok_wbl[sl] = P.v(lambda e, sl=sl: e.tensor_tensor(
                WBLf[sl], wbstg[sl], WBHf[sl], ALU.subtract))
            P.waits("scalar", tok_btg, tok_wb_mm[sl])
            tok_bth[sl] = P.s(lambda e, sl=sl: e.activation(BTH[sl], btstg[sl], AF.Copy))
            P.waits("vector", tok_bth[sl], tok_wb_mm[sl])
            tok_btl[sl] = P.v(lambda e, sl=sl: e.tensor_tensor(
                BTL[sl], btstg[sl], BTH[sl], ALU.subtract))

        stage_wb(0)
        tok_wbstg_free = []
        stage_wb(1)
        tok_h2 = None
        for tt in range(4):
            bankw(3)
            P.waits("tensor", tok_ath, tok_atl, *vsplit)
            for vg in range(VG):
                for si, (vv, aa) in enumerate(((VH, ATH), (VL, ATH), (VH, ATL))):
                    tok_ha = P.t(lambda e, vg=vg, tt=tt, vv=vv, aa=aa,
                                 st=(vg == 0 and si == 0),
                                 sp=(vg == VG - 1 and si == 2):
                                 e.matmul(PX[3][:, 0:136],
                                          vv[:, vg, tt * 128:(tt + 1) * 128],
                                          aa[:, vg, :], start=st, stop=sp))
            P.wait("vector", tok_ha)
            P.small_mode = True
            P.v(lambda e: e.tensor_copy(lg, PX[3][:, 128:136]))
            tok_lg = P.v(lambda e: e.max(m8, lg))
            tok_nm1 = P.v(lambda e: e.tensor_scalar(nm1, m8[:, 0:1], -1.0, None, ALU.mult))
            P.wait("scalar", tok_nm1)
            tok_q = P.s(lambda e: e.activation(qe, lg, AF.Exp, bias=nm1))
            P.wait("vector", tok_q)
            P.v(lambda e: e.tensor_single_scalar(msk, lg, m8[:, 1:2], ALU.is_ge))
            P.v(lambda e: e.tensor_tensor(qe, qe, msk, ALU.mult))
            P.v(lambda e: e.reduce_sum(den, qe, axis=AXX))
            P.v(lambda e: e.reciprocal(r2, den))
            P.v(lambda e: e.tensor_scalar(r2, r2, 2.0, None, ALU.mult))
            P.v(lambda e: e.tensor_scalar(qe, qe, r2, None, ALU.mult))
            cmb = qe.rearrange("p (a b) -> p a b", b=1).broadcast_to([128, 8, 16])
            tok_cp = P.v(lambda e: e.tensor_copy(H2W, PX[3][:, 0:128]))
            guard[3] = tok_cp
            tok_h2t = P.v(lambda e, cmb=cmb: e.tensor_tensor(
                H2T.rearrange("p (a b) -> p a b", b=16),
                H2W.rearrange("p (a b) -> p a b", b=16), cmb, ALU.mult))
            P.small_mode = False
            bankw(6)
            P.waits("tensor", tok_h2t, tok_id)
            tok_tr = P.t(lambda e: e.transpose(PX[6][:, 0:128], H2T, IDT[:, :]))
            P.waits("vector", tok_tr, tok_h2h_prev)
            tok_h2 = P.v(lambda e, tt=tt: e.tensor_copy(
                h2b[:, tt * 128:(tt + 1) * 128], PX[6][:, 0:128]))
            guard[6] = tok_h2
        P.waits("scalar", tok_h2, tok_wb_mm[0], tok_wb_mm[1])
        tok_h2h = P.s(lambda e: e.activation(H2Hf, h2b, AF.Copy))
        tok_h2h_prev = tok_h2h
        P.waits("vector", tok_h2h, tok_wb_mm[0], tok_wb_mm[1])
        tok_h2l = P.v(lambda e: e.tensor_tensor(H2Lf, h2b, H2Hf, ALU.subtract))
        for og in range(VG):
            sl = og % 2
            b = sl
            bankw(b)
            P.waits("tensor", tok_wbh[sl], tok_wbl[sl], tok_bth[sl], tok_btl[sl],
                    tok_h2h, tok_h2l)
            for kg in range(VG):
                for si, (wv, vv) in enumerate(((WBH[sl], VH), (WBH[sl], VL), (WBL[sl], VH))):
                    P.t(lambda e, kg=kg, wv=wv, vv=vv, b=b,
                        st=(kg == 0 and si == 0):
                        e.matmul(PX[b][:, :], wv[:, kg, :], vv[:, kg, :],
                                 start=st, stop=False))
            for si, (bb, hh2) in enumerate(((BTH[sl], H2Hf), (BTH[sl], H2Lf), (BTL[sl], H2Hf))):
                tok_base = P.t(lambda e, bb=bb, hh2=hh2, b=b, sp=(si == 2):
                               e.matmul(PX[b][:, :], bb, hh2, start=False, stop=sp))
            tok_wb_mm[sl] = tok_base
            P.waits("vector", tok_base, tok_fdma, tok_fsq, tok_fstat)
            tok_ev = P.v(lambda e, og=og, b=b: e.tensor_scalar(
                ftile, PX[b][:, :], SM[:, SC["b_base"] + og: SC["b_base"] + og + 1],
                None, ALU.add))
            guard[b] = tok_ev
            P.wait("gpsimd", tok_ev)
            tok_fdma = P.dma("gpsimd", d_fscr[:, og, th * 512:(th + 1) * 512], ftile)
            P.waits("scalar", tok_ev, tok_fstat)
            tok_fsq = P.s(lambda e: e.activation(fsq, ftile, AF.Square))
            if og == 0:
                bankw(4)
            P.wait("tensor", tok_ev)
            P.t(lambda e, og=og: e.matmul(PX[4][0:1, 0:512], ONESK[:, :], ftile,
                                          start=(og == 0), stop=(og == VG - 1)))
            P.wait("tensor", tok_fsq)
            tok_fstat = P.t(lambda e, og=og: e.matmul(
                PX[4][32:33, 0:512], ONESK[:, :], fsq,
                start=(og == 0), stop=(og == VG - 1)))
            if og + 2 < VG:
                stage_wb(og + 2)
        tok_coef = coef_rows(PX[4], 512, V, tok_fstat)
        guard[4] = tok_coef
        P.v(lambda e, th=th: e.tensor_copy(RROWB[0:1, th * 512:(th + 1) * 512], RROW[0:1, :]))
        frows_tok[th] = P.v(lambda e, th=th: e.tensor_copy(
            CROWB[0:1, th * 512:(th + 1) * 512], MUROW[0:1, :]))
        tok_vmm = tok_base
        tok_aq_free = tok_base

    fchp = [AR[:, 0:4096].rearrange("p (v t) -> p v t", t=128),
            AR[:, 4096:8192].rearrange("p (v t) -> p v t", t=128)]
    xchp = [AR[:, 8192:12288].rearrange("p (v t) -> p v t", t=128),
            AR[:, 12288:16384].rearrange("p (v t) -> p v t", t=128)]
    p6bk = [(6, 7), (2, 3)]
    tok_outd = None
    tok_os = [tok_base, tok_base]
    tok_od = [None, None]
    for ck in range(8):
        th2 = ck // 4
        sl = ck % 2
        fc3, xc3 = fchp[sl], xchp[sl]
        br, bc = p6bk[sl]
        P.waits("sync", tok_os[sl], tok_od[sl], tok_fdma, tok_x1dma)
        tok_fc = P.dma("sync", fc3, d_fscr[:, :, ck * 128:(ck + 1) * 128])
        tok_xc = P.dma("sync", xc3, d_x1scr[:, :, ck * 128:(ck + 1) * 128])
        tok_r = repl(br, bc, 128, frows_tok[th2],
                     src_r=RROWB[0:1, ck * 128:(ck + 1) * 128],
                     src_c=CROWB[0:1, ck * 128:(ck + 1) * 128])
        P.waits("vector", tok_fc, tok_xc, tok_r)
        rb = PX[br][:, 0:128].unsqueeze(1).broadcast_to([128, VG, 128])
        cbk = PX[bc][:, 0:128].unsqueeze(1).broadcast_to([128, VG, 128])
        P.v(lambda e, fc3=fc3, rb=rb: e.tensor_tensor(fc3, fc3, rb, ALU.mult))
        P.v(lambda e, fc3=fc3, cbk=cbk: e.tensor_tensor(fc3, fc3, cbk, ALU.add))
        tok_o = P.v(lambda e, fc3=fc3, xc3=xc3: e.tensor_tensor(fc3, fc3, xc3, ALU.add))
        guard[br] = tok_o
        guard[bc] = tok_o
        tok_os[sl] = tok_o
        P.wait("gpsimd", tok_o)
        tok_od[sl] = tok_outd = P.dma("gpsimd", d_out[:, :, ck * 128:(ck + 1) * 128], fc3)
    P.wait("gpsimd", tok_outd)
    return nc, P

    tok_vq = None
    tok_x1dma = None
    vstat_rows = []
    for qc in range(4):
        P.waits("sync", tok_adma[0], tok_adma[1], tok_astat, tok_vq, tok_x1dma)
        tok_aq = P.dma("sync", aq3, d_ascr[:, :, qc * 256:(qc + 1) * 256])
        vslot = AR[:, qc * 8192:(qc + 1) * 8192]
        vslot3 = vslot.rearrange("p (v t) -> p v t", t=256)
        P.wait("sync", tok_u)
        tok_xq = P.dma("sync", vslot3, d_xown[:, :, qc * 256:(qc + 1) * 256])
        tok_r = repl(6, 7, 256, tok_arows,
                     src_r=RROWB[0:1, qc * 256:(qc + 1) * 256],
                     src_c=CROWB[0:1, qc * 256:(qc + 1) * 256])
        P.waits("vector", tok_aq, tok_xq, tok_r)
        rb = PX[6][:, 0:256].unsqueeze(1).broadcast_to([128, VG, 256])
        cbk = PX[7][:, 0:256].unsqueeze(1).broadcast_to([128, VG, 256])
        P.v(lambda e, rb=rb: e.tensor_tensor(aq3, aq3, rb, ALU.mult))
        P.v(lambda e, cbk=cbk: e.tensor_tensor(aq3, aq3, cbk, ALU.add))
        tok_x1 = P.v(lambda e, vslot3=vslot3: e.tensor_tensor(aq3, aq3, vslot3, ALU.add))
        guard[6] = tok_x1
        guard[7] = tok_x1
        P.wait("gpsimd", tok_x1)
        tok_x1dma = P.dma("gpsimd", d_x1scr[:, :, qc * 256:(qc + 1) * 256], aq3)
        P.wait("scalar", tok_x1)
        tok_sq = P.s(lambda e, vslot=vslot: e.activation(vslot, A2[:, 0:8192], AF.Square))
        b = 4
        bankw(b)
        P.wait("tensor", tok_x1)
        for vg in range(VG):
            P.t(lambda e, vg=vg: e.matmul(PX[4][0:1, 0:256], ONESK[:, :], aq3[:, vg, :],
                                          start=(vg == 0), stop=(vg == VG - 1)))
        P.wait("tensor", tok_sq)
        for vg in range(VG):
            tok_st = P.t(lambda e, vg=vg, vslot3=vslot3: e.matmul(
                PX[4][32:33, 0:256], ONESK[:, :], vslot3[:, vg, :],
                start=(vg == 0), stop=(vg == VG - 1)))
        tok_coef = coef_rows(PX[4], 256, V, tok_st)
        guard[4] = tok_coef
        tok_r = repl(5, 0, 256, tok_coef)
        P.wait("vector", tok_r)
        rb = PX[5][:, 0:256].unsqueeze(1).broadcast_to([128, VG, 256])
        cbk = PX[0][:, 0:256].unsqueeze(1).broadcast_to([128, VG, 256])
        P.v(lambda e, vslot3=vslot3, rb=rb: e.tensor_tensor(vslot3, aq3, rb, ALU.mult))
        tok_vq = P.v(lambda e, vslot3=vslot3, cbk=cbk: e.tensor_tensor(vslot3, vslot3, cbk, ALU.add))
        guard[5] = tok_vq
        guard[0] = tok_vq
        vstat_rows.append(tok_vq)

    if stop == "x1v":
        P.wait("gpsimd", tok_vq)
        tg = P.dma("gpsimd", d_tap[:, :], AR[:, :])
        P.wait("gpsimd", tg)
        return nc, P

    P.waits("sync", tok_vq)
    tok_at = P.dma("sync", aTS, d_aT)
    m8 = RT[:, 0:8]
    lg = RT[:, 8:16]
    qe = RT[:, 16:24]
    msk = RT[:, 24:32]
    den = RT[:, 32:33]
    nm1 = RT[:, 33:34]
    r2 = RT[:, 34:35]
    tok_h2 = None
    for tt in range(8):
        qc, th = tt // 2, tt % 2
        bankw(3)
        P.waits("tensor", tok_at, vstat_rows[qc])
        for vg in range(VG):
            lhs = AR4[:, qc, vg, th * 128:(th + 1) * 128]
            tok_ha = P.t(lambda e, vg=vg, lhs=lhs: e.matmul(
                PX[3][:, 0:136], lhs, aTS[:, vg, :],
                start=(vg == 0), stop=(vg == VG - 1)))
        tok_rt = tok_ha
        if stop == "p5a" and tt == 0:
            P.wait("gpsimd", tok_rt)
            tg = P.dma("gpsimd", d_tap[:, 0:1024], h2b)
            P.wait("gpsimd", tg)
            return nc, P
        P.wait("vector", tok_rt)
        P.small_mode = True
        P.v(lambda e: e.tensor_copy(lg, PX[3][:, 128:136]))
        tok_lg = P.v(lambda e: e.max(m8, lg))
        tok_nm1 = P.v(lambda e: e.tensor_scalar(nm1, m8[:, 0:1], -1.0, None, ALU.mult))
        P.wait("scalar", tok_nm1)
        tok_q = P.s(lambda e: e.activation(qe, lg, AF.Exp, bias=nm1))
        P.wait("vector", tok_q)
        P.v(lambda e: e.tensor_single_scalar(msk, lg, m8[:, 1:2], ALU.is_ge))
        P.v(lambda e: e.tensor_tensor(qe, qe, msk, ALU.mult))
        P.v(lambda e: e.reduce_sum(den, qe, axis=AXX))
        P.v(lambda e: e.reciprocal(r2, den))
        P.v(lambda e: e.tensor_scalar(r2, r2, 2.0, None, ALU.mult))
        P.v(lambda e: e.tensor_scalar(qe, qe, r2, None, ALU.mult))
        if stop == "p5b" and tt == 0:
            P.wait("gpsimd", tok_ha)
            tg = P.dma("gpsimd", d_tap[:, 0:1024], h2b)
            P.wait("gpsimd", tg)
            return nc, P
        cmb = qe.rearrange("p (a b) -> p a b", b=1).broadcast_to([128, 8, 16])
        P.wait("vector", tok_ha)
        tok_cp = P.v(lambda e: e.tensor_copy(H2W, PX[3][:, 0:128]))
        if stop == "p5c0" and tt == 0:
            P.wait("gpsimd", tok_cp)
            tg = P.dma("gpsimd", d_tap[:, 0:128], H2W)
            P.wait("gpsimd", tg)
            return nc, P
        guard[3] = tok_cp
        tok_h2t = P.v(lambda e, cmb=cmb: e.tensor_tensor(
            H2T.rearrange("p (a b) -> p a b", b=16),
            H2W.rearrange("p (a b) -> p a b", b=16), cmb, ALU.mult))
        P.small_mode = False
        if stop == "p5c" and tt == 0:
            P.wait("gpsimd", tok_h2t)
            tg = P.dma("gpsimd", d_tap[:, 0:128], H2T)
            P.wait("gpsimd", tg)
            return nc, P
        bankw(6)
        P.waits("tensor", tok_h2t, tok_id)
        tok_tr = P.t(lambda e: e.transpose(PX[6][:, 0:128], H2T, IDT[:, :]))
        P.wait("vector", tok_tr)
        tok_h2 = P.v(lambda e, tt=tt: e.tensor_copy(h2b[:, tt * 128:(tt + 1) * 128], PX[6][:, 0:128]))
        guard[6] = tok_h2
        if stop == "ffn_h2" and tt == 1:
            P.wait("gpsimd", tok_h2)
            tg = P.dma("gpsimd", d_tap[:, 0:1024], h2b)
            P.wait("gpsimd", tg)
            return nc, P

    P.wait("sync", tok_h2)
    tok_wb = [tok_at, tok_at]
    tok_fdma = None
    tok_fsq = None
    tok_fstat = None
    tok_evprev = None
    for og in range(VG):
        sl = og % 2
        P.wait("sync", tok_wb[sl])
        if sl == 1:
            P.wait("sync", tok_ha)
        tok_wbd = P.dma("sync", wbslot[sl], d_wbT[og])
        tok_btd = P.dma("sync", btslot[sl], d_bT[:, og * 128:(og + 1) * 128])
        evt = []
        for tc in range(2):
            b = (og % 2) * 2 + tc
            bankw(b)
            P.waits("tensor", tok_wbd, tok_btd, tok_h2)
            for kg in range(VG):
                rhs = AR4[:, 2 * tc:2 * tc + 2, kg, :]
                P.t(lambda e, kg=kg, sl=sl, rhs=rhs, b=b: e.matmul(
                    PX[b][:, :], wbslot[sl][:, kg, :], rhs, start=(kg == 0), stop=False))
            tok_base = P.t(lambda e, sl=sl, tc=tc, b=b: e.matmul(
                PX[b][:, :], btslot[sl], h2b[:, tc * 512:(tc + 1) * 512], start=False, stop=True))
            tok_wb[sl] = tok_base
            P.waits("vector", tok_base, tok_fdma, tok_fsq, tok_fstat)
            tok_ev = P.v(lambda e, og=og, tc=tc, b=b: e.tensor_scalar(
                ftile[:, tc * 512:(tc + 1) * 512], PX[b][:, :],
                SM[:, SC["b_base"] + og: SC["b_base"] + og + 1], None, ALU.add))
            guard[b] = tok_ev
            evt.append(tok_ev)
        P.wait("gpsimd", evt[-1])
        tok_fdma = P.dma("gpsimd", d_fscr[:, og, :], ftile)
        P.waits("scalar", evt[-1], tok_fstat)
        tok_fsq = P.s(lambda e: e.activation(fsq, ftile, AF.Square))
        for tc in range(2):
            b = 4 + tc
            if og == 0:
                bankw(b)
            P.wait("tensor", evt[tc])
            P.t(lambda e, og=og, tc=tc, b=b: e.matmul(
                PX[b][0:1, 0:512], ONESK[:, :], ftile[:, tc * 512:(tc + 1) * 512],
                start=(og == 0), stop=(og == VG - 1)))
            P.wait("tensor", tok_fsq)
            tok_fstat = P.t(lambda e, og=og, tc=tc, b=b: e.matmul(
                PX[b][32:33, 0:512], ONESK[:, :], fsq[:, tc * 512:(tc + 1) * 512],
                start=(og == 0), stop=(og == VG - 1)))
        if stop == "ffn_f" and og == 1:
            P.wait("gpsimd", tok_fdma)
            tg = P.dma("gpsimd", d_tap[:, 0:1024], ftile)
            P.wait("gpsimd", tg)
            return nc, P
    for tc in range(2):
        tok_coef = coef_rows(PX[4 + tc], 512, V, tok_fstat)
        guard[4 + tc] = tok_coef
        P.v(lambda e, tc=tc: e.tensor_copy(RROWB[0:1, tc * 512:(tc + 1) * 512], RROW[0:1, :]))
        tok_frows = P.v(lambda e, tc=tc: e.tensor_copy(CROWB[0:1, tc * 512:(tc + 1) * 512], MUROW[0:1, :]))

    tok_outd = None
    for tc in range(2):
        fch = AR[:, 0:16384].rearrange("p (v t) -> p v t", t=512)
        xch = AR[:, 16384:32768].rearrange("p (v t) -> p v t", t=512)
        P.waits("sync", tok_base, tok_fdma, tok_outd)
        tok_fc = P.dma("sync", fch, d_fscr[:, :, tc * 512:(tc + 1) * 512])
        P.wait("sync", tok_x1dma)
        tok_xc = P.dma("sync", xch, d_x1scr[:, :, tc * 512:(tc + 1) * 512])
        tok_r = repl(6, 7, 512, tok_frows,
                     src_r=RROWB[0:1, tc * 512:(tc + 1) * 512],
                     src_c=CROWB[0:1, tc * 512:(tc + 1) * 512])
        P.waits("vector", tok_fc, tok_xc, tok_r)
        rb = PX[6][:, :].unsqueeze(1).broadcast_to([128, VG, 512])
        cbk = PX[7][:, :].unsqueeze(1).broadcast_to([128, VG, 512])
        P.v(lambda e, fch=fch, rb=rb: e.tensor_tensor(fch, fch, rb, ALU.mult))
        P.v(lambda e, fch=fch, cbk=cbk: e.tensor_tensor(fch, fch, cbk, ALU.add))
        tok_o = P.v(lambda e, fch=fch, xch=xch: e.tensor_tensor(fch, fch, xch, ALU.add))
        guard[6] = tok_o
        guard[7] = tok_o
        P.wait("gpsimd", tok_o)
        tok_outd = P.dma("gpsimd", d_out[:, :, tc * 512:(tc + 1) * 512], fch)
    P.wait("gpsimd", tok_outd)
    return nc, P


def finalize(nc, P):
    with (
        nc.semaphore("sD") as sD,
        nc.semaphore("sG") as sG,
        nc.semaphore("sT") as sT,
        nc.semaphore("sV") as sV,
        nc.semaphore("sS") as sS,
        nc.Block() as block,
    ):
        sems = dict(D=sD, G=sG, T=sT, V=sV, S=sS)

        def runner(ename):
            def run(e):
                for item in P.q[ename]:
                    if item[0] == "w":
                        e.wait_ge(sems[item[1]], item[2])
                    else:
                        _, fn, sem, amt = item
                        ins = fn(e)
                        if sem is not None:
                            ins.then_inc(sems[sem], amt)
            return run
        block.sync(runner("sync"))
        block.tensor(runner("tensor"))
        block.vector(runner("vector"))
        block.scalar(runner("scalar"))
        block.gpsimd(runner("gpsimd"))
    return nc


def build(stop=None):
    nc, P = build_program(stop)
    return finalize(nc, P)


def _pack_weights(inputs):
    f32 = np.float32
    g = lambda k: np.asarray(inputs[k], dtype=f32)
    w = {}
    w["winT"] = np.ascontiguousarray(g("L0_attn_Win").reshape(D, VG, 128).transpose(2, 1, 0))
    w["w1T"] = np.ascontiguousarray(g("L0_attn_W1").reshape(DEPTH, FF, DG, 128).transpose(0, 3, 2, 1))
    w["w2T"] = np.ascontiguousarray(g("L0_attn_W2").reshape(DEPTH, D, FG, 128).transpose(0, 3, 2, 1))
    w["woutT"] = np.ascontiguousarray(g("L0_attn_Wout").reshape(V, DG, 128).transpose(2, 1, 0))
    w["wbT"] = np.ascontiguousarray(g("L0_ffn_W_base").reshape(VG, 128, VG, 128).transpose(0, 3, 2, 1))
    at = g("L0_ffn_A_stack").reshape(128, VG, 128).transpose(2, 1, 0)
    wrt = g("L0_ffn_W_router").reshape(8, VG, 128).transpose(2, 1, 0)
    w["aT"] = np.ascontiguousarray(np.concatenate([at, wrt], axis=2))
    w["bT"] = np.ascontiguousarray(g("L0_ffn_B_stack").transpose(0, 2, 1).reshape(128, V))
    w["id128"] = np.eye(128, dtype=f32)

    def chg(vec):
        return np.asarray(vec, f32).reshape(4, 128).T

    sm = np.zeros((128, NS), f32)
    sm[:, SC["bin"]:SC["bin"] + 4] = chg(g("L0_attn_bin"))
    for name, key in (("ln1_w", "L0_attn_ln1_w"), ("ln1_b", "L0_attn_ln1_b"),
                      ("ln2_w", "L0_attn_ln2_w"), ("ln2_b", "L0_attn_ln2_b"),
                      ("phz_re", "L0_attn_phz_re"), ("phz_im", "L0_attn_phz_im"),
                      ("lc_re", "L0_attn_lc_re"), ("lc_im", "L0_attn_lc_im"),
                      ("sc_lin", "L0_attn_sc_lin"), ("b2", "L0_attn_b2")):
        arr = g(key)
        for i in range(DEPTH):
            sm[:, SC[name] + i * 4: SC[name] + (i + 1) * 4] = chg(arr[i])
    sm[:, SC["lnl_w"]:SC["lnl_w"] + 4] = chg(g("L0_attn_lnl_w"))
    sm[:, SC["lnl_b"]:SC["lnl_b"] + 4] = chg(g("L0_attn_lnl_b"))
    b1 = g("L0_attn_b1")
    for i in range(DEPTH):
        sm[:, SC["b1"] + i * 16: SC["b1"] + (i + 1) * 16] = b1[i].reshape(16, 128).T
    sm[:, SC["bout"]:SC["bout"] + 32] = g("L0_attn_bout").reshape(32, 128).T
    sm[:, SC["b_base"]:SC["b_base"] + 32] = g("L0_ffn_b_base").reshape(32, 128).T
    w["smalls_base"] = sm
    return w


_CACHE = {}


def kernel(**inputs):
    key = "full"
    if key not in _CACHE:
        _CACHE[key] = build()
    nc = _CACHE[key]
    w = _pack_weights(inputs)
    x = np.asarray(inputs["L0_attn_x"], np.float32)
    in_maps = []
    for c in range(8):
        b, m = c // 2, c % 2
        xT = np.ascontiguousarray(x[b].reshape(L, VG, 128).transpose(2, 1, 0))
        xown = np.ascontiguousarray(xT[:, :, m * TOK:(m + 1) * TOK])
        sm = w["smalls_base"].copy()
        sm[:, SC["sel"]] = 1.0 if m == 0 else 0.0
        sm[:, SC["sel"] + 1] = 0.0 if m == 0 else 1.0
        in_maps.append({
            "xT": xT, "xown": xown, "winT": w["winT"], "w1T": w["w1T"],
            "w2T": w["w2T"], "woutT": w["woutT"], "wbT": w["wbT"],
            "aT": w["aT"], "bT": w["bT"],
            "smalls": sm, "id128": w["id128"],
        })
    res = run_bass_kernel_spmd(nc, in_maps, core_ids=list(range(8)))
    out = np.empty((4, L, V), np.float32)
    for c in range(8):
        b, m = c // 2, c % 2
        o = res.results[c]["OUT"]
        out[b, m * TOK:(m + 1) * TOK] = np.ascontiguousarray(
            np.transpose(o, (2, 1, 0)).reshape(TOK, V))
    return out
